# version 16
# baseline (speedup 1.0000x reference)
import contextlib
import math

import numpy as np
import ml_dtypes

import concourse.bass as bass
import concourse.mybir as mybir
from concourse.bass_utils import run_bass_kernel_spmd

F32 = mybir.dt.float32
BF16 = mybir.dt.bfloat16
F8E4 = mybir.dt.float8e4
I16 = mybir.dt.int16
I32 = mybir.dt.int32
AF = mybir.ActivationFunctionType
OP = mybir.AluOpType
AX = mybir.AxisListType
LN2 = math.log(2.0)
INVLN2 = 1.0 / LN2
LN2_HI = float(np.float32(LN2))
LN2_LO = float(np.float64(LN2) - np.float64(np.float32(LN2)))


class Cfg:
    def __init__(self, T=2048, E=1024, HEADS=16, LAT=512, HF=4096, PAGE=256,
                 n_cores=8, ngrp=None):
        self.T, self.E, self.HEADS, self.LAT, self.HF = T, E, HEADS, LAT, HF
        self.PAGE = PAGE
        self.n_cores = n_cores
        self.D = 64
        assert E == HEADS * 64
        self.PAIRS = HEADS // 2
        self.TT = T // 128
        self.KE = E // 128
        self.KL = LAT // 128
        self.JT = HF // 128
        self.QBW = min(512, T)
        self.NQB = T // self.QBW
        self.KTILES = T // 128
        self.NBW = min(512, T)
        self.NB = T // self.NBW
        self.NEW = min(512, E)
        self.NE = E // self.NEW
        self.NGRP = ngrp if ngrp is not None else min(4, self.PAIRS)
        assert self.PAIRS % self.NGRP == 0
        self.PPG = self.PAIRS // self.NGRP
        self.VW = 65

    def score_items(self, qb):
        out = []
        j = 0
        for kt in range(self.KTILES):
            qlo = (kt * 128 // self.PAGE) * self.PAGE
            qs = max(qb * self.QBW, qlo)
            qe = (qb + 1) * self.QBW
            if qs >= qe:
                break
            out.append((j, kt, qs - qb * self.QBW, qe - qs))
            j += 1
        return out


class Builder:
    def __init__(self, cfg: Cfg):
        self.cfg = cfg
        self.nc = bass.Bass("TRN2", target_bir_lowering=False,
                            detect_race_conditions=False,
                            dynamic_dma_scratch_size=16384)
        self.es = contextlib.ExitStack()
        self.cnt = {}

    def sem(self, name):
        h = self.es.enter_context(self.nc.semaphore(name))
        self.cnt[name] = 0
        return h

    def inc(self, instr, name, n=1):
        instr.then_inc(self.sems[name], n)
        self.cnt[name] += n
        return self.cnt[name]

    def vfence(self, vec, instr):
        c = self.inc(instr, "vrd", 1)
        vec.wait_ge(self.sems["vrd"], c)

    def build(self):
        cfg = self.cfg
        nc = self.nc
        T, E, LAT, HF = cfg.T, cfg.E, cfg.LAT, cfg.HF
        TT, KE, KL, JT = cfg.TT, cfg.KE, cfg.KL, cfg.JT
        QBW, VW = cfg.QBW, cfg.VW
        S = self.es.enter_context
        d = self.d = {}
        t = self.t = {}

        d["x"] = nc.dram_tensor("x", [T, E], F32, kind="ExternalInput")
        d["wq"] = nc.dram_tensor("wq", [E, E], F32, kind="ExternalInput")
        d["wdn"] = nc.dram_tensor("wdn", [E, LAT], F32, kind="ExternalInput")
        d["wku"] = nc.dram_tensor("wku", [LAT, E], F32, kind="ExternalInput")
        d["wvu"] = nc.dram_tensor("wvu", [LAT, E], F32, kind="ExternalInput")
        d["wo"] = nc.dram_tensor("wo", [E, E], F32, kind="ExternalInput")
        d["w1b"] = nc.dram_tensor("w1b", [JT, E, 128], F32, kind="ExternalInput")
        d["w2t"] = nc.dram_tensor("w2t", [HF, E], F32, kind="ExternalInput")
        d["b1r"] = nc.dram_tensor("b1r", [128, JT], F32, kind="ExternalInput")
        d["b2r"] = nc.dram_tensor("b2r", [1, E], F32, kind="ExternalInput")
        d["tc"] = nc.dram_tensor("tc", [128, T], F32, kind="ExternalInput")
        d["ts"] = nc.dram_tensor("ts", [128, T], F32, kind="ExternalInput")
        d["Pswap"] = nc.dram_tensor("Pswap", [128, 128], BF16, kind="ExternalInput")
        d["identin"] = nc.dram_tensor("identin", [128, 128], BF16, kind="ExternalInput")
        d["e64in"] = nc.dram_tensor("e64in", [128, 64], BF16, kind="ExternalInput")
        d["onesin"] = nc.dram_tensor("onesin", [128, 128], BF16, kind="ExternalInput")
        d["onescol"] = nc.dram_tensor("onescol", [128, 1], BF16, kind="ExternalInput")
        d["out"] = nc.dram_tensor("out", [T, E], F32, kind="ExternalOutput")
        spill_kind = "ExternalOutput" if getattr(self, "debug", False) else "Internal"
        d["aspill"] = nc.dram_tensor("aspill", [E, T], BF16, kind=spill_kind)
        d["hqspill"] = nc.dram_tensor("hqspill", [JT * 128, T], F8E4, kind=spill_kind)
        if getattr(self, "debug", False):
            d["dbg_qT"] = nc.dram_tensor("dbg_qT", [128, cfg.PPG * T], F32, kind="ExternalOutput")
            d["dbg_kT"] = nc.dram_tensor("dbg_kT", [128, cfg.PPG * T], F32, kind="ExternalOutput")
            d["dbg_ones"] = nc.dram_tensor("dbg_ones", [128, 128], F32, kind="ExternalOutput")
            d["dbg_negB"] = nc.dram_tensor("dbg_negB", [128, cfg.PAIRS], F32, kind="ExternalOutput")
            d["dbg_b2q"] = nc.dram_tensor("dbg_b2q", [128, E], F32, kind="ExternalOutput")
            d["dbg_hq"] = nc.dram_tensor("dbg_hq", [128, T], F32, kind="ExternalOutput")
            d["dbg_w2q"] = nc.dram_tensor("dbg_w2q", [128, E], F32, kind="ExternalOutput")
            d["dbg_xnT"] = nc.dram_tensor("dbg_xnT", [128, T], F32, kind="ExternalOutput")


        L = dict(side="left")
        t["cpool"] = S(nc.sbuf_tensor("cpool", [128, 12], F32, **L))
        t["ones_c"] = S(nc.sbuf_tensor("ones_c", [128, 1], BF16, **L))
        t["onesf"] = S(nc.sbuf_tensor("onesf", [128, 128], BF16, **L))
        t["e64"] = S(nc.sbuf_tensor("e64s", [128, 64], BF16, **L))
        t["Psw"] = S(nc.sbuf_tensor("Psws", [128, 128], BF16, **L))
        t["ident"] = S(nc.sbuf_tensor("idents", [128, 128], BF16, **L))
        t["negB"] = S(nc.sbuf_tensor("negB", [128, cfg.PAIRS], F32, **L))
        t["b1sb"] = S(nc.sbuf_tensor("b1sb", [128, JT], F32, **L))
        t["stats"] = S(nc.sbuf_tensor("stats", [128, 8], F32, **L))
        t["s11"] = S(nc.sbuf_tensor("s11", [1, 8], F32, **L))
        t["xbuf"] = [S(nc.sbuf_tensor(f"xbuf{i}", [128, E], F32, **L)) for i in range(2)]
        t["w2q"] = S(nc.sbuf_tensor("w2q", [128, JT * E], F8E4, **L))

        R = dict(side="right")
        t["xnT"] = S(nc.sbuf_tensor("xnT", [128, KE * T], BF16, **R))

        self.sems = {n: self.sem(n) for n in (
            "dS", "dG", "dO", "dSp", "dX2",
            "vxn", "vvar", "vev", "ttr", "sln",
            "tpj", "vpe", "tsw", "vrp", "tnm", "vnm",
            "tS", "sE", "tPV", "vNR", "sRC", "sBC",
            "vW2", "dW", "dW1", "vHq", "tQ", "dHq",
            "tF1", "sH", "vH",
            "tF2", "tWO", "vEP", "sEP", "vrd",
        )}

        t["mgad"] = S(nc.sbuf_tensor("mgad", [128, T], BF16, **R))
        t["w2st"] = S(nc.sbuf_tensor("w2st", [128, E], F32, **R))
        t["wD"] = S(nc.sbuf_tensor("wD", [128, E], BF16, **R))
        t["wA"] = S(nc.sbuf_tensor("wA", [128, E], BF16, **R))
        t["wC"] = S(nc.sbuf_tensor("wC", [128, E], BF16, **R))
        t["wT"] = S(nc.sbuf_tensor("wT", [128, E], BF16, **R))

        sc1 = contextlib.ExitStack()
        t["xcb"] = sc1.enter_context(nc.sbuf_tensor("xcb", [128, E], F32, **R))
        t["xc2b"] = sc1.enter_context(nc.sbuf_tensor("xc2b", [128, E], F32, **R))
        t["xnb"] = [sc1.enter_context(nc.sbuf_tensor(f"xnb{i}", [128, E], BF16, **R))
                    for i in range(2)]
        ps = self.ps = {}
        ps["ptr"] = [sc1.enter_context(nc.psum_tensor(f"ptr{i}", [128, 128], BF16))
                     for i in range(2)]
        with nc.Block() as blk:
            self.phase1(blk)
        sc1.close()

        sc2 = contextlib.ExitStack()
        E2 = sc2.enter_context
        t["tc"] = E2(nc.sbuf_tensor("tcs", [128, T], F32, **R))
        t["ts"] = E2(nc.sbuf_tensor("tss", [128, T], F32, **R))
        t["latT"] = E2(nc.sbuf_tensor("latT", [128, KL * T], BF16, **R))
        t["vext"] = E2(nc.sbuf_tensor("vext", [128, TT * cfg.HEADS * VW], BF16, **R))
        t["qT"] = E2(nc.sbuf_tensor("qT", [128, cfg.PPG * T], BF16, **R))
        t["kT"] = E2(nc.sbuf_tensor("kT", [128, cfg.PPG * T], BF16, **R))
        t["wbuf"] = E2(nc.sbuf_tensor("wbuf", [128, max(KE * LAT, KL * E, (KE + KL) * cfg.PPG * 128, T + 4 * cfg.QBW)], BF16, **R))
        t["qraw"] = [E2(nc.sbuf_tensor(f"qraw{i}", [128, T], BF16, **R)) for i in range(2)]
        QBW_ = cfg.QBW
        t["slabA"] = [t["wbuf"][:, T:T + QBW_]] * 2
        t["slabB"] = [t["wbuf"][:, T + QBW_:T + 2 * QBW_]] * 2
        t["bcst"] = t["wbuf"][:, T + 2 * QBW_:T + 3 * QBW_]
        t["bcstB"] = t["wbuf"][:, T + 3 * QBW_:T + 4 * QBW_]
        t["ttmp"] = E2(nc.sbuf_tensor("ttmp", [128, cfg.NBW], BF16, **R))
        t["ttmp2"] = E2(nc.sbuf_tensor("ttmp2", [128, cfg.NBW], BF16, **R))
        t["expA"] = [E2(nc.sbuf_tensor(f"expA{i}", [128, QBW], BF16, **R)) for i in range(2)]
        t["expB"] = [E2(nc.sbuf_tensor(f"expB{i}", [128, QBW], BF16, **R)) for i in range(2)]
        t["recA"] = E2(nc.sbuf_tensor("recA", [128, QBW], BF16, **R))
        t["lnz"] = E2(nc.sbuf_tensor("lnz", [1, QBW], F32, **R))
        t["w1s1"] = E2(nc.sbuf_tensor("w1s1", [128, KE * 128], BF16, **R))
        _tps = cfg.JT // (cfg.NGRP * max(1, cfg.PPG * (T // cfg.QBW)))
        t["qth"] = [E2(nc.sbuf_tensor(f"qth{i}", [128, T], BF16, **R))
                    for i in range(max(2, 2 * _tps))]
        t["hqw"] = [E2(nc.sbuf_tensor("hqw0", [128, T], F8E4, **R))]
        t["recB"] = E2(nc.sbuf_tensor("recB", [128, QBW], BF16, **R))
        ps["pp"] = [E2(nc.psum_tensor(f"pp{i}", [128, 512], F32)) for i in range(2)]
        ps["pswp"] = [E2(nc.psum_tensor(f"pswp{i}", [128, 512], F32)) for i in range(2)]
        ps["pnorm"] = E2(nc.psum_tensor("pnorm", [128, 512], F32))
        ps["pvA"] = E2(nc.psum_tensor("pvA", [128, 512], F32))
        ps["pvB"] = E2(nc.psum_tensor("pvB", [128, 512], F32))
        ps["phq"] = E2(nc.psum_tensor("phq", [128, 512], F32))
        with nc.Block() as blk:
            self.phase_proj_v(blk)
        for g in range(cfg.NGRP):
            with nc.Block() as blk:
                self.phase_proj_qk(blk, g)
            with nc.Block() as blk:
                self.phase_attn(blk, g)
        if getattr(self, "debug", False):
            with nc.Block() as blk:
                @blk.gpsimd
                def _(gp):
                    dbga = self.sem("dbga")
                    gp.dma_start(d["dbg_qT"][:, :], t["qT"][:, :]).then_inc(dbga, 16)
                    gp.dma_start(d["dbg_kT"][:, :], t["kT"][:, :]).then_inc(dbga, 16)
                    gp.dma_start(d["dbg_ones"][:, :], t["onesf"][:, :]).then_inc(dbga, 16)
                    gp.wait_ge(dbga, 48)
        sc2.close()


        sc5 = contextlib.ExitStack()
        E5 = sc5.enter_context
        t["wosb"] = E5(nc.sbuf_tensor("wosb", [128, KE * E], BF16, **R))
        t["atb"] = [E5(nc.sbuf_tensor(f"atb{i}", [128, E], BF16, **R)) for i in range(2)]
        t["hqb"] = [E5(nc.sbuf_tensor(f"hqb{i}", [128, JT * 128], F8E4, **R))
                    for i in range(2)]
        t["o2b"] = E5(nc.sbuf_tensor("o2b", [128, E], F32, **R))
        t["oav"] = E5(nc.sbuf_tensor("oav", [128, E], BF16, **R))
        t["oD"] = E5(nc.sbuf_tensor("oD", [128, E], BF16, **R))
        t["oA"] = E5(nc.sbuf_tensor("oA", [128, E], BF16, **R))
        t["oB"] = E5(nc.sbuf_tensor("oB", [128, E], F32, **R))
        t["oC"] = E5(nc.sbuf_tensor("oC", [128, E], BF16, **R))
        t["oT"] = E5(nc.sbuf_tensor("oT", [128, E], BF16, **R))
        t["oS"] = E5(nc.sbuf_tensor("oS", [128, E], F32, **R))
        t["b2q"] = E5(nc.sbuf_tensor("b2q", [128, E], F32, **R))
        t["obuf"] = [E5(nc.sbuf_tensor(f"obuf{i}", [128, E], F32, **R)) for i in range(2)]
        t["b2rep"] = t["obuf"][0]
        ps["psf"] = [E5(nc.psum_tensor(f"psf{i}", [128, E], F32)) for i in range(2)]
        ps["pso"] = [E5(nc.psum_tensor(f"pso{i}", [128, E], F32)) for i in range(2)]
        with nc.Block() as blk:
            self.phase_out(blk)
        sc5.close()

        if getattr(self, "debug", False):

            with nc.Block() as blk:
                @blk.gpsimd
                def _(gp):
                    dbg = self.sem("dbgd")
                    gp.wait_ge(self.sems["vEP"], self.cnt["vEP"])
                    gp.dma_start(d["dbg_negB"][:, :], t["negB"][:, :]).then_inc(dbg, 16)
                    gp.dma_start(d["dbg_b2q"][:, :], t["b2q"][:, :]).then_inc(dbg, 16)
                    gp.dma_start(d["dbg_w2q"][:, :], t["w2q"][:, 0:E]).then_inc(dbg, 16)
                    gp.dma_start(d["dbg_xnT"][:, :], t["xnT"][:, 0:T]).then_inc(dbg, 16)
                    gp.wait_ge(dbg, 64)
            with nc.Block() as blk:
                @blk.gpsimd
                def _(gp):
                    dbg2 = self.sem("dbgd2")
                    gp.dma_start(d["dbg_hq"][:, :],
                                 d["hqspill"][0:128, 0:T]).then_inc(dbg2, 16)
                    gp.wait_ge(dbg2, 16)

        self.es.close()
        return nc

    def phase1(self, blk):
        cfg, t, d, ps = self.cfg, self.t, self.d, self.ps
        T, E, TT, KE = cfg.T, cfg.E, cfg.TT, cfg.KE
        NCONST = 6

        @blk.sync
        def _(sync):
            self.inc(sync.dma_start(t["Psw"][:, :], d["Pswap"][:, :]), "dS", 16)
            self.inc(sync.dma_start(t["ident"][:, :], d["identin"][:, :]), "dS", 16)
            self.inc(sync.dma_start(t["e64"][:, :], d["e64in"][:, :]), "dS", 16)
            self.inc(sync.dma_start(t["ones_c"][:, 0:1], d["onescol"][:, :]), "dS", 16)
            self.inc(sync.dma_start(t["onesf"][:, :], d["onesin"][:, :]), "dS", 16)
            self.inc(sync.dma_start(t["b1sb"][:, :], d["b1r"][:, :]), "dS", 16)
            for tt in range(TT):
                if tt >= 2:
                    sync.wait_ge(self.sems["vxn"], tt - 1)
                self.inc(sync.dma_start(t["xbuf"][tt % 2][:, :],
                                        d["x"][tt * 128:(tt + 1) * 128, :]), "dS", 16)

        @blk.vector
        def _(vec):
            W = TT * cfg.HEADS * cfg.VW
            vec.memset(t["cpool"][:, 0:1], 1.0 / E)
            vec.memset(t["cpool"][:, 1:2], 1e-5)
            vec.memset(t["cpool"][:, 2:3], 0.125)
            vec.memset(t["cpool"][:, 3:4], -1.0)
            self.vfence(vec, vec.memset(t["mgad"][:, 0:T], 1024.0))
            self.vfence(vec, vec.tensor_scalar(t["mgad"][:, 0:T], t["mgad"][:, 0:T],
                                               1.0, 2.0 ** 52, OP.mult, OP.add))
            self.vfence(vec, vec.tensor_scalar(t["mgad"][:, 0:T], t["mgad"][:, 0:T],
                                               -(2.0 ** 52), 2.0 ** -10,
                                               OP.add, OP.mult))
            for tt in range(TT):
                vec.wait_ge(self.sems["dS"], 16 * (NCONST + tt + 1))
                xb = t["xbuf"][tt % 2]
                self.vfence(vec, vec.reduce_sum(t["stats"][:, 0:1], xb[:, :],
                                                axis=AX.X))
                self.vfence(vec, vec.tensor_scalar_mul(t["stats"][:, 1:2],
                                                       t["stats"][:, 0:1], 1.0 / E))
                vec.tensor_scalar(t["xcb"][:, :], xb[:, :], t["stats"][:, 1:2], None,
                                  OP.subtract)
                vec.tensor_mul(t["xc2b"][:, :], t["xcb"][:, :], t["xcb"][:, :])
                vv = vec.reduce_sum(t["stats"][:, 2:3], t["xc2b"][:, :], axis=AX.X)
                self.inc(vv, "vvar", 1)
                vec.wait_ge(self.sems["sln"], tt + 1)
                self.vfence(vec, vec.reciprocal(t["stats"][:, 4:5], t["stats"][:, 3:4]))
                xn = vec.tensor_scalar(t["xnb"][tt % 2][:, :], t["xcb"][:, :],
                                       t["stats"][:, 4:5], None, OP.mult)
                self.inc(xn, "vxn", 1)
                for cc in range(KE):
                    g = tt * KE + cc
                    vec.wait_ge(self.sems["ttr"], g + 1)
                    ev = vec.tensor_copy(
                        t["xnT"][:, cc * T + tt * 128: cc * T + (tt + 1) * 128],
                        ps["ptr"][g % 2][:, 0:128])
                    self.inc(ev, "vev", 1)

        @blk.scalar
        def _(sc):
            for tt in range(TT):
                sc.wait_ge(self.sems["vvar"], tt + 1)
                a = sc.activation(t["stats"][:, 3:4], t["stats"][:, 2:3], AF.Sqrt,
                                  bias=t["cpool"][:, 1:2], scale=t["cpool"][:, 0:1])
                self.inc(a, "sln", 1)

        @blk.tensor
        def _(tn):
            tn.wait_ge(self.sems["dS"], 16 * 2)
            for tt in range(TT):
                tn.wait_ge(self.sems["vxn"], tt + 1)
                for cc in range(KE):
                    g = tt * KE + cc
                    if g >= 2:
                        tn.wait_ge(self.sems["vev"], g - 1)
                    tr = tn.transpose(ps["ptr"][g % 2][:, 0:128],
                                      t["xnb"][tt % 2][:, cc * 128:(cc + 1) * 128],
                                      t["ident"][:, :])
                    self.inc(tr, "ttr", 1)

    def phase_proj_v(self, blk):
        cfg, t, d, ps = self.cfg, self.t, self.d, self.ps
        T, E, LAT = cfg.T, cfg.E, cfg.LAT
        TT, KE, KL = cfg.TT, cfg.KE, cfg.KL
        NB, NBW, NE, NEW = cfg.NB, cfg.NBW, cfg.NE, cfg.NEW
        VW, HEADS = cfg.VW, cfg.HEADS
        lat_items = [(lt, nb) for lt in range(KL) for nb in range(NB)]
        v_items = [(tt, fc) for tt in range(TT) for fc in range(NE)]
        b_pj, b_pe = self.cnt["tpj"], self.cnt["vpe"]
        b_dG = self.cnt["dG"]
        b_dS = self.cnt["dS"]

        @blk.sync
        def _(sync):
            self.inc(sync.dma_start(t["tc"][:, :], d["tc"][:, :]), "dS", 16)
            self.inc(sync.dma_start(t["ts"][:, :], d["ts"][:, :]), "dS", 16)
            self.tabs_dS = self.cnt["dS"]

        @blk.gpsimd
        def _(gp):
            for kc in range(KE):
                self.inc(gp.dma_start(t["wbuf"][:, kc * LAT:(kc + 1) * LAT],
                                      d["wdn"][kc * 128:(kc + 1) * 128, :]), "dG", 16)
            gp.wait_ge(self.sems["tpj"], b_pj + len(lat_items))
            for lc in range(KL):
                self.inc(gp.dma_start(t["wbuf"][:, lc * E:(lc + 1) * E],
                                      d["wvu"][lc * 128:(lc + 1) * 128, :]), "dG", 16)

        @blk.tensor
        def _(tn):
            tn.wait_ge(self.sems["dG"], b_dG + 16 * KE)
            for i, (lt, nb) in enumerate(lat_items):
                if i >= 2:
                    tn.wait_ge(self.sems["vpe"], b_pe + i - 1)
                for kc in range(KE):
                    mm = tn.matmul(ps["pp"][i % 2][:, 0:NBW],
                                   t["wbuf"][:, kc * LAT + lt * 128: kc * LAT + (lt + 1) * 128],
                                   t["xnT"][:, kc * T + nb * NBW: kc * T + (nb + 1) * NBW],
                                   start=(kc == 0), stop=(kc == KE - 1))
                self.inc(mm, "tpj", 1)
            tn.wait_ge(self.sems["dG"], b_dG + 16 * (KE + KL))
            for i2, (tt, fc) in enumerate(v_items):
                i = len(lat_items) + i2
                if i >= 2:
                    tn.wait_ge(self.sems["vpe"], b_pe + i - 1)
                for lc in range(KL):
                    mm = tn.matmul(ps["pp"][i % 2][:, 0:NEW],
                                   t["latT"][:, lc * T + tt * 128: lc * T + (tt + 1) * 128],
                                   t["wbuf"][:, lc * E + fc * NEW: lc * E + (fc + 1) * NEW],
                                   start=(lc == 0), stop=(lc == KL - 1))
                self.inc(mm, "tpj", 1)

        @blk.vector
        def _(vec):
            W = TT * HEADS * VW
            for tt in range(TT):
                vec.memset(bass.AP(t["vext"], tt * HEADS * VW + 64,
                                   [[W, 128], [VW, HEADS]]), 1.0)
            vec.memset(t["recA"][:, :], 0.0)
            vec.memset(t["recB"][:, :], 0.0)
            for i, (lt, nb) in enumerate(lat_items):
                vec.wait_ge(self.sems["tpj"], b_pj + i + 1)
                ev = vec.tensor_copy(
                    t["latT"][:, lt * T + nb * NBW: lt * T + (nb + 1) * NBW],
                    ps["pp"][i % 2][:, 0:NBW])
                self.inc(ev, "vpe", 1)
            for i2, (tt, fc) in enumerate(v_items):
                i = len(lat_items) + i2
                vec.wait_ge(self.sems["tpj"], b_pj + i + 1)
                nheads = NEW // 64
                dst = bass.AP(t["vext"], tt * HEADS * VW + fc * nheads * VW,
                              [[W, 128], [VW, nheads], [1, 64]])
                src = bass.AP(ps["pp"][i % 2], 0, [[512, 128], [64, nheads], [1, 64]])
                ev = vec.tensor_copy(dst, src)
                self.inc(ev, "vpe", 1)

    def phase_proj_qk(self, blk, g):
        cfg, t, d, ps = self.cfg, self.t, self.d, self.ps
        T, E, LAT = cfg.T, cfg.E, cfg.LAT
        KE, KL, PPG = cfg.KE, cfg.KL, cfg.PPG
        NB, NBW = cfg.NB, cfg.NBW
        WQC = PPG * 128
        b_dG = self.cnt["dG"]
        b_pj, b_pe = self.cnt["tpj"], self.cnt["vpe"]
        b_sw, b_rp = self.cnt["tsw"], self.cnt["vrp"]
        b_nm, b_vn = self.cnt["tnm"], self.cnt["vnm"]

        proj_items = [(li, w, nb) for li in range(PPG) for w in ("q", "k")
                      for nb in range(NB)]
        VPP = 2 * (1 + NB) + 1
        TPP = 2 * NB

        @blk.gpsimd
        def _(gp):
            gp.wait_ge(self.sems["tpj"], b_pj)
            for kc in range(KE):
                self.inc(gp.dma_start(
                    t["wbuf"][:, kc * WQC:(kc + 1) * WQC],
                    d["wq"][kc * 128:(kc + 1) * 128, g * WQC:(g + 1) * WQC]), "dG", 16)
            for lc in range(KL):
                self.inc(gp.dma_start(
                    t["wbuf"][:, KE * WQC + lc * WQC: KE * WQC + (lc + 1) * WQC],
                    d["wku"][lc * 128:(lc + 1) * 128, g * WQC:(g + 1) * WQC]), "dG", 16)

        @blk.tensor
        def _(tn):
            for idx, (li, w, nb) in enumerate(proj_items):
                if w == "q":
                    tn.wait_ge(self.sems["dG"], b_dG + 16 * KE)
                    nk, wof, src = KE, 0, "xnT"
                else:
                    tn.wait_ge(self.sems["dG"], b_dG + 16 * (KE + KL))
                    nk, wof, src = KL, KE * WQC, "latT"
                if idx >= 2:
                    tn.wait_ge(self.sems["vpe"], b_pe + idx - 1)
                for kc in range(nk):
                    mm = tn.matmul(
                        ps["pp"][idx % 2][:, 0:NBW],
                        t["wbuf"][:, wof + kc * WQC + li * 128: wof + kc * WQC + (li + 1) * 128],
                        t[src][:, kc * T + nb * NBW: kc * T + (nb + 1) * NBW],
                        start=(kc == 0), stop=(kc == nk - 1))
                self.inc(mm, "tpj", 1)
                tn.wait_ge(self.sems["vpe"], b_pe + idx + 1)
                if idx >= 2:
                    tn.wait_ge(self.sems["vrp"], b_rp + idx - 1)
                sw = tn.matmul(ps["pswp"][idx % 2][:, 0:NBW], t["Psw"][:, :],
                               t["qraw"][li % 2][:, nb * NBW:(nb + 1) * NBW],
                               start=True, stop=True)
                self.inc(sw, "tsw", 1)
            for li in range(PPG):
                vb = b_vn + li * VPP
                for side in range(2):
                    for c in range(NB):
                        tn.wait_ge(self.sems["vnm"], vb + side * (1 + NB) + 1 + c)
                        mm = tn.matmul(ps["pnorm"][:, 0:NBW], t["onesf"][:, :],
                                       t["qraw"][0][:, c * NBW:(c + 1) * NBW],
                                       start=True, stop=True)
                        self.inc(mm, "tnm", 1)

        @blk.vector
        def _(vec):
            vec.wait_ge(self.sems["dS"], self.tabs_dS)
            for idx, (li, w, nb) in enumerate(proj_items):
                vec.wait_ge(self.sems["tpj"], b_pj + idx + 1)
                ev = vec.tensor_copy(t["qraw"][li % 2][:, nb * NBW:(nb + 1) * NBW],
                                     ps["pp"][idx % 2][:, 0:NBW])
                self.inc(ev, "vpe", 1)
                vec.wait_ge(self.sems["tsw"], b_sw + idx + 1)
                dstT = t["qT"] if w == "q" else t["kT"]
                cols = slice(nb * NBW, (nb + 1) * NBW)
                vec.tensor_mul(t["ttmp"][:, 0:NBW], t["qraw"][li % 2][:, cols],
                               t["tc"][:, cols])
                vec.tensor_mul(t["ttmp2"][:, 0:NBW], ps["pswp"][idx % 2][:, 0:NBW],
                               t["ts"][:, cols])
                rp = vec.tensor_add(dstT[:, li * T + nb * NBW: li * T + (nb + 1) * NBW],
                                    t["ttmp"][:, 0:NBW], t["ttmp2"][:, 0:NBW])
                self.inc(rp, "vrp", 1)
            for li in range(PPG):
                hp = g * PPG + li
                tb = b_nm + li * TPP
                for side, srcT in ((0, t["qT"]), (1, t["kT"])):
                    vec.wait_ge(self.sems["tnm"], tb + side * NB)
                    sq = vec.tensor_mul(t["qraw"][0][:, 0:T],
                                        srcT[:, li * T:(li + 1) * T],
                                        srcT[:, li * T:(li + 1) * T])
                    self.inc(sq, "vnm", 1)
                    for c in range(NB):
                        vec.wait_ge(self.sems["tnm"], tb + side * NB + c + 1)
                        self.vfence(vec, vec.reduce_max(t["stats"][:, 5:6],
                                                        ps["pnorm"][:, 0:NBW],
                                                        axis=AX.X))
                        if c == 0:
                            mx = vec.tensor_copy(t["stats"][:, 6 + side:7 + side],
                                                 t["stats"][:, 5:6])
                        else:
                            mx = vec.tensor_max(t["stats"][:, 6 + side:7 + side],
                                                t["stats"][:, 6 + side:7 + side],
                                                t["stats"][:, 5:6])
                        self.vfence(vec, mx)
                        self.inc(vec.tensor_copy(t["stats"][:, 4:5],
                                                 t["stats"][:, 5:6]), "vnm", 1)
                self.vfence(vec, vec.tensor_add(t["stats"][:, 5:6], t["stats"][:, 6:7],
                                                t["stats"][:, 7:8]))
                nb_ = vec.tensor_scalar_mul(t["negB"][:, hp:hp + 1],
                                            t["stats"][:, 5:6], -0.0625)
                self.inc(nb_, "vnm", 1)

    def phase_attn(self, blk, g):
        cfg, t, d, ps = self.cfg, self.t, self.d, self.ps
        T, QBW, VW, HEADS = cfg.T, cfg.QBW, cfg.VW, cfg.HEADS
        PPG, NQB, JT, E = cfg.PPG, cfg.NQB, cfg.JT, cfg.E
        KE = cfg.KE
        b_tS, b_sE = self.cnt["tS"], self.cnt["sE"]
        b_PV, b_NR = self.cnt["tPV"], self.cnt["vNR"]
        b_RC, b_BC = self.cnt["sRC"], self.cnt["sBC"]
        b_nm = self.cnt["tnm"]
        b_dSp = self.cnt["dSp"]
        sl0 = b_dSp // 32

        units = []
        for li in range(PPG):
            for qb in range(NQB):
                units.append((li, qb, cfg.score_items(qb)))
        assert cfg.JT % (cfg.NGRP * len(units)) == 0, "w2 tile/unit mismatch"
        TPS = cfg.JT // (cfg.NGRP * len(units))
        NQ = T // 512
        U = len(units)
        NSLOT = cfg.NGRP * U
        qsched = {}
        for ui in range(U):
            s = g * U + ui
            quarters = [(s * TPS + jr, q) for jr in range(TPS) for q in range(NQ)]
            n_items = len(units[ui][2])
            for k, (jt, q) in enumerate(quarters):
                pos = min(2 * k + 1, n_items - 1)
                qsched.setdefault((ui, pos), []).append((jt, q))
        flat = []
        for ui, (li, qb, items) in enumerate(units):
            for (j, kt, coff, N) in items:
                flat.append((ui, li, qb, j, kt, coff, N, j == 0,
                             j == len(items) - 1))
        cum = [0]
        for (_, _, items) in units:
            cum.append(cum[-1] + len(items))

        @blk.tensor
        def _(tn):
            def emit_scores(fi):
                ui, li, qb, j, kt, coff, N, first, last = flat[fi]
                if fi >= 2:
                    tn.wait_ge(self.sems["sE"], b_sE + 2 * (fi - 1))
                tn.matmul(ps["pp"][fi % 2][0:128, coff:coff + N],
                          t["kT"][0:64, li * T + kt * 128: li * T + (kt + 1) * 128],
                          t["qT"][0:64, li * T + qb * QBW + coff: li * T + qb * QBW + coff + N],
                          start=True, stop=True, tile_position=(0, 0))
                mmB = tn.matmul(ps["pswp"][fi % 2][0:128, coff:coff + N],
                                t["kT"][64:128, li * T + kt * 128: li * T + (kt + 1) * 128],
                                t["qT"][64:128, li * T + qb * QBW + coff: li * T + qb * QBW + coff + N],
                                start=True, stop=True, tile_position=(64, 0))
                self.inc(mmB, "tS", 1)

            def emit_quarter(jt, q):
                Q = NQ * jt + q
                tn.wait_ge(self.sems["dW1"], 16 * (jt + 1))
                if Q >= 1:
                    tn.wait_ge(self.sems["sH"], Q)
                for kc in range(KE):
                    mm = tn.matmul(
                        ps["phq"][:, 0:512],
                        t["w1s1"][:, kc * 128:(kc + 1) * 128],
                        t["xnT"][:, kc * T + q * 512: kc * T + q * 512 + 512],
                        start=(kc == 0), stop=(kc == KE - 1))
                self.inc(mm, "tQ", 1)

            emit_scores(0)
            for fi, (ui, li, qb, j, kt, coff, N, first, last) in enumerate(flat):
                hp = g * PPG + li
                if fi + 1 < len(flat):
                    emit_scores(fi + 1)
                for (jt_, q_) in qsched.get((ui, j), []):
                    emit_quarter(jt_, q_)
                tn.wait_ge(self.sems["sE"], b_sE + 2 * (fi + 1))
                if first and ui >= 1:
                    tn.wait_ge(self.sems["vNR"], b_NR + 2 * ui)
                vbase = kt * HEADS * VW
                tn.matmul(ps["pvA"][0:VW, coff:coff + N],
                          t["vext"][:, vbase + 2 * hp * VW: vbase + 2 * hp * VW + VW],
                          t["expA"][fi % 2][0:128, 0:N], start=first, stop=last)
                pv = tn.matmul(ps["pvB"][0:VW, coff:coff + N],
                               t["vext"][:, vbase + (2 * hp + 1) * VW: vbase + (2 * hp + 1) * VW + VW],
                               t["expB"][fi % 2][0:128, 0:N], start=first, stop=last)
                self.inc(pv, "tPV", 1)
                if last:
                    tn.wait_ge(self.sems["sRC"], b_RC + 2 * ui + 1)
                    bA = tn.matmul(ps["pnorm"][0:64, 0:QBW], t["e64"][:, 0:64],
                                   t["recA"][:, 0:QBW], start=True, stop=True)
                    self.inc(bA, "tnm", 1)
                    tn.wait_ge(self.sems["sRC"], b_RC + 2 * ui + 2)
                    tn.wait_ge(self.sems["sBC"], b_BC + 2 * ui + 1)
                    bB = tn.matmul(ps["pnorm"][0:64, 0:QBW], t["e64"][:, 0:64],
                                   t["recB"][:, 0:QBW], start=True, stop=True)
                    self.inc(bB, "tnm", 1)

        @blk.scalar
        def _(sc):
            for fi, (ui, li, qb, j, kt, coff, N, first, last) in enumerate(flat):
                hp = g * PPG + li
                sc.wait_ge(self.sems["tS"], b_tS + fi + 1)
                if fi >= 2:
                    sc.wait_ge(self.sems["tPV"], b_PV + fi - 1)
                eA = sc.activation(t["expA"][fi % 2][0:128, 0:N],
                                   ps["pp"][fi % 2][0:128, coff:coff + N], AF.Exp,
                                   bias=t["negB"][:, hp:hp + 1],
                                   scale=t["cpool"][:, 2:3])
                self.inc(eA, "sE", 1)
                eB = sc.activation(t["expB"][fi % 2][0:128, 0:N],
                                   ps["pswp"][fi % 2][0:128, coff:coff + N], AF.Exp,
                                   bias=t["negB"][:, hp:hp + 1],
                                   scale=t["cpool"][:, 2:3])
                self.inc(eB, "sE", 1)
                evacs = qsched.get((ui, j - 1), [])
                if last:
                    evacs = evacs + qsched.get((ui, j), [])
                for (jt_, q_) in evacs:
                    Q = NQ * jt_ + q_
                    sc.wait_ge(self.sems["tQ"], Q + 1)
                    NB_ = len(t["qth"])
                    if jt_ >= NB_ and q_ == 0:
                        sc.wait_ge(self.sems["vHq"], jt_ - NB_ + 1)
                    ev = sc.activation(
                        t["qth"][jt_ % NB_][:, q_ * 512:(q_ + 1) * 512],
                        ps["phq"][:, 0:512], AF.Relu,
                        bias=t["b1sb"][:, jt_:jt_ + 1])
                    self.inc(ev, "sH", 1)
                if last:
                    sc.wait_ge(self.sems["tPV"], b_PV + cum[ui + 1])
                    sc.activation(t["lnz"][0:1, 0:QBW],
                                  ps["pvA"][64:65, 0:QBW], AF.Ln)
                    rA = sc.activation(t["recA"][64:65, 0:QBW],
                                       t["lnz"][0:1, 0:QBW], AF.Exp,
                                       scale=t["cpool"][64:65, 3:4])
                    self.inc(rA, "sRC", 1)
                    sc.activation(t["lnz"][0:1, 0:QBW],
                                  ps["pvB"][64:65, 0:QBW], AF.Ln)
                    rB = sc.activation(t["recB"][64:65, 0:QBW],
                                       t["lnz"][0:1, 0:QBW], AF.Exp,
                                       scale=t["cpool"][64:65, 3:4])
                    self.inc(rB, "sRC", 1)
                    sc.wait_ge(self.sems["tnm"], b_nm + 2 * ui + 1)
                    cA = sc.activation(t["bcst"][0:64, 0:QBW],
                                       ps["pnorm"][0:64, 0:QBW], AF.Copy)
                    self.inc(cA, "sBC", 1)
                    sc.wait_ge(self.sems["tnm"], b_nm + 2 * ui + 2)
                    cB = sc.activation(t["bcstB"][0:64, 0:QBW],
                                       ps["pnorm"][0:64, 0:QBW], AF.Copy)
                    self.inc(cB, "sBC", 1)

        @blk.gpsimd
        def _(gp):
            for ui in range(len(units)):
                for r in range(TPS):
                    w = (g * len(units) + ui) * TPS + r
                    jt = w
                    if jt >= 1:
                        gp.wait_ge(self.sems["tQ"], NQ * jt)
                    self.inc(gp.dma_start(
                        t["w1s1"][:, :],
                        bass.AP(d["w1b"], jt * E * 128,
                                [[128, 128], [128 * 128, KE], [1, 128]])),
                        "dW1", 16)
                    if w >= 1:
                        gp.wait_ge(self.sems["vW2"], 2 * w)
                    self.inc(gp.dma_start(t["w2st"][:, :],
                                          d["w2t"][w * 128:(w + 1) * 128, :]),
                             "dW", 16)

        @blk.vector
        def _(vec):
            def h_chain(jt):
                vec.wait_ge(self.sems["sH"], NQ * (jt + 1))
                if jt >= 1:
                    vec.wait_ge(self.sems["dHq"], 16 * jt)
                h = t["qth"][jt % len(t["qth"])][:, 0:T]
                gD = t["qraw"][0][:, 0:T]
                gX = t["qraw"][1][:, 0:T]
                gY = t["wbuf"][:, 0:T]
                vec.tensor_single_scalar(gY, h, 4.0, OP.is_ge)
                vec.tensor_scalar(gD, gY, 255.0 * 2.0 ** 43, 2.0 ** 43,
                                  OP.mult, OP.add)
                for B in (4, 2, 1):
                    vec.tensor_scalar_mul(gX, gD, 2.0 ** (B - 49))
                    vec.tensor_tensor(gY, gX, h, OP.is_ge)
                    vec.tensor_scalar(gY, gY, -(2.0 ** B - 1.0), 2.0 ** B,
                                      OP.mult, OP.add)
                    vec.tensor_mul(gD, gD, gY)
                vec.tensor_scalar_mul(gX, gD, 1.5 * 2.0 ** -45)
                vec.tensor_add(gY, h, gX)
                vec.tensor_sub(gY, gY, gX)
                vec.tensor_scalar_mul(gX, gD, 1.5)
                vec.tensor_add(gD, h, gX)
                vec.tensor_sub(gD, gD, gX)
                vec.tensor_sub(gX, gD, gY)
                vec.tensor_mul(gX, gX, t["mgad"][:, 0:T])
                qv = vec.tensor_add(t["hqw"][0][:, 0:T], gY, gX)
                self.inc(qv, "vHq", 1)

            def w2_chain(w):
                vec.wait_ge(self.sems["dW"], 16 * (w + 1))
                vec.tensor_scalar_mul(t["wA"][:, :], t["w2st"][:, :], -1.0)
                vec.tensor_max(t["wC"][:, :], t["wA"][:, :], t["w2st"][:, :])
                vec.tensor_single_scalar(t["wA"][:, :], t["wC"][:, :], 0.25,
                                         OP.is_ge)
                vec.tensor_scalar(t["wD"][:, :], t["wA"][:, :], 15.0 * 2.0 ** 43,
                                  2.0 ** 43, OP.mult, OP.add)
                for B in (2, 1):
                    vec.tensor_scalar_mul(t["wT"][:, :], t["wD"][:, :],
                                          2.0 ** (B - 49))
                    vec.tensor_tensor(t["wA"][:, :], t["wT"][:, :], t["wC"][:, :],
                                      OP.is_ge)
                    vec.tensor_scalar(t["wA"][:, :], t["wA"][:, :],
                                      -(2.0 ** B - 1.0), 2.0 ** B, OP.mult, OP.add)
                    vec.tensor_mul(t["wD"][:, :], t["wD"][:, :], t["wA"][:, :])
                vec.tensor_scalar_mul(t["wT"][:, :], t["wD"][:, :], 1.5)
                vec.tensor_add(t["wC"][:, :], t["w2st"][:, :], t["wT"][:, :])
                vec.tensor_sub(t["wC"][:, :], t["wC"][:, :], t["wT"][:, :])
                vec.tensor_scalar_mul(t["wT"][:, :], t["wD"][:, :],
                                      1.5 * 2.0 ** -29)
                a = vec.tensor_add(t["w2st"][:, :], t["w2st"][:, :], t["wT"][:, :])
                self.inc(a, "vW2", 1)
                vec.tensor_sub(t["wA"][:, :], t["w2st"][:, :], t["wT"][:, :])
                vec.tensor_sub(t["w2st"][:, :], t["wC"][:, :], t["wA"][:, :])
                vec.tensor_mul(t["w2st"][:, :], t["w2st"][:, :], t["mgad"][:, 0:E])
                q2 = vec.tensor_add(t["w2q"][:, w * E:(w + 1) * E],
                                    t["wA"][:, :], t["w2st"][:, :])
                self.inc(q2, "vW2", 1)

            for ui, (li, qb, items) in enumerate(units):
                gsl = sl0 + ui
                if gsl >= 1:
                    vec.wait_ge(self.sems["dSp"], 32 * gsl)
                vec.wait_ge(self.sems["sBC"], b_BC + 2 * ui + 1)
                mA = vec.tensor_mul(t["slabA"][ui % 2][0:64, 0:QBW],
                                    ps["pvA"][0:64, 0:QBW], t["bcst"][0:64, 0:QBW])
                self.inc(mA, "vNR", 1)
                vec.wait_ge(self.sems["sBC"], b_BC + 2 * ui + 2)
                mB = vec.tensor_mul(t["slabB"][ui % 2][0:64, 0:QBW],
                                    ps["pvB"][0:64, 0:QBW], t["bcstB"][0:64, 0:QBW])
                self.inc(mB, "vNR", 1)
                for r in range(TPS):
                    w2_chain((g * len(units) + ui) * TPS + r)
                s_now = g * U + ui
                cslots = [s_now - 1] if s_now >= 1 else []
                if s_now == NSLOT - 1:
                    cslots.append(s_now)
                for cs in cslots:
                    for jr in range(TPS):
                        h_chain(cs * TPS + jr)

        @blk.sync
        def _(sync):
            def hq_spill(jt):
                sync.wait_ge(self.sems["vHq"], jt + 1)
                self.inc(sync.dma_start(
                    d["hqspill"][jt * 128:(jt + 1) * 128, :],
                    t["hqw"][0][:, 0:T]), "dHq", 16)

            for ui, (li, qb, items) in enumerate(units):
                hp = g * PPG + li
                s_now = g * U + ui
                for jr in range(TPS):
                    if s_now >= 1:
                        hq_spill((s_now - 1) * TPS + jr)
                    if s_now == NSLOT - 1:
                        pass
                sync.wait_ge(self.sems["vNR"], b_NR + 2 * ui + 1)
                self.inc(sync.dma_start(
                    d["aspill"][hp * 128: hp * 128 + 64, qb * QBW:(qb + 1) * QBW],
                    t["slabA"][ui % 2][0:64, 0:QBW]), "dSp", 16)
                sync.wait_ge(self.sems["vNR"], b_NR + 2 * ui + 2)
                self.inc(sync.dma_start(
                    d["aspill"][hp * 128 + 64: (hp + 1) * 128, qb * QBW:(qb + 1) * QBW],
                    t["slabB"][ui % 2][0:64, 0:QBW]), "dSp", 16)
            if g == cfg.NGRP - 1:
                for jr in range(TPS):
                    hq_spill((NSLOT - 1) * TPS + jr)
            sync.wait_ge(self.sems["dSp"], self.cnt["dSp"])


    def phase_out(self, blk):
        cfg, t, d, ps = self.cfg, self.t, self.d, self.ps
        T, E, JT, KE, TT = cfg.T, cfg.E, cfg.JT, cfg.KE, cfg.TT
        NE, NEW = cfg.NE, cfg.NEW
        b_dS, b_dG = self.cnt["dS"], self.cnt["dG"]
        b_F2, b_WO = self.cnt["tF2"], self.cnt["tWO"]
        b_EP = self.cnt["vEP"]
        b_dX2, b_dO = self.cnt["dX2"], self.cnt["dO"]
        b_dSp = self.cnt["dSp"]

        @blk.gpsimd
        def _(gp):
            for kc in range(KE):
                self.inc(gp.dma_start(t["wosb"][:, kc * E:(kc + 1) * E],
                                      d["wo"][kc * 128:(kc + 1) * 128, :]), "dG", 16)

        @blk.sync
        def _(sync):
            self.inc(sync.dma_start(t["b2rep"][:, :],
                                    bass.AP(d["b2r"], 0, [[0, 128], [1, E]])), "dS", 16)
            sync.wait_ge(self.sems["dSp"], b_dSp)

            def out_dma(k):
                sync.wait_ge(self.sems["vEP"], b_EP + 1 + 2 * k + 2)
                self.inc(sync.dma_start(d["out"][k * 128:(k + 1) * 128, :],
                                        t["obuf"][k % 2][:, :]), "dO", 16)

            for tt in range(TT):
                if tt >= 2:
                    sync.wait_ge(self.sems["tF2"], b_F2 + tt - 1)
                self.inc(sync.dma_start(
                    t["hqb"][tt % 2][:, :],
                    bass.AP(d["hqspill"], tt * 128,
                            [[T, 128], [128 * T, JT], [1, 128]])), "dS", 16)
                if tt >= 2:
                    sync.wait_ge(self.sems["tWO"], b_WO + tt - 1)
                self.inc(sync.dma_start(
                    t["atb"][tt % 2][:, :],
                    bass.AP(d["aspill"], tt * 128,
                            [[T, 128], [128 * T, KE], [1, 128]])), "dS", 16)
                if tt >= 2:
                    sync.wait_ge(self.sems["vEP"], b_EP + 1 + 2 * (tt - 2) + 2)
                self.inc(sync.dma_start(t["xbuf"][tt % 2][:, :],
                                        d["x"][tt * 128:(tt + 1) * 128, :]), "dX2", 16)
                if tt >= 2:
                    out_dma(tt - 2)
            out_dma(TT - 2)
            out_dma(TT - 1)
            sync.wait_ge(self.sems["dO"], b_dO + 16 * TT)

        @blk.tensor
        def _(tn):
            tn.wait_ge(self.sems["dG"], b_dG + 16 * KE)
            for tt in range(TT):
                if tt >= 2:
                    tn.wait_ge(self.sems["vEP"], b_EP + 1 + 2 * (tt - 2) + 1)
                tn.wait_ge(self.sems["dS"], b_dS + 16 * (2 * tt + 2))
                for jc in range(JT):
                    for ne in range(NE):
                        mm = tn.matmul(
                            ps["psf"][tt % 2][:, ne * NEW:(ne + 1) * NEW],
                            t["hqb"][tt % 2][:, jc * 128:(jc + 1) * 128],
                            t["w2q"][:, jc * E + ne * NEW: jc * E + (ne + 1) * NEW],
                            start=(jc == 0), stop=(jc == JT - 1))
                self.inc(mm, "tF2", 1)
                tn.wait_ge(self.sems["dS"], b_dS + 16 * (2 * tt + 3))
                if tt >= 2:
                    tn.wait_ge(self.sems["vEP"], b_EP + 1 + 2 * (tt - 2) + 2)
                for kc in range(KE):
                    for ne in range(NE):
                        mm = tn.matmul(
                            ps["pso"][tt % 2][:, ne * NEW:(ne + 1) * NEW],
                            t["atb"][tt % 2][:, kc * 128:(kc + 1) * 128],
                            t["wosb"][:, kc * E + ne * NEW: kc * E + (ne + 1) * NEW],
                            start=(kc == 0), stop=(kc == KE - 1))
                self.inc(mm, "tWO", 1)

        @blk.vector
        def _(vec):
            def gq5(xn_f, out_ap):
                vec.tensor_scalar_mul(t["oA"][:, :], xn_f, -1.0)
                vec.tensor_max(t["oav"][:, :], t["oA"][:, :], xn_f)
                vec.tensor_single_scalar(t["oA"][:, :], t["oav"][:, :], 2.0 ** -6,
                                         OP.is_ge)
                vec.tensor_scalar(t["oD"][:, :], t["oA"][:, :], 255.0 * 2.0 ** 28,
                                  2.0 ** 28, OP.mult, OP.add)
                for B in (8, 4, 2, 1):
                    vec.tensor_scalar_mul(t["oT"][:, :], t["oD"][:, :],
                                          2.0 ** (B - 42))
                    vec.tensor_tensor(t["oA"][:, :], t["oT"][:, :], t["oav"][:, :],
                                      OP.is_ge)
                    vec.tensor_scalar(t["oA"][:, :], t["oA"][:, :],
                                      -(2.0 ** B - 1.0), 2.0 ** B, OP.mult, OP.add)
                    vec.tensor_mul(t["oD"][:, :], t["oD"][:, :], t["oA"][:, :])
                vec.tensor_scalar_mul(t["oT"][:, :], t["oD"][:, :],
                                      1.5 * 2.0 ** -29)
                vec.tensor_add(t["oB"][:, :], xn_f, t["oT"][:, :])
                vec.tensor_sub(t["oB"][:, :], t["oB"][:, :], t["oT"][:, :])
                vec.tensor_scalar_mul(t["oT"][:, :], t["oD"][:, :], 1.5)
                vec.tensor_add(t["oC"][:, :], xn_f, t["oT"][:, :])
                vec.tensor_sub(t["oC"][:, :], t["oC"][:, :], t["oT"][:, :])
                vec.tensor_sub(t["oS"][:, :], t["oC"][:, :], t["oB"][:, :])
                vec.tensor_mul(t["oS"][:, :], t["oS"][:, :], t["mgad"][:, 0:E])
                return vec.tensor_add(out_ap, t["oB"][:, :], t["oS"][:, :])

            vec.wait_ge(self.sems["dS"], b_dS + 16)
            qb2 = gq5(t["b2rep"][:, :], t["b2q"][:, :])
            self.inc(qb2, "vEP", 1)
            for tt in range(TT):
                vec.wait_ge(self.sems["tF2"], b_F2 + tt + 1)
                vec.tensor_add(t["o2b"][:, :], ps["psf"][tt % 2][:, 0:E],
                               t["b2q"][:, :])
                q5 = gq5(t["o2b"][:, :], t["o2b"][:, :])
                self.inc(q5, "vEP", 1)
                vec.wait_ge(self.sems["tWO"], b_WO + tt + 1)
                vec.wait_ge(self.sems["dX2"], b_dX2 + 16 * (tt + 1))
                if tt >= 2:
                    vec.wait_ge(self.sems["dO"], b_dO + 16 * (tt - 1))
                vec.tensor_add(t["oS"][:, :], t["o2b"][:, :],
                               ps["pso"][tt % 2][:, 0:E])
                fin = vec.tensor_add(t["obuf"][tt % 2][:, :], t["oS"][:, :],
                                     t["xbuf"][tt % 2][:, :])
                self.inc(fin, "vEP", 1)


_BUILD_CACHE = {}


def get_nc(cfg: Cfg):
    key = (cfg.T, cfg.E, cfg.HEADS, cfg.LAT, cfg.HF, cfg.PAGE)
    if key not in _BUILD_CACHE:
        _BUILD_CACHE[key] = Builder(cfg).build()
    return _BUILD_CACHE[key]


def make_host_constants(cfg: Cfg):
    T = cfg.T
    bf = ml_dtypes.bfloat16
    inv = 1.0 / 10000.0 ** (np.arange(0, 64, 2, dtype=np.float32) / 64.0)
    ang = np.outer(np.arange(T, dtype=np.float32), inv)
    cos, sin = np.cos(ang), np.sin(ang)
    CT = np.zeros((128, T), np.float32)
    ST = np.zeros((128, T), np.float32)
    for p in range(128):
        j = p % 64
        jj = j % 32
        CT[p] = cos[:, jj]
        ST[p] = (-sin[:, jj]) if j < 32 else sin[:, jj]
    P = np.zeros((128, 128), np.float32)
    for i in range(128):
        jj = i + 32 if (i % 64) < 32 else i - 32
        P[jj, i] = 1.0
    E64 = np.zeros((128, 64), np.float32)
    E64[64, :] = 1.0
    return {
        "tc": np.ascontiguousarray(CT),
        "ts": np.ascontiguousarray(ST),
        "Pswap": np.ascontiguousarray(P.astype(bf)),
        "identin": np.ascontiguousarray(np.eye(128, dtype=np.float32).astype(bf)),
        "e64in": np.ascontiguousarray(E64.astype(bf)),
        "onesin": np.ascontiguousarray(np.ones((128, 128), np.float32).astype(bf)),
        "onescol": np.ascontiguousarray(np.ones((128, 1), np.float32).astype(bf)),
    }


def head_perm(E):
    perm = []
    for h in range(E // 64):
        base = h * 64
        perm += [base + 2 * i for i in range(32)] + [base + 2 * i + 1 for i in range(32)]
    return np.array(perm)


def prepare_inputs(cfg: Cfg, x, wq, w_down, wk_up, wv_up, wo, w1, b1, w2, b2):
    perm = head_perm(cfg.E)
    c = lambda a: np.ascontiguousarray(np.asarray(a), dtype=np.float32)
    wq, w_down, wk_up, wv_up, wo = (np.asarray(a, dtype=np.float32)
                                    for a in (wq, w_down, wk_up, wv_up, wo))
    w1, b1, w2, b2 = (np.asarray(a, dtype=np.float32) for a in (w1, b1, w2, b2))
    base = {
        "wq": c(wq[:, perm]),
        "wdn": c(w_down),
        "wku": c(wk_up[:, perm]),
        "wvu": c(wv_up),
        "wo": c(wo),
        "w1b": c(w1.T.reshape(cfg.E, cfg.JT, 128).transpose(1, 0, 2)),
        "w2t": c(w2.T),
        "b1r": c(b1.reshape(cfg.JT, 128).T),
        "b2r": c(b2.reshape(1, cfg.E)),
    }
    base.update(make_host_constants(cfg))
    in_maps = []
    for i in range(x.shape[0]):
        m = dict(base)
        m["x"] = c(x[i])
        in_maps.append(m)
    return in_maps


def kernel(L0_attn_x, L0_attn_wq, L0_attn_w_down, L0_attn_wk_up, L0_attn_wv_up,
           L0_attn_wo, L0_ffn_w1, L0_ffn_b1, L0_ffn_w2, L0_ffn_b2):
    cfg = Cfg()
    x = np.asarray(L0_attn_x, dtype=np.float32)
    in_maps = prepare_inputs(cfg, x, L0_attn_wq, L0_attn_w_down, L0_attn_wk_up,
                             L0_attn_wv_up, L0_attn_wo, L0_ffn_w1, L0_ffn_b1,
                             L0_ffn_w2, L0_ffn_b2)
    nc = get_nc(cfg)
    res = run_bass_kernel_spmd(nc, in_maps, core_ids=list(range(cfg.n_cores)))
    return np.stack([np.asarray(r["out"], dtype=np.float32) for r in res.results])


# version 17
# speedup vs baseline: 1.1450x; 1.1450x over previous
import contextlib
import math

import numpy as np
import ml_dtypes

import concourse.bass as bass
import concourse.mybir as mybir
from concourse.bass_utils import run_bass_kernel_spmd

F32 = mybir.dt.float32
BF16 = mybir.dt.bfloat16
F8E4 = mybir.dt.float8e4
I16 = mybir.dt.int16
I32 = mybir.dt.int32
AF = mybir.ActivationFunctionType
OP = mybir.AluOpType
AX = mybir.AxisListType
LN2 = math.log(2.0)
INVLN2 = 1.0 / LN2
LN2_HI = float(np.float32(LN2))
LN2_LO = float(np.float64(LN2) - np.float64(np.float32(LN2)))


class Cfg:
    def __init__(self, T=2048, E=1024, HEADS=16, LAT=512, HF=4096, PAGE=256,
                 n_cores=8, ngrp=None):
        self.T, self.E, self.HEADS, self.LAT, self.HF = T, E, HEADS, LAT, HF
        self.PAGE = PAGE
        self.n_cores = n_cores
        self.D = 64
        assert E == HEADS * 64
        self.PAIRS = HEADS // 2
        self.TT = T // 128
        self.KE = E // 128
        self.KL = LAT // 128
        self.JT = HF // 128
        self.QBW = min(512, T)
        self.NQB = T // self.QBW
        self.KTILES = T // 128
        self.NBW = min(512, T)
        self.NB = T // self.NBW
        self.NEW = min(512, E)
        self.NE = E // self.NEW
        self.NGRP = ngrp if ngrp is not None else min(4, self.PAIRS)
        assert self.PAIRS % self.NGRP == 0
        self.PPG = self.PAIRS // self.NGRP
        self.VW = 65

    def score_items(self, qb):
        out = []
        j = 0
        for kt in range(self.KTILES):
            qlo = (kt * 128 // self.PAGE) * self.PAGE
            qs = max(qb * self.QBW, qlo)
            qe = (qb + 1) * self.QBW
            if qs >= qe:
                break
            out.append((j, kt, qs - qb * self.QBW, qe - qs))
            j += 1
        return out


class Builder:
    def __init__(self, cfg: Cfg):
        self.cfg = cfg
        self.nc = bass.Bass("TRN2", target_bir_lowering=False,
                            detect_race_conditions=False,
                            dynamic_dma_scratch_size=16384)
        self.es = contextlib.ExitStack()
        self.cnt = {}

    def sem(self, name):
        h = self.es.enter_context(self.nc.semaphore(name))
        self.cnt[name] = 0
        return h

    def inc(self, instr, name, n=1):
        instr.then_inc(self.sems[name], n)
        self.cnt[name] += n
        return self.cnt[name]

    def vfence(self, vec, instr):
        c = self.inc(instr, "vrd", 1)
        vec.wait_ge(self.sems["vrd"], c)

    def build(self):
        cfg = self.cfg
        nc = self.nc
        T, E, LAT, HF = cfg.T, cfg.E, cfg.LAT, cfg.HF
        TT, KE, KL, JT = cfg.TT, cfg.KE, cfg.KL, cfg.JT
        QBW, VW = cfg.QBW, cfg.VW
        S = self.es.enter_context
        d = self.d = {}
        t = self.t = {}

        d["x"] = nc.dram_tensor("x", [T, E], F32, kind="ExternalInput")
        d["wq"] = nc.dram_tensor("wq", [E, E], F32, kind="ExternalInput")
        d["wdn"] = nc.dram_tensor("wdn", [E, LAT], F32, kind="ExternalInput")
        d["wku"] = nc.dram_tensor("wku", [LAT, E], F32, kind="ExternalInput")
        d["wvu"] = nc.dram_tensor("wvu", [LAT, E], F32, kind="ExternalInput")
        d["wo"] = nc.dram_tensor("wo", [E, E], F32, kind="ExternalInput")
        d["w1b"] = nc.dram_tensor("w1b", [JT, E, 128], F32, kind="ExternalInput")
        d["w2t"] = nc.dram_tensor("w2t", [HF, E], F32, kind="ExternalInput")
        d["b1r"] = nc.dram_tensor("b1r", [128, JT], F32, kind="ExternalInput")
        d["b2r"] = nc.dram_tensor("b2r", [1, E], F32, kind="ExternalInput")
        d["tc"] = nc.dram_tensor("tc", [128, T], F32, kind="ExternalInput")
        d["ts"] = nc.dram_tensor("ts", [128, T], F32, kind="ExternalInput")
        d["Pswap"] = nc.dram_tensor("Pswap", [128, 128], BF16, kind="ExternalInput")
        d["identin"] = nc.dram_tensor("identin", [128, 128], BF16, kind="ExternalInput")
        d["e64in"] = nc.dram_tensor("e64in", [128, 64], BF16, kind="ExternalInput")
        d["onesin"] = nc.dram_tensor("onesin", [128, 128], BF16, kind="ExternalInput")
        d["onescol"] = nc.dram_tensor("onescol", [128, 1], BF16, kind="ExternalInput")
        d["out"] = nc.dram_tensor("out", [T, E], F32, kind="ExternalOutput")
        spill_kind = "ExternalOutput" if getattr(self, "debug", False) else "Internal"
        d["aspill"] = nc.dram_tensor("aspill", [E, T], BF16, kind=spill_kind)
        if getattr(self, "debug", False):
            d["dbg_qT"] = nc.dram_tensor("dbg_qT", [128, cfg.PPG * T], F32, kind="ExternalOutput")
            d["dbg_kT"] = nc.dram_tensor("dbg_kT", [128, cfg.PPG * T], F32, kind="ExternalOutput")
            d["dbg_ones"] = nc.dram_tensor("dbg_ones", [128, 128], F32, kind="ExternalOutput")
            d["dbg_negB"] = nc.dram_tensor("dbg_negB", [128, cfg.PAIRS], F32, kind="ExternalOutput")
            d["dbg_b2q"] = nc.dram_tensor("dbg_b2q", [128, E], F32, kind="ExternalOutput")
            d["dbg_hq"] = nc.dram_tensor("dbg_hq", [128, T], F32, kind="ExternalOutput")
            d["dbg_w2q"] = nc.dram_tensor("dbg_w2q", [128, E], F32, kind="ExternalOutput")
            d["dbg_xnT"] = nc.dram_tensor("dbg_xnT", [128, T], F32, kind="ExternalOutput")


        L = dict(side="left")
        t["cpool"] = S(nc.sbuf_tensor("cpool", [128, 12], F32, **L))
        t["ones_c"] = S(nc.sbuf_tensor("ones_c", [128, 1], BF16, **L))
        t["onesf"] = S(nc.sbuf_tensor("onesf", [128, 128], BF16, **L))
        t["e64"] = S(nc.sbuf_tensor("e64s", [128, 64], BF16, **L))
        t["Psw"] = S(nc.sbuf_tensor("Psws", [128, 128], BF16, **L))
        t["ident"] = S(nc.sbuf_tensor("idents", [128, 128], BF16, **L))
        t["negB"] = S(nc.sbuf_tensor("negB", [128, cfg.PAIRS], F32, **L))
        t["b1sb"] = S(nc.sbuf_tensor("b1sb", [128, JT], F32, **L))
        t["stats"] = S(nc.sbuf_tensor("stats", [128, 8], F32, **L))
        t["s11"] = S(nc.sbuf_tensor("s11", [1, 8], F32, **L))
        t["xbuf"] = [S(nc.sbuf_tensor(f"xbuf{i}", [128, E], F32, **L)) for i in range(2)]
        t["w2q"] = S(nc.sbuf_tensor("w2q", [128, JT * E], F8E4, **L))

        R = dict(side="right")
        t["xnT"] = S(nc.sbuf_tensor("xnT", [128, KE * T], BF16, **R))

        self.sems = {n: self.sem(n) for n in (
            "dS", "dG", "dO", "dSp", "dX2",
            "vxn", "vvar", "vev", "ttr", "sln",
            "tpj", "vpe", "tsw", "vrp", "tnm", "vnm",
            "tS", "sE", "tPV", "vNR", "sRC", "sBC", "vPC",
            "vW2", "dW",
            "tF1", "sH", "vH",
            "tF2", "tWO", "vEP", "sEP", "vrd",
        )}

        t["mgad"] = S(nc.sbuf_tensor("mgad", [128, T], BF16, **R))
        t["w2st"] = S(nc.sbuf_tensor("w2st", [128, E], F32, **R))
        t["wD"] = S(nc.sbuf_tensor("wD", [128, E], BF16, **R))
        t["wA"] = S(nc.sbuf_tensor("wA", [128, E], BF16, **R))
        t["wC"] = S(nc.sbuf_tensor("wC", [128, E], BF16, **R))
        t["wT"] = S(nc.sbuf_tensor("wT", [128, E], BF16, **R))

        sc1 = contextlib.ExitStack()
        t["xcb"] = sc1.enter_context(nc.sbuf_tensor("xcb", [128, E], F32, **R))
        t["xc2b"] = sc1.enter_context(nc.sbuf_tensor("xc2b", [128, E], F32, **R))
        t["xnb"] = [sc1.enter_context(nc.sbuf_tensor(f"xnb{i}", [128, E], BF16, **R))
                    for i in range(2)]
        ps = self.ps = {}
        ps["ptr"] = [sc1.enter_context(nc.psum_tensor(f"ptr{i}", [128, 128], BF16))
                     for i in range(2)]
        with nc.Block() as blk:
            self.phase1(blk)
        sc1.close()

        sc2 = contextlib.ExitStack()
        E2 = sc2.enter_context
        t["tc"] = E2(nc.sbuf_tensor("tcs", [128, T], F32, **R))
        t["ts"] = E2(nc.sbuf_tensor("tss", [128, T], F32, **R))
        t["latT"] = E2(nc.sbuf_tensor("latT", [128, KL * T], BF16, **R))
        t["vext"] = E2(nc.sbuf_tensor("vext", [128, TT * cfg.HEADS * VW], BF16, **R))
        t["qT"] = E2(nc.sbuf_tensor("qT", [128, cfg.PPG * T], BF16, **R))
        t["kT"] = E2(nc.sbuf_tensor("kT", [128, cfg.PPG * T], BF16, **R))
        t["wbuf"] = E2(nc.sbuf_tensor("wbuf", [128, max(KE * LAT, KL * E, (KE + KL) * cfg.PPG * 128, 2 * cfg.QBW)], BF16, **R))
        t["qraw"] = [E2(nc.sbuf_tensor(f"qraw{i}", [128, T], BF16, **R)) for i in range(2)]
        t["ttmp"] = E2(nc.sbuf_tensor("ttmp", [128, cfg.NBW], BF16, **R))
        t["ttmp2"] = E2(nc.sbuf_tensor("ttmp2", [128, cfg.NBW], BF16, **R))
        t["expA"] = [E2(nc.sbuf_tensor(f"expA{i}", [128, QBW], BF16, **R)) for i in range(2)]
        t["expB"] = [E2(nc.sbuf_tensor(f"expB{i}", [128, QBW], BF16, **R)) for i in range(2)]
        t["recA"] = E2(nc.sbuf_tensor("recA", [128, QBW], BF16, **R))
        t["lnz"] = E2(nc.sbuf_tensor("lnz", [1, QBW], F32, **R))
        t["pvcA"] = t["wbuf"][:, 0:cfg.QBW]
        t["pvcB"] = t["wbuf"][:, cfg.QBW:2 * cfg.QBW]
        t["recB"] = E2(nc.sbuf_tensor("recB", [128, QBW], BF16, **R))
        t["bcst"] = E2(nc.sbuf_tensor("bcst", [64, QBW], BF16, **R))
        t["bcstB"] = E2(nc.sbuf_tensor("bcstB", [64, QBW], BF16, **R))
        t["slabA"] = [E2(nc.sbuf_tensor(f"slabA{i}", [64, QBW], BF16, **R)) for i in range(2)]
        t["slabB"] = [E2(nc.sbuf_tensor(f"slabB{i}", [64, QBW], BF16, **R)) for i in range(2)]
        ps["pp"] = [E2(nc.psum_tensor(f"pp{i}", [128, 512], F32)) for i in range(2)]
        ps["pswp"] = [E2(nc.psum_tensor(f"pswp{i}", [128, 512], F32)) for i in range(2)]
        ps["pnorm"] = E2(nc.psum_tensor("pnorm", [128, 512], F32))
        ps["pvA"] = E2(nc.psum_tensor("pvA", [128, 512], F32))
        ps["pvB"] = E2(nc.psum_tensor("pvB", [128, 512], F32))
        with nc.Block() as blk:
            self.phase_proj_v(blk)
        for g in range(cfg.NGRP):
            with nc.Block() as blk:
                self.phase_proj_qk(blk, g)
            with nc.Block() as blk:
                self.phase_attn(blk, g)
        if getattr(self, "debug", False):
            with nc.Block() as blk:
                @blk.gpsimd
                def _(gp):
                    dbga = self.sem("dbga")
                    gp.dma_start(d["dbg_qT"][:, :], t["qT"][:, :]).then_inc(dbga, 16)
                    gp.dma_start(d["dbg_kT"][:, :], t["kT"][:, :]).then_inc(dbga, 16)
                    gp.dma_start(d["dbg_ones"][:, :], t["onesf"][:, :]).then_inc(dbga, 16)
                    gp.wait_ge(dbga, 48)
        sc2.close()

        t["hq"] = S(nc.sbuf_tensor("hq", [128, JT * T], F8E4, **R))
        sc4 = contextlib.ExitStack()
        E4 = sc4.enter_context
        t["w1sb"] = [E4(nc.sbuf_tensor(f"w1sb{i}", [128, KE * 128], BF16, **R))
                     for i in range(2)]
        t["qth"] = [E4(nc.sbuf_tensor(f"qth{i}", [128, T], BF16, **R)) for i in range(3)]
        t["gD"] = E4(nc.sbuf_tensor("gD", [128, T], BF16, **R))
        t["gA"] = E4(nc.sbuf_tensor("gA", [128, T], BF16, **R))
        t["gB"] = E4(nc.sbuf_tensor("gB", [128, T], BF16, **R))
        t["gC"] = E4(nc.sbuf_tensor("gC", [128, T], BF16, **R))
        t["gT"] = E4(nc.sbuf_tensor("gT", [128, T], BF16, **R))
        ps["ph4"] = [E4(nc.psum_tensor(f"ph4_{i}", [128, min(2048, T)], F32))
                     for i in range(2)]
        with nc.Block() as blk:
            self.phase_ffn1(blk)
        sc4.close()

        sc5 = contextlib.ExitStack()
        E5 = sc5.enter_context
        t["wosb"] = E5(nc.sbuf_tensor("wosb", [128, KE * E], BF16, **R))
        t["atb"] = [E5(nc.sbuf_tensor(f"atb{i}", [128, E], BF16, **R)) for i in range(2)]
        t["o2b"] = E5(nc.sbuf_tensor("o2b", [128, E], F32, **R))
        t["oav"] = E5(nc.sbuf_tensor("oav", [128, E], BF16, **R))
        t["oD"] = E5(nc.sbuf_tensor("oD", [128, E], BF16, **R))
        t["oA"] = E5(nc.sbuf_tensor("oA", [128, E], BF16, **R))
        t["oB"] = E5(nc.sbuf_tensor("oB", [128, E], F32, **R))
        t["oC"] = E5(nc.sbuf_tensor("oC", [128, E], BF16, **R))
        t["oT"] = E5(nc.sbuf_tensor("oT", [128, E], BF16, **R))
        t["oS"] = E5(nc.sbuf_tensor("oS", [128, E], F32, **R))
        t["b2q"] = E5(nc.sbuf_tensor("b2q", [128, E], F32, **R))
        t["obuf"] = [E5(nc.sbuf_tensor(f"obuf{i}", [128, E], F32, **R)) for i in range(2)]
        t["b2rep"] = t["obuf"][0]
        ps["psf"] = [E5(nc.psum_tensor(f"psf{i}", [128, E], F32)) for i in range(2)]
        ps["pso"] = [E5(nc.psum_tensor(f"pso{i}", [128, E], F32)) for i in range(2)]
        with nc.Block() as blk:
            self.phase_out(blk)
        sc5.close()

        if getattr(self, "debug", False):
            t["dbgf"] = S(nc.sbuf_tensor("dbgf", [128, T], F32, **R))
            with nc.Block() as blk:
                @blk.vector
                def _(vec):
                    vec.tensor_copy(t["dbgf"][:, 0:T], t["hq"][:, 0:T])
                @blk.gpsimd
                def _(gp):
                    dbg = self.sem("dbgd")
                    gp.wait_ge(self.sems["vEP"], self.cnt["vEP"])
                    gp.dma_start(d["dbg_negB"][:, :], t["negB"][:, :]).then_inc(dbg, 16)
                    gp.dma_start(d["dbg_b2q"][:, :], t["b2q"][:, :]).then_inc(dbg, 16)
                    gp.dma_start(d["dbg_w2q"][:, :], t["w2q"][:, 0:E]).then_inc(dbg, 16)
                    gp.dma_start(d["dbg_xnT"][:, :], t["xnT"][:, 0:T]).then_inc(dbg, 16)
                    gp.wait_ge(dbg, 64)
            with nc.Block() as blk:
                @blk.gpsimd
                def _(gp):
                    dbg2 = self.sem("dbgd2")
                    gp.dma_start(d["dbg_hq"][:, :], t["dbgf"][:, 0:T]).then_inc(dbg2, 16)
                    gp.wait_ge(dbg2, 16)

        self.es.close()
        return nc

    def phase1(self, blk):
        cfg, t, d, ps = self.cfg, self.t, self.d, self.ps
        T, E, TT, KE = cfg.T, cfg.E, cfg.TT, cfg.KE
        NCONST = 6

        @blk.sync
        def _(sync):
            self.inc(sync.dma_start(t["Psw"][:, :], d["Pswap"][:, :]), "dS", 16)
            self.inc(sync.dma_start(t["ident"][:, :], d["identin"][:, :]), "dS", 16)
            self.inc(sync.dma_start(t["e64"][:, :], d["e64in"][:, :]), "dS", 16)
            self.inc(sync.dma_start(t["ones_c"][:, 0:1], d["onescol"][:, :]), "dS", 16)
            self.inc(sync.dma_start(t["onesf"][:, :], d["onesin"][:, :]), "dS", 16)
            self.inc(sync.dma_start(t["b1sb"][:, :], d["b1r"][:, :]), "dS", 16)
            for tt in range(TT):
                if tt >= 2:
                    sync.wait_ge(self.sems["vxn"], tt - 1)
                self.inc(sync.dma_start(t["xbuf"][tt % 2][:, :],
                                        d["x"][tt * 128:(tt + 1) * 128, :]), "dS", 16)

        @blk.vector
        def _(vec):
            W = TT * cfg.HEADS * cfg.VW
            vec.memset(t["cpool"][:, 0:1], 1.0 / E)
            vec.memset(t["cpool"][:, 1:2], 1e-5)
            vec.memset(t["cpool"][:, 2:3], 0.125)
            vec.memset(t["cpool"][:, 3:4], -1.0)
            self.vfence(vec, vec.memset(t["mgad"][:, 0:T], 1024.0))
            self.vfence(vec, vec.tensor_scalar(t["mgad"][:, 0:T], t["mgad"][:, 0:T],
                                               1.0, 2.0 ** 52, OP.mult, OP.add))
            self.vfence(vec, vec.tensor_scalar(t["mgad"][:, 0:T], t["mgad"][:, 0:T],
                                               -(2.0 ** 52), 2.0 ** -10,
                                               OP.add, OP.mult))
            for tt in range(TT):
                vec.wait_ge(self.sems["dS"], 16 * (NCONST + tt + 1))
                xb = t["xbuf"][tt % 2]
                self.vfence(vec, vec.reduce_sum(t["stats"][:, 0:1], xb[:, :],
                                                axis=AX.X))
                self.vfence(vec, vec.tensor_scalar_mul(t["stats"][:, 1:2],
                                                       t["stats"][:, 0:1], 1.0 / E))
                vec.tensor_scalar(t["xcb"][:, :], xb[:, :], t["stats"][:, 1:2], None,
                                  OP.subtract)
                vec.tensor_mul(t["xc2b"][:, :], t["xcb"][:, :], t["xcb"][:, :])
                vv = vec.reduce_sum(t["stats"][:, 2:3], t["xc2b"][:, :], axis=AX.X)
                self.inc(vv, "vvar", 1)
                vec.wait_ge(self.sems["sln"], tt + 1)
                self.vfence(vec, vec.reciprocal(t["stats"][:, 4:5], t["stats"][:, 3:4]))
                xn = vec.tensor_scalar(t["xnb"][tt % 2][:, :], t["xcb"][:, :],
                                       t["stats"][:, 4:5], None, OP.mult)
                self.inc(xn, "vxn", 1)
                for cc in range(KE):
                    g = tt * KE + cc
                    vec.wait_ge(self.sems["ttr"], g + 1)
                    ev = vec.tensor_copy(
                        t["xnT"][:, cc * T + tt * 128: cc * T + (tt + 1) * 128],
                        ps["ptr"][g % 2][:, 0:128])
                    self.inc(ev, "vev", 1)

        @blk.scalar
        def _(sc):
            for tt in range(TT):
                sc.wait_ge(self.sems["vvar"], tt + 1)
                a = sc.activation(t["stats"][:, 3:4], t["stats"][:, 2:3], AF.Sqrt,
                                  bias=t["cpool"][:, 1:2], scale=t["cpool"][:, 0:1])
                self.inc(a, "sln", 1)

        @blk.tensor
        def _(tn):
            tn.wait_ge(self.sems["dS"], 16 * 2)
            for tt in range(TT):
                tn.wait_ge(self.sems["vxn"], tt + 1)
                for cc in range(KE):
                    g = tt * KE + cc
                    if g >= 2:
                        tn.wait_ge(self.sems["vev"], g - 1)
                    tr = tn.transpose(ps["ptr"][g % 2][:, 0:128],
                                      t["xnb"][tt % 2][:, cc * 128:(cc + 1) * 128],
                                      t["ident"][:, :])
                    self.inc(tr, "ttr", 1)

    def phase_proj_v(self, blk):
        cfg, t, d, ps = self.cfg, self.t, self.d, self.ps
        T, E, LAT = cfg.T, cfg.E, cfg.LAT
        TT, KE, KL = cfg.TT, cfg.KE, cfg.KL
        NB, NBW, NE, NEW = cfg.NB, cfg.NBW, cfg.NE, cfg.NEW
        VW, HEADS = cfg.VW, cfg.HEADS
        lat_items = [(lt, nb) for lt in range(KL) for nb in range(NB)]
        v_items = [(tt, fc) for tt in range(TT) for fc in range(NE)]
        b_pj, b_pe = self.cnt["tpj"], self.cnt["vpe"]
        b_dG = self.cnt["dG"]
        b_dS = self.cnt["dS"]

        @blk.sync
        def _(sync):
            self.inc(sync.dma_start(t["tc"][:, :], d["tc"][:, :]), "dS", 16)
            self.inc(sync.dma_start(t["ts"][:, :], d["ts"][:, :]), "dS", 16)
            self.tabs_dS = self.cnt["dS"]

        @blk.gpsimd
        def _(gp):
            for kc in range(KE):
                self.inc(gp.dma_start(t["wbuf"][:, kc * LAT:(kc + 1) * LAT],
                                      d["wdn"][kc * 128:(kc + 1) * 128, :]), "dG", 16)
            gp.wait_ge(self.sems["tpj"], b_pj + len(lat_items))
            for lc in range(KL):
                self.inc(gp.dma_start(t["wbuf"][:, lc * E:(lc + 1) * E],
                                      d["wvu"][lc * 128:(lc + 1) * 128, :]), "dG", 16)

        @blk.tensor
        def _(tn):
            tn.wait_ge(self.sems["dG"], b_dG + 16 * KE)
            for i, (lt, nb) in enumerate(lat_items):
                if i >= 2:
                    tn.wait_ge(self.sems["vpe"], b_pe + i - 1)
                for kc in range(KE):
                    mm = tn.matmul(ps["pp"][i % 2][:, 0:NBW],
                                   t["wbuf"][:, kc * LAT + lt * 128: kc * LAT + (lt + 1) * 128],
                                   t["xnT"][:, kc * T + nb * NBW: kc * T + (nb + 1) * NBW],
                                   start=(kc == 0), stop=(kc == KE - 1))
                self.inc(mm, "tpj", 1)
            tn.wait_ge(self.sems["dG"], b_dG + 16 * (KE + KL))
            for i2, (tt, fc) in enumerate(v_items):
                i = len(lat_items) + i2
                if i >= 2:
                    tn.wait_ge(self.sems["vpe"], b_pe + i - 1)
                for lc in range(KL):
                    mm = tn.matmul(ps["pp"][i % 2][:, 0:NEW],
                                   t["latT"][:, lc * T + tt * 128: lc * T + (tt + 1) * 128],
                                   t["wbuf"][:, lc * E + fc * NEW: lc * E + (fc + 1) * NEW],
                                   start=(lc == 0), stop=(lc == KL - 1))
                self.inc(mm, "tpj", 1)

        @blk.vector
        def _(vec):
            W = TT * HEADS * VW
            for tt in range(TT):
                vec.memset(bass.AP(t["vext"], tt * HEADS * VW + 64,
                                   [[W, 128], [VW, HEADS]]), 1.0)
            vec.memset(t["recA"][:, :], 0.0)
            vec.memset(t["recB"][:, :], 0.0)
            for i, (lt, nb) in enumerate(lat_items):
                vec.wait_ge(self.sems["tpj"], b_pj + i + 1)
                ev = vec.tensor_copy(
                    t["latT"][:, lt * T + nb * NBW: lt * T + (nb + 1) * NBW],
                    ps["pp"][i % 2][:, 0:NBW])
                self.inc(ev, "vpe", 1)
            for i2, (tt, fc) in enumerate(v_items):
                i = len(lat_items) + i2
                vec.wait_ge(self.sems["tpj"], b_pj + i + 1)
                nheads = NEW // 64
                dst = bass.AP(t["vext"], tt * HEADS * VW + fc * nheads * VW,
                              [[W, 128], [VW, nheads], [1, 64]])
                src = bass.AP(ps["pp"][i % 2], 0, [[512, 128], [64, nheads], [1, 64]])
                ev = vec.tensor_copy(dst, src)
                self.inc(ev, "vpe", 1)

    def phase_proj_qk(self, blk, g):
        cfg, t, d, ps = self.cfg, self.t, self.d, self.ps
        T, E, LAT = cfg.T, cfg.E, cfg.LAT
        KE, KL, PPG = cfg.KE, cfg.KL, cfg.PPG
        NB, NBW = cfg.NB, cfg.NBW
        WQC = PPG * 128
        b_dG = self.cnt["dG"]
        b_pj, b_pe = self.cnt["tpj"], self.cnt["vpe"]
        b_sw, b_rp = self.cnt["tsw"], self.cnt["vrp"]
        b_nm, b_vn = self.cnt["tnm"], self.cnt["vnm"]

        proj_items = [(li, w, nb) for li in range(PPG) for w in ("q", "k")
                      for nb in range(NB)]
        VPP = 2 * (1 + NB) + 1
        TPP = 2 * NB

        @blk.gpsimd
        def _(gp):
            gp.wait_ge(self.sems["tpj"], b_pj)
            for kc in range(KE):
                self.inc(gp.dma_start(
                    t["wbuf"][:, kc * WQC:(kc + 1) * WQC],
                    d["wq"][kc * 128:(kc + 1) * 128, g * WQC:(g + 1) * WQC]), "dG", 16)
            for lc in range(KL):
                self.inc(gp.dma_start(
                    t["wbuf"][:, KE * WQC + lc * WQC: KE * WQC + (lc + 1) * WQC],
                    d["wku"][lc * 128:(lc + 1) * 128, g * WQC:(g + 1) * WQC]), "dG", 16)

        @blk.tensor
        def _(tn):
            for idx, (li, w, nb) in enumerate(proj_items):
                if w == "q":
                    tn.wait_ge(self.sems["dG"], b_dG + 16 * KE)
                    nk, wof, src = KE, 0, "xnT"
                else:
                    tn.wait_ge(self.sems["dG"], b_dG + 16 * (KE + KL))
                    nk, wof, src = KL, KE * WQC, "latT"
                if idx >= 2:
                    tn.wait_ge(self.sems["vpe"], b_pe + idx - 1)
                for kc in range(nk):
                    mm = tn.matmul(
                        ps["pp"][idx % 2][:, 0:NBW],
                        t["wbuf"][:, wof + kc * WQC + li * 128: wof + kc * WQC + (li + 1) * 128],
                        t[src][:, kc * T + nb * NBW: kc * T + (nb + 1) * NBW],
                        start=(kc == 0), stop=(kc == nk - 1))
                self.inc(mm, "tpj", 1)
                tn.wait_ge(self.sems["vpe"], b_pe + idx + 1)
                if idx >= 2:
                    tn.wait_ge(self.sems["vrp"], b_rp + idx - 1)
                sw = tn.matmul(ps["pswp"][idx % 2][:, 0:NBW], t["Psw"][:, :],
                               t["qraw"][li % 2][:, nb * NBW:(nb + 1) * NBW],
                               start=True, stop=True)
                self.inc(sw, "tsw", 1)
            for li in range(PPG):
                vb = b_vn + li * VPP
                for side in range(2):
                    for c in range(NB):
                        tn.wait_ge(self.sems["vnm"], vb + side * (1 + NB) + 1 + c)
                        mm = tn.matmul(ps["pnorm"][:, 0:NBW], t["onesf"][:, :],
                                       t["qraw"][0][:, c * NBW:(c + 1) * NBW],
                                       start=True, stop=True)
                        self.inc(mm, "tnm", 1)

        @blk.vector
        def _(vec):
            vec.wait_ge(self.sems["dS"], self.tabs_dS)
            for idx, (li, w, nb) in enumerate(proj_items):
                vec.wait_ge(self.sems["tpj"], b_pj + idx + 1)
                ev = vec.tensor_copy(t["qraw"][li % 2][:, nb * NBW:(nb + 1) * NBW],
                                     ps["pp"][idx % 2][:, 0:NBW])
                self.inc(ev, "vpe", 1)
                vec.wait_ge(self.sems["tsw"], b_sw + idx + 1)
                dstT = t["qT"] if w == "q" else t["kT"]
                cols = slice(nb * NBW, (nb + 1) * NBW)
                vec.tensor_mul(t["ttmp"][:, 0:NBW], t["qraw"][li % 2][:, cols],
                               t["tc"][:, cols])
                vec.tensor_mul(t["ttmp2"][:, 0:NBW], ps["pswp"][idx % 2][:, 0:NBW],
                               t["ts"][:, cols])
                rp = vec.tensor_add(dstT[:, li * T + nb * NBW: li * T + (nb + 1) * NBW],
                                    t["ttmp"][:, 0:NBW], t["ttmp2"][:, 0:NBW])
                self.inc(rp, "vrp", 1)
            for li in range(PPG):
                hp = g * PPG + li
                tb = b_nm + li * TPP
                for side, srcT in ((0, t["qT"]), (1, t["kT"])):
                    vec.wait_ge(self.sems["tnm"], tb + side * NB)
                    sq = vec.tensor_mul(t["qraw"][0][:, 0:T],
                                        srcT[:, li * T:(li + 1) * T],
                                        srcT[:, li * T:(li + 1) * T])
                    self.inc(sq, "vnm", 1)
                    for c in range(NB):
                        vec.wait_ge(self.sems["tnm"], tb + side * NB + c + 1)
                        self.vfence(vec, vec.reduce_max(t["stats"][:, 5:6],
                                                        ps["pnorm"][:, 0:NBW],
                                                        axis=AX.X))
                        if c == 0:
                            mx = vec.tensor_copy(t["stats"][:, 6 + side:7 + side],
                                                 t["stats"][:, 5:6])
                        else:
                            mx = vec.tensor_max(t["stats"][:, 6 + side:7 + side],
                                                t["stats"][:, 6 + side:7 + side],
                                                t["stats"][:, 5:6])
                        self.vfence(vec, mx)
                        self.inc(vec.tensor_copy(t["stats"][:, 4:5],
                                                 t["stats"][:, 5:6]), "vnm", 1)
                self.vfence(vec, vec.tensor_add(t["stats"][:, 5:6], t["stats"][:, 6:7],
                                                t["stats"][:, 7:8]))
                nb_ = vec.tensor_scalar_mul(t["negB"][:, hp:hp + 1],
                                            t["stats"][:, 5:6], -0.0625)
                self.inc(nb_, "vnm", 1)

    def phase_attn(self, blk, g):
        cfg, t, d, ps = self.cfg, self.t, self.d, self.ps
        T, QBW, VW, HEADS = cfg.T, cfg.QBW, cfg.VW, cfg.HEADS
        PPG, NQB, JT, E = cfg.PPG, cfg.NQB, cfg.JT, cfg.E
        b_tS, b_sE = self.cnt["tS"], self.cnt["sE"]
        b_PV, b_NR = self.cnt["tPV"], self.cnt["vNR"]
        b_RC, b_BC = self.cnt["sRC"], self.cnt["sBC"]
        b_PC = self.cnt["vPC"]
        b_nm = self.cnt["tnm"]
        b_dSp = self.cnt["dSp"]
        sl0 = b_dSp // 32

        units = []
        for li in range(PPG):
            for qb in range(NQB):
                units.append((li, qb, cfg.score_items(qb)))
        assert cfg.JT % (cfg.NGRP * len(units)) == 0, "w2 tile/unit mismatch"
        TPS = cfg.JT // (cfg.NGRP * len(units))
        flat = []
        for ui, (li, qb, items) in enumerate(units):
            for (j, kt, coff, N) in items:
                flat.append((ui, li, qb, j, kt, coff, N, j == 0,
                             j == len(items) - 1))
        cum = [0]
        for (_, _, items) in units:
            cum.append(cum[-1] + len(items))

        @blk.tensor
        def _(tn):
            def emit_scores(fi):
                ui, li, qb, j, kt, coff, N, first, last = flat[fi]
                if fi >= 2:
                    tn.wait_ge(self.sems["sE"], b_sE + 2 * (fi - 1))
                tn.matmul(ps["pp"][fi % 2][0:128, coff:coff + N],
                          t["kT"][0:64, li * T + kt * 128: li * T + (kt + 1) * 128],
                          t["qT"][0:64, li * T + qb * QBW + coff: li * T + qb * QBW + coff + N],
                          start=True, stop=True, tile_position=(0, 0))
                mmB = tn.matmul(ps["pswp"][fi % 2][0:128, coff:coff + N],
                                t["kT"][64:128, li * T + kt * 128: li * T + (kt + 1) * 128],
                                t["qT"][64:128, li * T + qb * QBW + coff: li * T + qb * QBW + coff + N],
                                start=True, stop=True, tile_position=(64, 0))
                self.inc(mmB, "tS", 1)

            emit_scores(0)
            for fi, (ui, li, qb, j, kt, coff, N, first, last) in enumerate(flat):
                hp = g * PPG + li
                if fi + 1 < len(flat):
                    emit_scores(fi + 1)
                tn.wait_ge(self.sems["sE"], b_sE + 2 * (fi + 1))
                if first and ui >= 1:
                    tn.wait_ge(self.sems["vPC"], b_PC + 2 * ui)
                vbase = kt * HEADS * VW
                tn.matmul(ps["pvA"][0:VW, coff:coff + N],
                          t["vext"][:, vbase + 2 * hp * VW: vbase + 2 * hp * VW + VW],
                          t["expA"][fi % 2][0:128, 0:N], start=first, stop=last)
                pv = tn.matmul(ps["pvB"][0:VW, coff:coff + N],
                               t["vext"][:, vbase + (2 * hp + 1) * VW: vbase + (2 * hp + 1) * VW + VW],
                               t["expB"][fi % 2][0:128, 0:N], start=first, stop=last)
                self.inc(pv, "tPV", 1)
                if last:
                    tn.wait_ge(self.sems["sRC"], b_RC + 2 * ui + 1)
                    bA = tn.matmul(ps["pnorm"][0:64, 0:QBW], t["e64"][:, 0:64],
                                   t["recA"][:, 0:QBW], start=True, stop=True)
                    self.inc(bA, "tnm", 1)
                    tn.wait_ge(self.sems["sRC"], b_RC + 2 * ui + 2)
                    tn.wait_ge(self.sems["sBC"], b_BC + 2 * ui + 1)
                    bB = tn.matmul(ps["pnorm"][0:64, 0:QBW], t["e64"][:, 0:64],
                                   t["recB"][:, 0:QBW], start=True, stop=True)
                    self.inc(bB, "tnm", 1)

        @blk.scalar
        def _(sc):
            for fi, (ui, li, qb, j, kt, coff, N, first, last) in enumerate(flat):
                hp = g * PPG + li
                sc.wait_ge(self.sems["tS"], b_tS + fi + 1)
                if fi >= 2:
                    sc.wait_ge(self.sems["tPV"], b_PV + fi - 1)
                eA = sc.activation(t["expA"][fi % 2][0:128, 0:N],
                                   ps["pp"][fi % 2][0:128, coff:coff + N], AF.Exp,
                                   bias=t["negB"][:, hp:hp + 1],
                                   scale=t["cpool"][:, 2:3])
                self.inc(eA, "sE", 1)
                eB = sc.activation(t["expB"][fi % 2][0:128, 0:N],
                                   ps["pswp"][fi % 2][0:128, coff:coff + N], AF.Exp,
                                   bias=t["negB"][:, hp:hp + 1],
                                   scale=t["cpool"][:, 2:3])
                self.inc(eB, "sE", 1)
                if last:
                    sc.wait_ge(self.sems["vPC"], b_PC + 2 * ui + 1)
                    sc.activation(t["lnz"][0:1, 0:QBW],
                                  t["pvcA"][64:65, 0:QBW], AF.Ln)
                    rA = sc.activation(t["recA"][64:65, 0:QBW],
                                       t["lnz"][0:1, 0:QBW], AF.Exp,
                                       scale=t["cpool"][64:65, 3:4])
                    self.inc(rA, "sRC", 1)
                    sc.wait_ge(self.sems["vPC"], b_PC + 2 * ui + 2)
                    sc.activation(t["lnz"][0:1, 0:QBW],
                                  t["pvcB"][64:65, 0:QBW], AF.Ln)
                    rB = sc.activation(t["recB"][64:65, 0:QBW],
                                       t["lnz"][0:1, 0:QBW], AF.Exp,
                                       scale=t["cpool"][64:65, 3:4])
                    self.inc(rB, "sRC", 1)
                    sc.wait_ge(self.sems["tnm"], b_nm + 2 * ui + 1)
                    cA = sc.activation(t["bcst"][0:64, 0:QBW],
                                       ps["pnorm"][0:64, 0:QBW], AF.Copy)
                    self.inc(cA, "sBC", 1)
                    sc.wait_ge(self.sems["tnm"], b_nm + 2 * ui + 2)
                    cB = sc.activation(t["bcstB"][0:64, 0:QBW],
                                       ps["pnorm"][0:64, 0:QBW], AF.Copy)
                    self.inc(cB, "sBC", 1)

        @blk.gpsimd
        def _(gp):
            for ui in range(len(units)):
                for r in range(TPS):
                    w = (g * len(units) + ui) * TPS + r
                    if w >= 1:
                        gp.wait_ge(self.sems["vW2"], 2 * w)
                    self.inc(gp.dma_start(t["w2st"][:, :],
                                          d["w2t"][w * 128:(w + 1) * 128, :]),
                             "dW", 16)

        @blk.vector
        def _(vec):
            def w2_chain(w):
                vec.wait_ge(self.sems["dW"], 16 * (w + 1))
                vec.tensor_scalar_mul(t["wA"][:, :], t["w2st"][:, :], -1.0)
                vec.tensor_max(t["wC"][:, :], t["wA"][:, :], t["w2st"][:, :])
                vec.tensor_single_scalar(t["wA"][:, :], t["wC"][:, :], 0.25,
                                         OP.is_ge)
                vec.tensor_scalar(t["wD"][:, :], t["wA"][:, :], 15.0 * 2.0 ** 43,
                                  2.0 ** 43, OP.mult, OP.add)
                for B in (2, 1):
                    vec.tensor_scalar_mul(t["wT"][:, :], t["wD"][:, :],
                                          2.0 ** (B - 49))
                    vec.tensor_tensor(t["wA"][:, :], t["wT"][:, :], t["wC"][:, :],
                                      OP.is_ge)
                    vec.tensor_scalar(t["wA"][:, :], t["wA"][:, :],
                                      -(2.0 ** B - 1.0), 2.0 ** B, OP.mult, OP.add)
                    vec.tensor_mul(t["wD"][:, :], t["wD"][:, :], t["wA"][:, :])
                vec.tensor_scalar_mul(t["wT"][:, :], t["wD"][:, :], 1.5)
                vec.tensor_add(t["wC"][:, :], t["w2st"][:, :], t["wT"][:, :])
                vec.tensor_sub(t["wC"][:, :], t["wC"][:, :], t["wT"][:, :])
                vec.tensor_scalar_mul(t["wT"][:, :], t["wD"][:, :],
                                      1.5 * 2.0 ** -29)
                a = vec.tensor_add(t["w2st"][:, :], t["w2st"][:, :], t["wT"][:, :])
                self.inc(a, "vW2", 1)
                vec.tensor_sub(t["wA"][:, :], t["w2st"][:, :], t["wT"][:, :])
                vec.tensor_sub(t["w2st"][:, :], t["wC"][:, :], t["wA"][:, :])
                vec.tensor_mul(t["w2st"][:, :], t["w2st"][:, :], t["mgad"][:, 0:E])
                q2 = vec.tensor_add(t["w2q"][:, w * E:(w + 1) * E],
                                    t["wA"][:, :], t["w2st"][:, :])
                self.inc(q2, "vW2", 1)

            for ui, (li, qb, items) in enumerate(units):
                gsl = sl0 + ui
                vec.wait_ge(self.sems["tPV"], b_PV + cum[ui + 1])
                if ui >= 1:
                    vec.wait_ge(self.sems["sRC"], b_RC + 2 * ui)
                pA = vec.tensor_copy(t["pvcA"][0:65, 0:QBW],
                                     ps["pvA"][0:65, 0:QBW])
                self.inc(pA, "vPC", 1)
                pB = vec.tensor_copy(t["pvcB"][0:65, 0:QBW],
                                     ps["pvB"][0:65, 0:QBW])
                self.inc(pB, "vPC", 1)
                if gsl >= 2:
                    vec.wait_ge(self.sems["dSp"], 32 * (gsl - 1))
                vec.wait_ge(self.sems["sBC"], b_BC + 2 * ui + 1)
                mA = vec.tensor_mul(t["slabA"][ui % 2][0:64, 0:QBW],
                                    t["pvcA"][0:64, 0:QBW], t["bcst"][0:64, 0:QBW])
                self.inc(mA, "vNR", 1)
                vec.wait_ge(self.sems["sBC"], b_BC + 2 * ui + 2)
                mB = vec.tensor_mul(t["slabB"][ui % 2][0:64, 0:QBW],
                                    t["pvcB"][0:64, 0:QBW], t["bcstB"][0:64, 0:QBW])
                self.inc(mB, "vNR", 1)
                for r in range(TPS):
                    w2_chain((g * len(units) + ui) * TPS + r)

        @blk.sync
        def _(sync):
            for ui, (li, qb, items) in enumerate(units):
                hp = g * PPG + li
                sync.wait_ge(self.sems["vNR"], b_NR + 2 * ui + 1)
                self.inc(sync.dma_start(
                    d["aspill"][hp * 128: hp * 128 + 64, qb * QBW:(qb + 1) * QBW],
                    t["slabA"][ui % 2][0:64, 0:QBW]), "dSp", 16)
                sync.wait_ge(self.sems["vNR"], b_NR + 2 * ui + 2)
                self.inc(sync.dma_start(
                    d["aspill"][hp * 128 + 64: (hp + 1) * 128, qb * QBW:(qb + 1) * QBW],
                    t["slabB"][ui % 2][0:64, 0:QBW]), "dSp", 16)


    def phase_ffn1(self, blk):
        cfg, t, d, ps = self.cfg, self.t, self.d, self.ps
        T, E, JT, KE = cfg.T, cfg.E, cfg.JT, cfg.KE
        NB, NBW = cfg.NB, cfg.NBW
        b_dG = self.cnt["dG"]
        b_F1, b_sH, b_vH = self.cnt["tF1"], self.cnt["sH"], self.cnt["vH"]

        @blk.gpsimd
        def _(gp):
            for jt in range(JT):
                if jt >= 2:
                    gp.wait_ge(self.sems["tF1"], b_F1 + jt - 1)
                self.inc(gp.dma_start(
                    t["w1sb"][jt % 2][:, :],
                    bass.AP(d["w1b"], jt * E * 128,
                            [[128, 128], [128 * 128, KE], [1, 128]])), "dG", 16)
        @blk.tensor
        def _(tn):
            for jt in range(JT):
                tn.wait_ge(self.sems["dG"], b_dG + 16 * (jt + 1))
                if jt >= 2:
                    tn.wait_ge(self.sems["sH"], b_sH + (jt - 2) + 1)
                for nb in range(NB):
                    for kc in range(KE):
                        mm = tn.matmul(
                            ps["ph4"][jt % 2][:, nb * NBW:(nb + 1) * NBW],
                            t["w1sb"][jt % 2][:, kc * 128:(kc + 1) * 128],
                            t["xnT"][:, kc * T + nb * NBW: kc * T + (nb + 1) * NBW],
                            start=(kc == 0), stop=(kc == KE - 1))
                self.inc(mm, "tF1", 1)

        @blk.scalar
        def _(sc):
            for jt in range(JT):
                sc.wait_ge(self.sems["tF1"], b_F1 + jt + 1)
                if jt >= 3:
                    sc.wait_ge(self.sems["vH"], b_vH + (jt - 3) + 1)
                h = sc.activation(t["qth"][jt % 3][:, 0:T], ps["ph4"][jt % 2][:, 0:T],
                                  AF.Relu, bias=t["b1sb"][:, jt:jt + 1])
                self.inc(h, "sH", 1)

        @blk.vector
        def _(vec):
            for jt in range(JT):
                vec.wait_ge(self.sems["sH"], b_sH + jt + 1)
                h = t["qth"][jt % 3][:, 0:T]
                gD, gA, gB, gC = (t[k][:, 0:T] for k in ("gD", "gA", "gB", "gC"))
                gT = t["gT"][:, 0:T]
                vec.tensor_single_scalar(gA, h, 4.0, OP.is_ge)
                vec.tensor_scalar(gD, gA, 255.0 * 2.0 ** 43, 2.0 ** 43,
                                  OP.mult, OP.add)
                for B in (4, 2, 1):
                    vec.tensor_scalar_mul(gT, gD, 2.0 ** (B - 49))
                    vec.tensor_tensor(gA, gT, h, OP.is_ge)
                    vec.tensor_scalar(gA, gA, -(2.0 ** B - 1.0), 2.0 ** B,
                                      OP.mult, OP.add)
                    vec.tensor_mul(gD, gD, gA)
                vec.tensor_scalar_mul(gT, gD, 1.5 * 2.0 ** -45)
                vec.tensor_add(gB, h, gT)
                vec.tensor_sub(gB, gB, gT)
                vec.tensor_scalar_mul(gT, gD, 1.5)
                z = vec.tensor_add(gC, h, gT)
                self.inc(z, "vH", 1)
                vec.tensor_sub(gC, gC, gT)
                vec.tensor_sub(gA, gC, gB)
                vec.tensor_mul(gA, gA, t["mgad"][:, 0:T])
                vec.tensor_add(t["hq"][:, jt * T:(jt + 1) * T], gB, gA)
    def phase_out(self, blk):
        cfg, t, d, ps = self.cfg, self.t, self.d, self.ps
        T, E, JT, KE, TT = cfg.T, cfg.E, cfg.JT, cfg.KE, cfg.TT
        NE, NEW = cfg.NE, cfg.NEW
        b_dS, b_dG = self.cnt["dS"], self.cnt["dG"]
        b_F2, b_WO = self.cnt["tF2"], self.cnt["tWO"]
        b_EP = self.cnt["vEP"]
        b_dX2, b_dO = self.cnt["dX2"], self.cnt["dO"]
        b_dSp = self.cnt["dSp"]

        @blk.gpsimd
        def _(gp):
            for kc in range(KE):
                self.inc(gp.dma_start(t["wosb"][:, kc * E:(kc + 1) * E],
                                      d["wo"][kc * 128:(kc + 1) * 128, :]), "dG", 16)

        @blk.sync
        def _(sync):
            self.inc(sync.dma_start(t["b2rep"][:, :],
                                    bass.AP(d["b2r"], 0, [[0, 128], [1, E]])), "dS", 16)
            sync.wait_ge(self.sems["dSp"], b_dSp)

            def out_dma(k):
                sync.wait_ge(self.sems["vEP"], b_EP + 1 + 2 * k + 2)
                self.inc(sync.dma_start(d["out"][k * 128:(k + 1) * 128, :],
                                        t["obuf"][k % 2][:, :]), "dO", 16)

            for tt in range(TT):
                if tt >= 2:
                    sync.wait_ge(self.sems["tWO"], b_WO + tt - 1)
                self.inc(sync.dma_start(
                    t["atb"][tt % 2][:, :],
                    bass.AP(d["aspill"], tt * 128,
                            [[T, 128], [128 * T, KE], [1, 128]])), "dS", 16)
                if tt >= 2:
                    sync.wait_ge(self.sems["vEP"], b_EP + 1 + 2 * (tt - 2) + 2)
                self.inc(sync.dma_start(t["xbuf"][tt % 2][:, :],
                                        d["x"][tt * 128:(tt + 1) * 128, :]), "dX2", 16)
                if tt >= 2:
                    out_dma(tt - 2)
            out_dma(TT - 2)
            out_dma(TT - 1)
            sync.wait_ge(self.sems["dO"], b_dO + 16 * TT)

        @blk.tensor
        def _(tn):
            tn.wait_ge(self.sems["dG"], b_dG + 16 * KE)
            for tt in range(TT):
                if tt >= 2:
                    tn.wait_ge(self.sems["vEP"], b_EP + 1 + 2 * (tt - 2) + 1)
                for jc in range(JT):
                    for ne in range(NE):
                        mm = tn.matmul(
                            ps["psf"][tt % 2][:, ne * NEW:(ne + 1) * NEW],
                            t["hq"][:, jc * T + tt * 128: jc * T + (tt + 1) * 128],
                            t["w2q"][:, jc * E + ne * NEW: jc * E + (ne + 1) * NEW],
                            start=(jc == 0), stop=(jc == JT - 1))
                self.inc(mm, "tF2", 1)
                tn.wait_ge(self.sems["dS"], b_dS + 16 * (tt + 2))
                if tt >= 2:
                    tn.wait_ge(self.sems["vEP"], b_EP + 1 + 2 * (tt - 2) + 2)
                for kc in range(KE):
                    for ne in range(NE):
                        mm = tn.matmul(
                            ps["pso"][tt % 2][:, ne * NEW:(ne + 1) * NEW],
                            t["atb"][tt % 2][:, kc * 128:(kc + 1) * 128],
                            t["wosb"][:, kc * E + ne * NEW: kc * E + (ne + 1) * NEW],
                            start=(kc == 0), stop=(kc == KE - 1))
                self.inc(mm, "tWO", 1)

        @blk.vector
        def _(vec):
            def gq5(xn_f, out_ap):
                vec.tensor_scalar_mul(t["oA"][:, :], xn_f, -1.0)
                vec.tensor_max(t["oav"][:, :], t["oA"][:, :], xn_f)
                vec.tensor_single_scalar(t["oA"][:, :], t["oav"][:, :], 2.0 ** -6,
                                         OP.is_ge)
                vec.tensor_scalar(t["oD"][:, :], t["oA"][:, :], 255.0 * 2.0 ** 28,
                                  2.0 ** 28, OP.mult, OP.add)
                for B in (8, 4, 2, 1):
                    vec.tensor_scalar_mul(t["oT"][:, :], t["oD"][:, :],
                                          2.0 ** (B - 42))
                    vec.tensor_tensor(t["oA"][:, :], t["oT"][:, :], t["oav"][:, :],
                                      OP.is_ge)
                    vec.tensor_scalar(t["oA"][:, :], t["oA"][:, :],
                                      -(2.0 ** B - 1.0), 2.0 ** B, OP.mult, OP.add)
                    vec.tensor_mul(t["oD"][:, :], t["oD"][:, :], t["oA"][:, :])
                vec.tensor_scalar_mul(t["oT"][:, :], t["oD"][:, :],
                                      1.5 * 2.0 ** -29)
                vec.tensor_add(t["oB"][:, :], xn_f, t["oT"][:, :])
                vec.tensor_sub(t["oB"][:, :], t["oB"][:, :], t["oT"][:, :])
                vec.tensor_scalar_mul(t["oT"][:, :], t["oD"][:, :], 1.5)
                vec.tensor_add(t["oC"][:, :], xn_f, t["oT"][:, :])
                vec.tensor_sub(t["oC"][:, :], t["oC"][:, :], t["oT"][:, :])
                vec.tensor_sub(t["oS"][:, :], t["oC"][:, :], t["oB"][:, :])
                vec.tensor_mul(t["oS"][:, :], t["oS"][:, :], t["mgad"][:, 0:E])
                return vec.tensor_add(out_ap, t["oB"][:, :], t["oS"][:, :])

            vec.wait_ge(self.sems["dS"], b_dS + 16)
            qb2 = gq5(t["b2rep"][:, :], t["b2q"][:, :])
            self.inc(qb2, "vEP", 1)
            for tt in range(TT):
                vec.wait_ge(self.sems["tF2"], b_F2 + tt + 1)
                vec.tensor_add(t["o2b"][:, :], ps["psf"][tt % 2][:, 0:E],
                               t["b2q"][:, :])
                q5 = gq5(t["o2b"][:, :], t["o2b"][:, :])
                self.inc(q5, "vEP", 1)
                vec.wait_ge(self.sems["tWO"], b_WO + tt + 1)
                vec.wait_ge(self.sems["dX2"], b_dX2 + 16 * (tt + 1))
                if tt >= 2:
                    vec.wait_ge(self.sems["dO"], b_dO + 16 * (tt - 1))
                vec.tensor_add(t["oS"][:, :], t["o2b"][:, :],
                               ps["pso"][tt % 2][:, 0:E])
                fin = vec.tensor_add(t["obuf"][tt % 2][:, :], t["oS"][:, :],
                                     t["xbuf"][tt % 2][:, :])
                self.inc(fin, "vEP", 1)


_BUILD_CACHE = {}


def get_nc(cfg: Cfg):
    key = (cfg.T, cfg.E, cfg.HEADS, cfg.LAT, cfg.HF, cfg.PAGE)
    if key not in _BUILD_CACHE:
        _BUILD_CACHE[key] = Builder(cfg).build()
    return _BUILD_CACHE[key]


def make_host_constants(cfg: Cfg):
    T = cfg.T
    bf = ml_dtypes.bfloat16
    inv = 1.0 / 10000.0 ** (np.arange(0, 64, 2, dtype=np.float32) / 64.0)
    ang = np.outer(np.arange(T, dtype=np.float32), inv)
    cos, sin = np.cos(ang), np.sin(ang)
    CT = np.zeros((128, T), np.float32)
    ST = np.zeros((128, T), np.float32)
    for p in range(128):
        j = p % 64
        jj = j % 32
        CT[p] = cos[:, jj]
        ST[p] = (-sin[:, jj]) if j < 32 else sin[:, jj]
    P = np.zeros((128, 128), np.float32)
    for i in range(128):
        jj = i + 32 if (i % 64) < 32 else i - 32
        P[jj, i] = 1.0
    E64 = np.zeros((128, 64), np.float32)
    E64[64, :] = 1.0
    return {
        "tc": np.ascontiguousarray(CT),
        "ts": np.ascontiguousarray(ST),
        "Pswap": np.ascontiguousarray(P.astype(bf)),
        "identin": np.ascontiguousarray(np.eye(128, dtype=np.float32).astype(bf)),
        "e64in": np.ascontiguousarray(E64.astype(bf)),
        "onesin": np.ascontiguousarray(np.ones((128, 128), np.float32).astype(bf)),
        "onescol": np.ascontiguousarray(np.ones((128, 1), np.float32).astype(bf)),
    }


def head_perm(E):
    perm = []
    for h in range(E // 64):
        base = h * 64
        perm += [base + 2 * i for i in range(32)] + [base + 2 * i + 1 for i in range(32)]
    return np.array(perm)


def prepare_inputs(cfg: Cfg, x, wq, w_down, wk_up, wv_up, wo, w1, b1, w2, b2):
    perm = head_perm(cfg.E)
    c = lambda a: np.ascontiguousarray(np.asarray(a), dtype=np.float32)
    wq, w_down, wk_up, wv_up, wo = (np.asarray(a, dtype=np.float32)
                                    for a in (wq, w_down, wk_up, wv_up, wo))
    w1, b1, w2, b2 = (np.asarray(a, dtype=np.float32) for a in (w1, b1, w2, b2))
    base = {
        "wq": c(wq[:, perm]),
        "wdn": c(w_down),
        "wku": c(wk_up[:, perm]),
        "wvu": c(wv_up),
        "wo": c(wo),
        "w1b": c(w1.T.reshape(cfg.E, cfg.JT, 128).transpose(1, 0, 2)),
        "w2t": c(w2.T),
        "b1r": c(b1.reshape(cfg.JT, 128).T),
        "b2r": c(b2.reshape(1, cfg.E)),
    }
    base.update(make_host_constants(cfg))
    in_maps = []
    for i in range(x.shape[0]):
        m = dict(base)
        m["x"] = c(x[i])
        in_maps.append(m)
    return in_maps


def kernel(L0_attn_x, L0_attn_wq, L0_attn_w_down, L0_attn_wk_up, L0_attn_wv_up,
           L0_attn_wo, L0_ffn_w1, L0_ffn_b1, L0_ffn_w2, L0_ffn_b2):
    cfg = Cfg()
    x = np.asarray(L0_attn_x, dtype=np.float32)
    in_maps = prepare_inputs(cfg, x, L0_attn_wq, L0_attn_w_down, L0_attn_wk_up,
                             L0_attn_wv_up, L0_attn_wo, L0_ffn_w1, L0_ffn_b1,
                             L0_ffn_w2, L0_ffn_b2)
    nc = get_nc(cfg)
    res = run_bass_kernel_spmd(nc, in_maps, core_ids=list(range(cfg.n_cores)))
    return np.stack([np.asarray(r["out"], dtype=np.float32) for r in res.results])


# version 18
# speedup vs baseline: 1.2143x; 1.0605x over previous
import contextlib
import math

import numpy as np
import ml_dtypes

import concourse.bass as bass
import concourse.mybir as mybir
from concourse.bass_utils import run_bass_kernel_spmd

F32 = mybir.dt.float32
BF16 = mybir.dt.bfloat16
F16 = mybir.dt.float16
F8E4 = mybir.dt.float8e4
I16 = mybir.dt.int16
I32 = mybir.dt.int32
AF = mybir.ActivationFunctionType
OP = mybir.AluOpType
AX = mybir.AxisListType
LN2 = math.log(2.0)
INVLN2 = 1.0 / LN2
LN2_HI = float(np.float32(LN2))
LN2_LO = float(np.float64(LN2) - np.float64(np.float32(LN2)))


class Cfg:
    def __init__(self, T=2048, E=1024, HEADS=16, LAT=512, HF=4096, PAGE=256,
                 n_cores=8, ngrp=None):
        self.T, self.E, self.HEADS, self.LAT, self.HF = T, E, HEADS, LAT, HF
        self.PAGE = PAGE
        self.n_cores = n_cores
        self.D = 64
        assert E == HEADS * 64
        self.PAIRS = HEADS // 2
        self.TT = T // 128
        self.KE = E // 128
        self.KL = LAT // 128
        self.JT = HF // 128
        self.QBW = min(512, T)
        self.NQB = T // self.QBW
        self.KTILES = T // 128
        self.NBW = min(512, T)
        self.NB = T // self.NBW
        self.NEW = min(512, E)
        self.NE = E // self.NEW
        self.NGRP = ngrp if ngrp is not None else min(4, self.PAIRS)
        assert self.PAIRS % self.NGRP == 0
        self.PPG = self.PAIRS // self.NGRP
        self.VW = 65

    def score_items(self, qb):
        out = []
        j = 0
        for kt in range(self.KTILES):
            qlo = (kt * 128 // self.PAGE) * self.PAGE
            qs = max(qb * self.QBW, qlo)
            qe = (qb + 1) * self.QBW
            if qs >= qe:
                break
            out.append((j, kt, qs - qb * self.QBW, qe - qs))
            j += 1
        return out


class Builder:
    def __init__(self, cfg: Cfg):
        self.cfg = cfg
        self.nc = bass.Bass("TRN2", target_bir_lowering=False,
                            detect_race_conditions=False,
                            dynamic_dma_scratch_size=16384)
        self.es = contextlib.ExitStack()
        self.cnt = {}

    def sem(self, name):
        h = self.es.enter_context(self.nc.semaphore(name))
        self.cnt[name] = 0
        return h

    def inc(self, instr, name, n=1):
        instr.then_inc(self.sems[name], n)
        self.cnt[name] += n
        return self.cnt[name]

    def vfence(self, vec, instr):
        c = self.inc(instr, "vrd", 1)
        vec.wait_ge(self.sems["vrd"], c)

    def build(self):
        cfg = self.cfg
        nc = self.nc
        T, E, LAT, HF = cfg.T, cfg.E, cfg.LAT, cfg.HF
        TT, KE, KL, JT = cfg.TT, cfg.KE, cfg.KL, cfg.JT
        QBW, VW = cfg.QBW, cfg.VW
        S = self.es.enter_context
        d = self.d = {}
        t = self.t = {}

        d["x"] = nc.dram_tensor("x", [T, E], F32, kind="ExternalInput")
        d["wq"] = nc.dram_tensor("wq", [E, E], F32, kind="ExternalInput")
        d["wdn"] = nc.dram_tensor("wdn", [E, LAT], F32, kind="ExternalInput")
        d["wku"] = nc.dram_tensor("wku", [LAT, E], F32, kind="ExternalInput")
        d["wvu"] = nc.dram_tensor("wvu", [LAT, E], F32, kind="ExternalInput")
        d["wo"] = nc.dram_tensor("wo", [E, E], F32, kind="ExternalInput")
        d["w1b"] = nc.dram_tensor("w1b", [JT, E, 128], F32, kind="ExternalInput")
        d["w2t"] = nc.dram_tensor("w2t", [HF, E], F32, kind="ExternalInput")
        d["b1r"] = nc.dram_tensor("b1r", [128, JT], F32, kind="ExternalInput")
        d["b2r"] = nc.dram_tensor("b2r", [1, E], F32, kind="ExternalInput")
        d["tc"] = nc.dram_tensor("tc", [128, T], F32, kind="ExternalInput")
        d["ts"] = nc.dram_tensor("ts", [128, T], F32, kind="ExternalInput")
        d["Pswap"] = nc.dram_tensor("Pswap", [128, 128], BF16, kind="ExternalInput")
        d["identin"] = nc.dram_tensor("identin", [128, 128], BF16, kind="ExternalInput")
        d["e64in"] = nc.dram_tensor("e64in", [128, 64], BF16, kind="ExternalInput")
        d["onesin"] = nc.dram_tensor("onesin", [128, 128], BF16, kind="ExternalInput")
        d["onescol"] = nc.dram_tensor("onescol", [128, 1], BF16, kind="ExternalInput")
        d["out"] = nc.dram_tensor("out", [T, E], F32, kind="ExternalOutput")
        spill_kind = "ExternalOutput" if getattr(self, "debug", False) else "Internal"
        d["aspill"] = nc.dram_tensor("aspill", [E, T], BF16, kind=spill_kind)
        if getattr(self, "debug", False):
            d["dbg_qT"] = nc.dram_tensor("dbg_qT", [128, cfg.PPG * T], F32, kind="ExternalOutput")
            d["dbg_kT"] = nc.dram_tensor("dbg_kT", [128, cfg.PPG * T], F32, kind="ExternalOutput")
            d["dbg_ones"] = nc.dram_tensor("dbg_ones", [128, 128], F32, kind="ExternalOutput")
            d["dbg_negB"] = nc.dram_tensor("dbg_negB", [128, cfg.PAIRS], F32, kind="ExternalOutput")
            d["dbg_b2q"] = nc.dram_tensor("dbg_b2q", [128, E], F32, kind="ExternalOutput")
            d["dbg_hq"] = nc.dram_tensor("dbg_hq", [128, T], F32, kind="ExternalOutput")
            d["dbg_w2q"] = nc.dram_tensor("dbg_w2q", [128, E], F32, kind="ExternalOutput")
            d["dbg_xnT"] = nc.dram_tensor("dbg_xnT", [128, T], F32, kind="ExternalOutput")


        L = dict(side="left")
        t["cpool"] = S(nc.sbuf_tensor("cpool", [128, 12], F32, **L))
        t["ones_c"] = S(nc.sbuf_tensor("ones_c", [128, 1], BF16, **L))
        t["onesf"] = S(nc.sbuf_tensor("onesf", [128, 128], BF16, **L))
        t["e64"] = S(nc.sbuf_tensor("e64s", [128, 64], BF16, **L))
        t["Psw"] = S(nc.sbuf_tensor("Psws", [128, 128], BF16, **L))
        t["ident"] = S(nc.sbuf_tensor("idents", [128, 128], BF16, **L))
        t["negB"] = S(nc.sbuf_tensor("negB", [128, cfg.PAIRS], F32, **L))
        t["b1sb"] = S(nc.sbuf_tensor("b1sb", [128, JT], F32, **L))
        t["stats"] = S(nc.sbuf_tensor("stats", [128, 8], F32, **L))
        t["s11"] = S(nc.sbuf_tensor("s11", [1, 8], F32, **L))
        t["xbuf"] = [S(nc.sbuf_tensor(f"xbuf{i}", [128, E], F32, **L)) for i in range(2)]
        t["w2q"] = S(nc.sbuf_tensor("w2q", [128, JT * E], F8E4, **L))

        R = dict(side="right")
        t["xnT"] = S(nc.sbuf_tensor("xnT", [128, KE * T], BF16, **R))

        self.sems = {n: self.sem(n) for n in (
            "dS", "dG", "dO", "dSp", "dX2",
            "vxn", "vvar", "vev", "ttr", "sln",
            "tpj", "vpe", "tsw", "vrp", "tnm", "vnm",
            "tS", "sE", "tPV", "vNR", "sRC", "sBC",
            "vW2", "dW",
            "tF1", "sH", "vH",
            "tF2", "tWO", "vEP", "sEP", "vrd",
        )}

        t["mgad"] = S(nc.sbuf_tensor("mgad", [128, T], BF16, **R))
        t["w2st"] = S(nc.sbuf_tensor("w2st", [128, E], F32, **R))
        t["wD"] = S(nc.sbuf_tensor("wD", [128, E], BF16, **R))
        t["wA"] = S(nc.sbuf_tensor("wA", [128, E], BF16, **R))
        t["wC"] = S(nc.sbuf_tensor("wC", [128, E], BF16, **R))
        t["wT"] = S(nc.sbuf_tensor("wT", [128, E], BF16, **R))

        sc1 = contextlib.ExitStack()
        t["xcb"] = sc1.enter_context(nc.sbuf_tensor("xcb", [128, E], F32, **R))
        t["xc2b"] = sc1.enter_context(nc.sbuf_tensor("xc2b", [128, E], F32, **R))
        t["xnb"] = [sc1.enter_context(nc.sbuf_tensor(f"xnb{i}", [128, E], BF16, **R))
                    for i in range(2)]
        ps = self.ps = {}
        ps["ptr"] = [sc1.enter_context(nc.psum_tensor(f"ptr{i}", [128, 128], BF16))
                     for i in range(2)]
        with nc.Block() as blk:
            self.phase1(blk)
        sc1.close()

        sc2 = contextlib.ExitStack()
        E2 = sc2.enter_context
        t["tc"] = E2(nc.sbuf_tensor("tcs", [128, T], F32, **R))
        t["ts"] = E2(nc.sbuf_tensor("tss", [128, T], F32, **R))
        t["latT"] = E2(nc.sbuf_tensor("latT", [128, KL * T], BF16, **R))
        t["vext"] = E2(nc.sbuf_tensor("vext", [128, TT * cfg.HEADS * VW], BF16, **R))
        t["qT"] = E2(nc.sbuf_tensor("qT", [128, cfg.PPG * T], BF16, **R))
        t["kT"] = E2(nc.sbuf_tensor("kT", [128, cfg.PPG * T], BF16, **R))
        t["wbuf"] = E2(nc.sbuf_tensor("wbuf", [128, max(KE * LAT, KL * E, (KE + KL) * cfg.PPG * 128)], BF16, **R))
        t["qraw"] = [E2(nc.sbuf_tensor(f"qraw{i}", [128, T], BF16, **R)) for i in range(2)]
        t["ttmp"] = E2(nc.sbuf_tensor("ttmp", [128, cfg.NBW], BF16, **R))
        t["ttmp2"] = E2(nc.sbuf_tensor("ttmp2", [128, cfg.NBW], BF16, **R))
        t["expA"] = [E2(nc.sbuf_tensor(f"expA{i}", [128, QBW], BF16, **R)) for i in range(2)]
        t["expB"] = [E2(nc.sbuf_tensor(f"expB{i}", [128, QBW], BF16, **R)) for i in range(2)]
        t["recA"] = E2(nc.sbuf_tensor("recA", [128, QBW], BF16, **R))
        t["lnz"] = E2(nc.sbuf_tensor("lnz", [1, QBW], F32, **R))
        t["recB"] = E2(nc.sbuf_tensor("recB", [128, QBW], BF16, **R))
        t["bcst"] = E2(nc.sbuf_tensor("bcst", [64, QBW], BF16, **R))
        t["bcstB"] = E2(nc.sbuf_tensor("bcstB", [64, QBW], BF16, **R))
        t["slabA"] = [E2(nc.sbuf_tensor(f"slabA{i}", [64, QBW], BF16, **R)) for i in range(2)]
        t["slabB"] = [E2(nc.sbuf_tensor(f"slabB{i}", [64, QBW], BF16, **R)) for i in range(2)]
        ps["pp"] = [E2(nc.psum_tensor(f"pp{i}", [128, 512], F32)) for i in range(2)]
        ps["pswp"] = [E2(nc.psum_tensor(f"pswp{i}", [128, 512], F32)) for i in range(2)]
        ps["pnorm"] = E2(nc.psum_tensor("pnorm", [128, 512], F32))
        ps["pvA"] = E2(nc.psum_tensor("pvA", [128, 512], F32))
        ps["pvB"] = E2(nc.psum_tensor("pvB", [128, 512], F32))
        with nc.Block() as blk:
            self.phase_proj_v(blk)
        for g in range(cfg.NGRP):
            with nc.Block() as blk:
                self.phase_proj_qk(blk, g)
            with nc.Block() as blk:
                self.phase_attn(blk, g)
        if getattr(self, "debug", False):
            with nc.Block() as blk:
                @blk.gpsimd
                def _(gp):
                    dbga = self.sem("dbga")
                    gp.dma_start(d["dbg_qT"][:, :], t["qT"][:, :]).then_inc(dbga, 16)
                    gp.dma_start(d["dbg_kT"][:, :], t["kT"][:, :]).then_inc(dbga, 16)
                    gp.dma_start(d["dbg_ones"][:, :], t["onesf"][:, :]).then_inc(dbga, 16)
                    gp.wait_ge(dbga, 48)
        sc2.close()

        t["hq"] = S(nc.sbuf_tensor("hq", [128, JT * T], F8E4, **R))
        sc4 = contextlib.ExitStack()
        E4 = sc4.enter_context
        t["w1sb"] = [E4(nc.sbuf_tensor(f"w1sb{i}", [128, KE * 128], BF16, **R))
                     for i in range(2)]
        t["qth"] = [E4(nc.sbuf_tensor(f"qth{i}", [128, T], BF16, **R)) for i in range(3)]
        t["gD"] = E4(nc.sbuf_tensor("gD", [128, T], BF16, **R))
        t["gA"] = E4(nc.sbuf_tensor("gA", [128, T], BF16, **R))
        t["gB"] = E4(nc.sbuf_tensor("gB", [128, T], BF16, **R))
        t["gC"] = E4(nc.sbuf_tensor("gC", [128, T], BF16, **R))
        t["gT"] = E4(nc.sbuf_tensor("gT", [128, T], BF16, **R))
        ps["ph4"] = [E4(nc.psum_tensor(f"ph4_{i}", [128, min(2048, T)], F32))
                     for i in range(2)]
        with nc.Block() as blk:
            self.phase_ffn1(blk)
        sc4.close()

        sc5 = contextlib.ExitStack()
        E5 = sc5.enter_context
        t["wosb"] = E5(nc.sbuf_tensor("wosb", [128, KE * E], BF16, **R))
        t["atb"] = [E5(nc.sbuf_tensor(f"atb{i}", [128, E], BF16, **R)) for i in range(2)]
        t["o2b"] = E5(nc.sbuf_tensor("o2b", [128, E], F32, **R))
        t["oav"] = E5(nc.sbuf_tensor("oav", [128, E], BF16, **R))
        t["oD"] = E5(nc.sbuf_tensor("oD", [128, E], BF16, **R))
        t["oA"] = E5(nc.sbuf_tensor("oA", [128, E], BF16, **R))
        t["oB"] = E5(nc.sbuf_tensor("oB", [128, E], F16, **R))
        t["oC"] = E5(nc.sbuf_tensor("oC", [128, E], BF16, **R))
        t["oT"] = E5(nc.sbuf_tensor("oT", [128, E], BF16, **R))
        t["oS"] = E5(nc.sbuf_tensor("oS", [128, E], F32, **R))
        t["b2q"] = E5(nc.sbuf_tensor("b2q", [128, E], F32, **R))
        t["obuf"] = [E5(nc.sbuf_tensor(f"obuf{i}", [128, E], F32, **R)) for i in range(2)]
        t["b2rep"] = t["obuf"][0]
        ps["psf"] = [E5(nc.psum_tensor(f"psf{i}", [128, E], F32)) for i in range(2)]
        ps["pso"] = [E5(nc.psum_tensor(f"pso{i}", [128, E], F32)) for i in range(2)]
        with nc.Block() as blk:
            self.phase_out(blk)
        sc5.close()

        if getattr(self, "debug", False):
            t["dbgf"] = S(nc.sbuf_tensor("dbgf", [128, T], F32, **R))
            with nc.Block() as blk:
                @blk.vector
                def _(vec):
                    vec.tensor_copy(t["dbgf"][:, 0:T], t["hq"][:, 0:T])
                @blk.gpsimd
                def _(gp):
                    dbg = self.sem("dbgd")
                    gp.wait_ge(self.sems["vEP"], self.cnt["vEP"])
                    gp.dma_start(d["dbg_negB"][:, :], t["negB"][:, :]).then_inc(dbg, 16)
                    gp.dma_start(d["dbg_b2q"][:, :], t["b2q"][:, :]).then_inc(dbg, 16)
                    gp.dma_start(d["dbg_w2q"][:, :], t["w2q"][:, 0:E]).then_inc(dbg, 16)
                    gp.dma_start(d["dbg_xnT"][:, :], t["xnT"][:, 0:T]).then_inc(dbg, 16)
                    gp.wait_ge(dbg, 64)
            with nc.Block() as blk:
                @blk.gpsimd
                def _(gp):
                    dbg2 = self.sem("dbgd2")
                    gp.dma_start(d["dbg_hq"][:, :], t["dbgf"][:, 0:T]).then_inc(dbg2, 16)
                    gp.wait_ge(dbg2, 16)

        self.es.close()
        return nc

    def phase1(self, blk):
        cfg, t, d, ps = self.cfg, self.t, self.d, self.ps
        T, E, TT, KE = cfg.T, cfg.E, cfg.TT, cfg.KE
        NCONST = 6

        @blk.sync
        def _(sync):
            self.inc(sync.dma_start(t["Psw"][:, :], d["Pswap"][:, :]), "dS", 16)
            self.inc(sync.dma_start(t["ident"][:, :], d["identin"][:, :]), "dS", 16)
            self.inc(sync.dma_start(t["e64"][:, :], d["e64in"][:, :]), "dS", 16)
            self.inc(sync.dma_start(t["ones_c"][:, 0:1], d["onescol"][:, :]), "dS", 16)
            self.inc(sync.dma_start(t["onesf"][:, :], d["onesin"][:, :]), "dS", 16)
            self.inc(sync.dma_start(t["b1sb"][:, :], d["b1r"][:, :]), "dS", 16)
            for tt in range(TT):
                if tt >= 2:
                    sync.wait_ge(self.sems["vxn"], tt - 1)
                self.inc(sync.dma_start(t["xbuf"][tt % 2][:, :],
                                        d["x"][tt * 128:(tt + 1) * 128, :]), "dS", 16)

        @blk.vector
        def _(vec):
            W = TT * cfg.HEADS * cfg.VW
            vec.memset(t["cpool"][:, 0:1], 1.0 / E)
            vec.memset(t["cpool"][:, 1:2], 1e-5)
            vec.memset(t["cpool"][:, 2:3], 0.125)
            vec.memset(t["cpool"][:, 3:4], -1.0)
            self.vfence(vec, vec.memset(t["mgad"][:, 0:T], 1024.0))
            self.vfence(vec, vec.tensor_scalar(t["mgad"][:, 0:T], t["mgad"][:, 0:T],
                                               1.0, 2.0 ** 52, OP.mult, OP.add))
            self.vfence(vec, vec.tensor_scalar(t["mgad"][:, 0:T], t["mgad"][:, 0:T],
                                               -(2.0 ** 52), 2.0 ** -10,
                                               OP.add, OP.mult))
            for tt in range(TT):
                vec.wait_ge(self.sems["dS"], 16 * (NCONST + tt + 1))
                xb = t["xbuf"][tt % 2]
                self.vfence(vec, vec.reduce_sum(t["stats"][:, 0:1], xb[:, :],
                                                axis=AX.X))
                self.vfence(vec, vec.tensor_scalar_mul(t["stats"][:, 1:2],
                                                       t["stats"][:, 0:1], 1.0 / E))
                vec.tensor_scalar(t["xcb"][:, :], xb[:, :], t["stats"][:, 1:2], None,
                                  OP.subtract)
                vec.tensor_mul(t["xc2b"][:, :], t["xcb"][:, :], t["xcb"][:, :])
                vv = vec.reduce_sum(t["stats"][:, 2:3], t["xc2b"][:, :], axis=AX.X)
                self.inc(vv, "vvar", 1)
                vec.wait_ge(self.sems["sln"], tt + 1)
                self.vfence(vec, vec.reciprocal(t["stats"][:, 4:5], t["stats"][:, 3:4]))
                xn = vec.tensor_scalar(t["xnb"][tt % 2][:, :], t["xcb"][:, :],
                                       t["stats"][:, 4:5], None, OP.mult)
                self.inc(xn, "vxn", 1)
                for cc in range(KE):
                    g = tt * KE + cc
                    vec.wait_ge(self.sems["ttr"], g + 1)
                    ev = vec.tensor_copy(
                        t["xnT"][:, cc * T + tt * 128: cc * T + (tt + 1) * 128],
                        ps["ptr"][g % 2][:, 0:128])
                    self.inc(ev, "vev", 1)

        @blk.scalar
        def _(sc):
            for tt in range(TT):
                sc.wait_ge(self.sems["vvar"], tt + 1)
                a = sc.activation(t["stats"][:, 3:4], t["stats"][:, 2:3], AF.Sqrt,
                                  bias=t["cpool"][:, 1:2], scale=t["cpool"][:, 0:1])
                self.inc(a, "sln", 1)

        @blk.tensor
        def _(tn):
            tn.wait_ge(self.sems["dS"], 16 * 2)
            for tt in range(TT):
                tn.wait_ge(self.sems["vxn"], tt + 1)
                for cc in range(KE):
                    g = tt * KE + cc
                    if g >= 2:
                        tn.wait_ge(self.sems["vev"], g - 1)
                    tr = tn.transpose(ps["ptr"][g % 2][:, 0:128],
                                      t["xnb"][tt % 2][:, cc * 128:(cc + 1) * 128],
                                      t["ident"][:, :])
                    self.inc(tr, "ttr", 1)

    def phase_proj_v(self, blk):
        cfg, t, d, ps = self.cfg, self.t, self.d, self.ps
        T, E, LAT = cfg.T, cfg.E, cfg.LAT
        TT, KE, KL = cfg.TT, cfg.KE, cfg.KL
        NB, NBW, NE, NEW = cfg.NB, cfg.NBW, cfg.NE, cfg.NEW
        VW, HEADS = cfg.VW, cfg.HEADS
        lat_items = [(lt, nb) for lt in range(KL) for nb in range(NB)]
        v_items = [(tt, fc) for tt in range(TT) for fc in range(NE)]
        b_pj, b_pe = self.cnt["tpj"], self.cnt["vpe"]
        b_dG = self.cnt["dG"]
        b_dS = self.cnt["dS"]

        @blk.sync
        def _(sync):
            self.inc(sync.dma_start(t["tc"][:, :], d["tc"][:, :]), "dS", 16)
            self.inc(sync.dma_start(t["ts"][:, :], d["ts"][:, :]), "dS", 16)
            self.tabs_dS = self.cnt["dS"]

        @blk.gpsimd
        def _(gp):
            for kc in range(KE):
                self.inc(gp.dma_start(t["wbuf"][:, kc * LAT:(kc + 1) * LAT],
                                      d["wdn"][kc * 128:(kc + 1) * 128, :]), "dG", 16)
            gp.wait_ge(self.sems["tpj"], b_pj + len(lat_items))
            for lc in range(KL):
                self.inc(gp.dma_start(t["wbuf"][:, lc * E:(lc + 1) * E],
                                      d["wvu"][lc * 128:(lc + 1) * 128, :]), "dG", 16)

        @blk.tensor
        def _(tn):
            tn.wait_ge(self.sems["dG"], b_dG + 16 * KE)
            for i, (lt, nb) in enumerate(lat_items):
                if i >= 2:
                    tn.wait_ge(self.sems["vpe"], b_pe + i - 1)
                for kc in range(KE):
                    mm = tn.matmul(ps["pp"][i % 2][:, 0:NBW],
                                   t["wbuf"][:, kc * LAT + lt * 128: kc * LAT + (lt + 1) * 128],
                                   t["xnT"][:, kc * T + nb * NBW: kc * T + (nb + 1) * NBW],
                                   start=(kc == 0), stop=(kc == KE - 1))
                self.inc(mm, "tpj", 1)
            tn.wait_ge(self.sems["dG"], b_dG + 16 * (KE + KL))
            for i2, (tt, fc) in enumerate(v_items):
                i = len(lat_items) + i2
                if i >= 2:
                    tn.wait_ge(self.sems["vpe"], b_pe + i - 1)
                for lc in range(KL):
                    mm = tn.matmul(ps["pp"][i % 2][:, 0:NEW],
                                   t["latT"][:, lc * T + tt * 128: lc * T + (tt + 1) * 128],
                                   t["wbuf"][:, lc * E + fc * NEW: lc * E + (fc + 1) * NEW],
                                   start=(lc == 0), stop=(lc == KL - 1))
                self.inc(mm, "tpj", 1)

        @blk.vector
        def _(vec):
            W = TT * HEADS * VW
            for tt in range(TT):
                vec.memset(bass.AP(t["vext"], tt * HEADS * VW + 64,
                                   [[W, 128], [VW, HEADS]]), 1.0)
            vec.memset(t["recA"][:, :], 0.0)
            vec.memset(t["recB"][:, :], 0.0)
            for i, (lt, nb) in enumerate(lat_items):
                vec.wait_ge(self.sems["tpj"], b_pj + i + 1)
                ev = vec.tensor_copy(
                    t["latT"][:, lt * T + nb * NBW: lt * T + (nb + 1) * NBW],
                    ps["pp"][i % 2][:, 0:NBW])
                self.inc(ev, "vpe", 1)
            for i2, (tt, fc) in enumerate(v_items):
                i = len(lat_items) + i2
                vec.wait_ge(self.sems["tpj"], b_pj + i + 1)
                nheads = NEW // 64
                dst = bass.AP(t["vext"], tt * HEADS * VW + fc * nheads * VW,
                              [[W, 128], [VW, nheads], [1, 64]])
                src = bass.AP(ps["pp"][i % 2], 0, [[512, 128], [64, nheads], [1, 64]])
                ev = vec.tensor_copy(dst, src)
                self.inc(ev, "vpe", 1)

    def phase_proj_qk(self, blk, g):
        cfg, t, d, ps = self.cfg, self.t, self.d, self.ps
        T, E, LAT = cfg.T, cfg.E, cfg.LAT
        KE, KL, PPG = cfg.KE, cfg.KL, cfg.PPG
        NB, NBW = cfg.NB, cfg.NBW
        WQC = PPG * 128
        b_dG = self.cnt["dG"]
        b_pj, b_pe = self.cnt["tpj"], self.cnt["vpe"]
        b_sw, b_rp = self.cnt["tsw"], self.cnt["vrp"]
        b_nm, b_vn = self.cnt["tnm"], self.cnt["vnm"]

        proj_items = [(li, w, nb) for li in range(PPG) for w in ("q", "k")
                      for nb in range(NB)]
        VPP = 2 * (1 + NB) + 1
        TPP = 2 * NB

        @blk.gpsimd
        def _(gp):
            gp.wait_ge(self.sems["tpj"], b_pj)
            for kc in range(KE):
                self.inc(gp.dma_start(
                    t["wbuf"][:, kc * WQC:(kc + 1) * WQC],
                    d["wq"][kc * 128:(kc + 1) * 128, g * WQC:(g + 1) * WQC]), "dG", 16)
            for lc in range(KL):
                self.inc(gp.dma_start(
                    t["wbuf"][:, KE * WQC + lc * WQC: KE * WQC + (lc + 1) * WQC],
                    d["wku"][lc * 128:(lc + 1) * 128, g * WQC:(g + 1) * WQC]), "dG", 16)

        @blk.tensor
        def _(tn):
            for idx, (li, w, nb) in enumerate(proj_items):
                if w == "q":
                    tn.wait_ge(self.sems["dG"], b_dG + 16 * KE)
                    nk, wof, src = KE, 0, "xnT"
                else:
                    tn.wait_ge(self.sems["dG"], b_dG + 16 * (KE + KL))
                    nk, wof, src = KL, KE * WQC, "latT"
                if idx >= 2:
                    tn.wait_ge(self.sems["vpe"], b_pe + idx - 1)
                for kc in range(nk):
                    mm = tn.matmul(
                        ps["pp"][idx % 2][:, 0:NBW],
                        t["wbuf"][:, wof + kc * WQC + li * 128: wof + kc * WQC + (li + 1) * 128],
                        t[src][:, kc * T + nb * NBW: kc * T + (nb + 1) * NBW],
                        start=(kc == 0), stop=(kc == nk - 1))
                self.inc(mm, "tpj", 1)
                tn.wait_ge(self.sems["vpe"], b_pe + idx + 1)
                if idx >= 2:
                    tn.wait_ge(self.sems["vrp"], b_rp + idx - 1)
                sw = tn.matmul(ps["pswp"][idx % 2][:, 0:NBW], t["Psw"][:, :],
                               t["qraw"][li % 2][:, nb * NBW:(nb + 1) * NBW],
                               start=True, stop=True)
                self.inc(sw, "tsw", 1)
            for li in range(PPG):
                vb = b_vn + li * VPP
                for side in range(2):
                    for c in range(NB):
                        tn.wait_ge(self.sems["vnm"], vb + side * (1 + NB) + 1 + c)
                        mm = tn.matmul(ps["pnorm"][:, 0:NBW], t["onesf"][:, :],
                                       t["qraw"][0][:, c * NBW:(c + 1) * NBW],
                                       start=True, stop=True)
                        self.inc(mm, "tnm", 1)

        @blk.vector
        def _(vec):
            vec.wait_ge(self.sems["dS"], self.tabs_dS)
            for idx, (li, w, nb) in enumerate(proj_items):
                vec.wait_ge(self.sems["tpj"], b_pj + idx + 1)
                ev = vec.tensor_copy(t["qraw"][li % 2][:, nb * NBW:(nb + 1) * NBW],
                                     ps["pp"][idx % 2][:, 0:NBW])
                self.inc(ev, "vpe", 1)
                vec.wait_ge(self.sems["tsw"], b_sw + idx + 1)
                dstT = t["qT"] if w == "q" else t["kT"]
                cols = slice(nb * NBW, (nb + 1) * NBW)
                vec.tensor_mul(t["ttmp"][:, 0:NBW], t["qraw"][li % 2][:, cols],
                               t["tc"][:, cols])
                vec.tensor_mul(t["ttmp2"][:, 0:NBW], ps["pswp"][idx % 2][:, 0:NBW],
                               t["ts"][:, cols])
                rp = vec.tensor_add(dstT[:, li * T + nb * NBW: li * T + (nb + 1) * NBW],
                                    t["ttmp"][:, 0:NBW], t["ttmp2"][:, 0:NBW])
                self.inc(rp, "vrp", 1)
            for li in range(PPG):
                hp = g * PPG + li
                tb = b_nm + li * TPP
                for side, srcT in ((0, t["qT"]), (1, t["kT"])):
                    vec.wait_ge(self.sems["tnm"], tb + side * NB)
                    sq = vec.tensor_mul(t["qraw"][0][:, 0:T],
                                        srcT[:, li * T:(li + 1) * T],
                                        srcT[:, li * T:(li + 1) * T])
                    self.inc(sq, "vnm", 1)
                    for c in range(NB):
                        vec.wait_ge(self.sems["tnm"], tb + side * NB + c + 1)
                        self.vfence(vec, vec.reduce_max(t["stats"][:, 5:6],
                                                        ps["pnorm"][:, 0:NBW],
                                                        axis=AX.X))
                        if c == 0:
                            mx = vec.tensor_copy(t["stats"][:, 6 + side:7 + side],
                                                 t["stats"][:, 5:6])
                        else:
                            mx = vec.tensor_max(t["stats"][:, 6 + side:7 + side],
                                                t["stats"][:, 6 + side:7 + side],
                                                t["stats"][:, 5:6])
                        self.vfence(vec, mx)
                        self.inc(vec.tensor_copy(t["stats"][:, 4:5],
                                                 t["stats"][:, 5:6]), "vnm", 1)
                self.vfence(vec, vec.tensor_add(t["stats"][:, 5:6], t["stats"][:, 6:7],
                                                t["stats"][:, 7:8]))
                nb_ = vec.tensor_scalar_mul(t["negB"][:, hp:hp + 1],
                                            t["stats"][:, 5:6], -0.0625)
                self.inc(nb_, "vnm", 1)

    def phase_attn(self, blk, g):
        cfg, t, d, ps = self.cfg, self.t, self.d, self.ps
        T, QBW, VW, HEADS = cfg.T, cfg.QBW, cfg.VW, cfg.HEADS
        PPG, NQB, JT, E = cfg.PPG, cfg.NQB, cfg.JT, cfg.E
        b_tS, b_sE = self.cnt["tS"], self.cnt["sE"]
        b_PV, b_NR = self.cnt["tPV"], self.cnt["vNR"]
        b_RC, b_BC = self.cnt["sRC"], self.cnt["sBC"]
        b_nm = self.cnt["tnm"]
        b_dSp = self.cnt["dSp"]
        sl0 = b_dSp // 32

        units = []
        for li in range(PPG):
            for qb in range(NQB):
                units.append((li, qb, cfg.score_items(qb)))
        assert cfg.JT % (cfg.NGRP * len(units)) == 0, "w2 tile/unit mismatch"
        TPS = cfg.JT // (cfg.NGRP * len(units))
        flat = []
        for ui, (li, qb, items) in enumerate(units):
            for (j, kt, coff, N) in items:
                flat.append((ui, li, qb, j, kt, coff, N, j == 0,
                             j == len(items) - 1))
        cum = [0]
        for (_, _, items) in units:
            cum.append(cum[-1] + len(items))

        @blk.tensor
        def _(tn):
            def emit_scores(fi):
                ui, li, qb, j, kt, coff, N, first, last = flat[fi]
                if fi >= 2:
                    tn.wait_ge(self.sems["sE"], b_sE + 2 * (fi - 1))
                tn.matmul(ps["pp"][fi % 2][0:128, coff:coff + N],
                          t["kT"][0:64, li * T + kt * 128: li * T + (kt + 1) * 128],
                          t["qT"][0:64, li * T + qb * QBW + coff: li * T + qb * QBW + coff + N],
                          start=True, stop=True, tile_position=(0, 0))
                mmB = tn.matmul(ps["pswp"][fi % 2][0:128, coff:coff + N],
                                t["kT"][64:128, li * T + kt * 128: li * T + (kt + 1) * 128],
                                t["qT"][64:128, li * T + qb * QBW + coff: li * T + qb * QBW + coff + N],
                                start=True, stop=True, tile_position=(64, 0))
                self.inc(mmB, "tS", 1)

            emit_scores(0)
            for fi, (ui, li, qb, j, kt, coff, N, first, last) in enumerate(flat):
                hp = g * PPG + li
                if fi + 1 < len(flat):
                    emit_scores(fi + 1)
                tn.wait_ge(self.sems["sE"], b_sE + 2 * (fi + 1))
                if first and ui >= 1:
                    tn.wait_ge(self.sems["vNR"], b_NR + 2 * ui)
                vbase = kt * HEADS * VW
                tn.matmul(ps["pvA"][0:VW, coff:coff + N],
                          t["vext"][:, vbase + 2 * hp * VW: vbase + 2 * hp * VW + VW],
                          t["expA"][fi % 2][0:128, 0:N], start=first, stop=last)
                pv = tn.matmul(ps["pvB"][0:VW, coff:coff + N],
                               t["vext"][:, vbase + (2 * hp + 1) * VW: vbase + (2 * hp + 1) * VW + VW],
                               t["expB"][fi % 2][0:128, 0:N], start=first, stop=last)
                self.inc(pv, "tPV", 1)
                if last:
                    tn.wait_ge(self.sems["sRC"], b_RC + 2 * ui + 1)
                    bA = tn.matmul(ps["pnorm"][0:64, 0:QBW], t["e64"][:, 0:64],
                                   t["recA"][:, 0:QBW], start=True, stop=True)
                    self.inc(bA, "tnm", 1)
                    tn.wait_ge(self.sems["sRC"], b_RC + 2 * ui + 2)
                    tn.wait_ge(self.sems["sBC"], b_BC + 2 * ui + 1)
                    bB = tn.matmul(ps["pnorm"][0:64, 0:QBW], t["e64"][:, 0:64],
                                   t["recB"][:, 0:QBW], start=True, stop=True)
                    self.inc(bB, "tnm", 1)

        @blk.scalar
        def _(sc):
            for fi, (ui, li, qb, j, kt, coff, N, first, last) in enumerate(flat):
                hp = g * PPG + li
                sc.wait_ge(self.sems["tS"], b_tS + fi + 1)
                if fi >= 2:
                    sc.wait_ge(self.sems["tPV"], b_PV + fi - 1)
                eA = sc.activation(t["expA"][fi % 2][0:128, 0:N],
                                   ps["pp"][fi % 2][0:128, coff:coff + N], AF.Exp,
                                   bias=t["negB"][:, hp:hp + 1],
                                   scale=t["cpool"][:, 2:3])
                self.inc(eA, "sE", 1)
                eB = sc.activation(t["expB"][fi % 2][0:128, 0:N],
                                   ps["pswp"][fi % 2][0:128, coff:coff + N], AF.Exp,
                                   bias=t["negB"][:, hp:hp + 1],
                                   scale=t["cpool"][:, 2:3])
                self.inc(eB, "sE", 1)
                if last:
                    sc.wait_ge(self.sems["tPV"], b_PV + cum[ui + 1])
                    sc.activation(t["lnz"][0:1, 0:QBW],
                                  ps["pvA"][64:65, 0:QBW], AF.Ln)
                    rA = sc.activation(t["recA"][64:65, 0:QBW],
                                       t["lnz"][0:1, 0:QBW], AF.Exp,
                                       scale=t["cpool"][64:65, 3:4])
                    self.inc(rA, "sRC", 1)
                    sc.activation(t["lnz"][0:1, 0:QBW],
                                  ps["pvB"][64:65, 0:QBW], AF.Ln)
                    rB = sc.activation(t["recB"][64:65, 0:QBW],
                                       t["lnz"][0:1, 0:QBW], AF.Exp,
                                       scale=t["cpool"][64:65, 3:4])
                    self.inc(rB, "sRC", 1)
                    sc.wait_ge(self.sems["tnm"], b_nm + 2 * ui + 1)
                    cA = sc.activation(t["bcst"][0:64, 0:QBW],
                                       ps["pnorm"][0:64, 0:QBW], AF.Copy)
                    self.inc(cA, "sBC", 1)
                    sc.wait_ge(self.sems["tnm"], b_nm + 2 * ui + 2)
                    cB = sc.activation(t["bcstB"][0:64, 0:QBW],
                                       ps["pnorm"][0:64, 0:QBW], AF.Copy)
                    self.inc(cB, "sBC", 1)

        @blk.gpsimd
        def _(gp):
            for ui in range(len(units)):
                for r in range(TPS):
                    w = (g * len(units) + ui) * TPS + r
                    if w >= 1:
                        gp.wait_ge(self.sems["vW2"], 2 * w)
                    self.inc(gp.dma_start(t["w2st"][:, :],
                                          d["w2t"][w * 128:(w + 1) * 128, :]),
                             "dW", 16)

        @blk.vector
        def _(vec):
            def w2_chain(w):
                vec.wait_ge(self.sems["dW"], 16 * (w + 1))
                vec.tensor_scalar_mul(t["wA"][:, :], t["w2st"][:, :], -1.0)
                vec.tensor_max(t["wC"][:, :], t["wA"][:, :], t["w2st"][:, :])
                vec.tensor_single_scalar(t["wA"][:, :], t["wC"][:, :], 0.25,
                                         OP.is_ge)
                vec.tensor_scalar(t["wD"][:, :], t["wA"][:, :], 15.0 * 2.0 ** 43,
                                  2.0 ** 43, OP.mult, OP.add)
                for B in (2, 1):
                    vec.tensor_scalar_mul(t["wT"][:, :], t["wD"][:, :],
                                          2.0 ** (B - 49))
                    vec.tensor_tensor(t["wA"][:, :], t["wT"][:, :], t["wC"][:, :],
                                      OP.is_ge)
                    vec.tensor_scalar(t["wA"][:, :], t["wA"][:, :],
                                      -(2.0 ** B - 1.0), 2.0 ** B, OP.mult, OP.add)
                    vec.tensor_mul(t["wD"][:, :], t["wD"][:, :], t["wA"][:, :])
                w2qs = t["w2q"][:, w * E:(w + 1) * E]
                vec.tensor_scalar_mul(t["wT"][:, :], t["wD"][:, :], 1.5)
                vec.tensor_add(t["wC"][:, :], t["w2st"][:, :], t["wT"][:, :])
                vec.tensor_sub(t["wC"][:, :], t["wC"][:, :], t["wT"][:, :])
                a = vec.tensor_copy(w2qs, t["w2st"][:, :])
                self.inc(a, "vW2", 1)
                vec.tensor_sub(t["wA"][:, :], t["wC"][:, :], w2qs)
                vec.tensor_mul(t["wA"][:, :], t["wA"][:, :], t["mgad"][:, 0:E])
                q2 = vec.tensor_add(w2qs, w2qs, t["wA"][:, :])
                self.inc(q2, "vW2", 1)

            for ui, (li, qb, items) in enumerate(units):
                gsl = sl0 + ui
                if gsl >= 2:
                    vec.wait_ge(self.sems["dSp"], 32 * (gsl - 1))
                vec.wait_ge(self.sems["sBC"], b_BC + 2 * ui + 1)
                mA = vec.tensor_mul(t["slabA"][ui % 2][0:64, 0:QBW],
                                    ps["pvA"][0:64, 0:QBW], t["bcst"][0:64, 0:QBW])
                self.inc(mA, "vNR", 1)
                vec.wait_ge(self.sems["sBC"], b_BC + 2 * ui + 2)
                mB = vec.tensor_mul(t["slabB"][ui % 2][0:64, 0:QBW],
                                    ps["pvB"][0:64, 0:QBW], t["bcstB"][0:64, 0:QBW])
                self.inc(mB, "vNR", 1)
                for r in range(TPS):
                    w2_chain((g * len(units) + ui) * TPS + r)

        @blk.sync
        def _(sync):
            for ui, (li, qb, items) in enumerate(units):
                hp = g * PPG + li
                sync.wait_ge(self.sems["vNR"], b_NR + 2 * ui + 1)
                self.inc(sync.dma_start(
                    d["aspill"][hp * 128: hp * 128 + 64, qb * QBW:(qb + 1) * QBW],
                    t["slabA"][ui % 2][0:64, 0:QBW]), "dSp", 16)
                sync.wait_ge(self.sems["vNR"], b_NR + 2 * ui + 2)
                self.inc(sync.dma_start(
                    d["aspill"][hp * 128 + 64: (hp + 1) * 128, qb * QBW:(qb + 1) * QBW],
                    t["slabB"][ui % 2][0:64, 0:QBW]), "dSp", 16)


    def phase_ffn1(self, blk):
        cfg, t, d, ps = self.cfg, self.t, self.d, self.ps
        T, E, JT, KE = cfg.T, cfg.E, cfg.JT, cfg.KE
        NB, NBW = cfg.NB, cfg.NBW
        b_dG = self.cnt["dG"]
        b_F1, b_sH, b_vH = self.cnt["tF1"], self.cnt["sH"], self.cnt["vH"]

        @blk.gpsimd
        def _(gp):
            for jt in range(JT):
                if jt >= 2:
                    gp.wait_ge(self.sems["tF1"], b_F1 + jt - 1)
                self.inc(gp.dma_start(
                    t["w1sb"][jt % 2][:, :],
                    bass.AP(d["w1b"], jt * E * 128,
                            [[128, 128], [128 * 128, KE], [1, 128]])), "dG", 16)
        @blk.tensor
        def _(tn):
            for jt in range(JT):
                tn.wait_ge(self.sems["dG"], b_dG + 16 * (jt + 1))
                if jt >= 2:
                    tn.wait_ge(self.sems["sH"], b_sH + (jt - 2) + 1)
                for nb in range(NB):
                    for kc in range(KE):
                        mm = tn.matmul(
                            ps["ph4"][jt % 2][:, nb * NBW:(nb + 1) * NBW],
                            t["w1sb"][jt % 2][:, kc * 128:(kc + 1) * 128],
                            t["xnT"][:, kc * T + nb * NBW: kc * T + (nb + 1) * NBW],
                            start=(kc == 0), stop=(kc == KE - 1))
                self.inc(mm, "tF1", 1)

        @blk.scalar
        def _(sc):
            for jt in range(JT):
                sc.wait_ge(self.sems["tF1"], b_F1 + jt + 1)
                if jt >= 3:
                    sc.wait_ge(self.sems["vH"], b_vH + (jt - 3) + 1)
                h = sc.activation(t["qth"][jt % 3][:, 0:T], ps["ph4"][jt % 2][:, 0:T],
                                  AF.Relu, bias=t["b1sb"][:, jt:jt + 1])
                self.inc(h, "sH", 1)

        @blk.vector
        def _(vec):
            for jt in range(JT):
                vec.wait_ge(self.sems["sH"], b_sH + jt + 1)
                h = t["qth"][jt % 3][:, 0:T]
                gD, gA, gB, gC = (t[k][:, 0:T] for k in ("gD", "gA", "gB", "gC"))
                gT = t["gT"][:, 0:T]
                vec.tensor_single_scalar(gA, h, 4.0, OP.is_ge)
                vec.tensor_scalar(gD, gA, 255.0 * 2.0 ** 43, 2.0 ** 43,
                                  OP.mult, OP.add)
                for B in (4, 2, 1):
                    vec.tensor_scalar_mul(gT, gD, 2.0 ** (B - 49))
                    vec.tensor_tensor(gA, gT, h, OP.is_ge)
                    vec.tensor_scalar(gA, gA, -(2.0 ** B - 1.0), 2.0 ** B,
                                      OP.mult, OP.add)
                    vec.tensor_mul(gD, gD, gA)
                hqs = t["hq"][:, jt * T:(jt + 1) * T]
                vec.tensor_copy(hqs, h)
                vec.tensor_scalar_mul(gT, gD, 1.5)
                z = vec.tensor_add(gC, h, gT)
                self.inc(z, "vH", 1)
                vec.tensor_sub(gC, gC, gT)
                vec.tensor_sub(gA, gC, hqs)
                vec.tensor_mul(gA, gA, t["mgad"][:, 0:T])
                vec.tensor_add(hqs, hqs, gA)
    def phase_out(self, blk):
        cfg, t, d, ps = self.cfg, self.t, self.d, self.ps
        T, E, JT, KE, TT = cfg.T, cfg.E, cfg.JT, cfg.KE, cfg.TT
        NE, NEW = cfg.NE, cfg.NEW
        b_dS, b_dG = self.cnt["dS"], self.cnt["dG"]
        b_F2, b_WO = self.cnt["tF2"], self.cnt["tWO"]
        b_EP = self.cnt["vEP"]
        b_dX2, b_dO = self.cnt["dX2"], self.cnt["dO"]
        b_dSp = self.cnt["dSp"]

        @blk.gpsimd
        def _(gp):
            for kc in range(KE):
                self.inc(gp.dma_start(t["wosb"][:, kc * E:(kc + 1) * E],
                                      d["wo"][kc * 128:(kc + 1) * 128, :]), "dG", 16)

        @blk.sync
        def _(sync):
            self.inc(sync.dma_start(t["b2rep"][:, :],
                                    bass.AP(d["b2r"], 0, [[0, 128], [1, E]])), "dS", 16)
            sync.wait_ge(self.sems["dSp"], b_dSp)

            def out_dma(k):
                sync.wait_ge(self.sems["vEP"], b_EP + 1 + 2 * k + 2)
                self.inc(sync.dma_start(d["out"][k * 128:(k + 1) * 128, :],
                                        t["obuf"][k % 2][:, :]), "dO", 16)

            for tt in range(TT):
                if tt >= 2:
                    sync.wait_ge(self.sems["tWO"], b_WO + tt - 1)
                self.inc(sync.dma_start(
                    t["atb"][tt % 2][:, :],
                    bass.AP(d["aspill"], tt * 128,
                            [[T, 128], [128 * T, KE], [1, 128]])), "dS", 16)
                if tt >= 2:
                    sync.wait_ge(self.sems["vEP"], b_EP + 1 + 2 * (tt - 2) + 2)
                self.inc(sync.dma_start(t["xbuf"][tt % 2][:, :],
                                        d["x"][tt * 128:(tt + 1) * 128, :]), "dX2", 16)
                if tt >= 2:
                    out_dma(tt - 2)
            out_dma(TT - 2)
            out_dma(TT - 1)
            sync.wait_ge(self.sems["dO"], b_dO + 16 * TT)

        @blk.tensor
        def _(tn):
            tn.wait_ge(self.sems["dG"], b_dG + 16 * KE)
            for tt in range(TT):
                if tt >= 2:
                    tn.wait_ge(self.sems["vEP"], b_EP + 1 + 2 * (tt - 2) + 1)
                for jc in range(JT):
                    for ne in range(NE):
                        mm = tn.matmul(
                            ps["psf"][tt % 2][:, ne * NEW:(ne + 1) * NEW],
                            t["hq"][:, jc * T + tt * 128: jc * T + (tt + 1) * 128],
                            t["w2q"][:, jc * E + ne * NEW: jc * E + (ne + 1) * NEW],
                            start=(jc == 0), stop=(jc == JT - 1))
                self.inc(mm, "tF2", 1)
                tn.wait_ge(self.sems["dS"], b_dS + 16 * (tt + 2))
                if tt >= 2:
                    tn.wait_ge(self.sems["vEP"], b_EP + 1 + 2 * (tt - 2) + 2)
                for kc in range(KE):
                    for ne in range(NE):
                        mm = tn.matmul(
                            ps["pso"][tt % 2][:, ne * NEW:(ne + 1) * NEW],
                            t["atb"][tt % 2][:, kc * 128:(kc + 1) * 128],
                            t["wosb"][:, kc * E + ne * NEW: kc * E + (ne + 1) * NEW],
                            start=(kc == 0), stop=(kc == KE - 1))
                self.inc(mm, "tWO", 1)

        @blk.vector
        def _(vec):
            def gq5(xn_f, out_ap):
                vec.tensor_scalar_mul(t["oA"][:, :], xn_f, -1.0)
                vec.tensor_max(t["oav"][:, :], t["oA"][:, :], xn_f)
                vec.tensor_single_scalar(t["oA"][:, :], t["oav"][:, :], 2.0 ** -6,
                                         OP.is_ge)
                vec.tensor_scalar(t["oD"][:, :], t["oA"][:, :], 255.0 * 2.0 ** 28,
                                  2.0 ** 28, OP.mult, OP.add)
                for B in (8, 4, 2, 1):
                    vec.tensor_scalar_mul(t["oT"][:, :], t["oD"][:, :],
                                          2.0 ** (B - 42))
                    vec.tensor_tensor(t["oA"][:, :], t["oT"][:, :], t["oav"][:, :],
                                      OP.is_ge)
                    vec.tensor_scalar(t["oA"][:, :], t["oA"][:, :],
                                      -(2.0 ** B - 1.0), 2.0 ** B, OP.mult, OP.add)
                    vec.tensor_mul(t["oD"][:, :], t["oD"][:, :], t["oA"][:, :])
                vec.tensor_copy(t["oB"][:, :], xn_f)
                vec.tensor_scalar_mul(t["oT"][:, :], t["oD"][:, :], 1.5)
                vec.tensor_add(t["oC"][:, :], xn_f, t["oT"][:, :])
                vec.tensor_sub(t["oC"][:, :], t["oC"][:, :], t["oT"][:, :])
                vec.tensor_sub(t["oS"][:, :], t["oC"][:, :], t["oB"][:, :])
                vec.tensor_mul(t["oS"][:, :], t["oS"][:, :], t["mgad"][:, 0:E])
                return vec.tensor_add(out_ap, t["oB"][:, :], t["oS"][:, :])

            vec.wait_ge(self.sems["dS"], b_dS + 16)
            qb2 = gq5(t["b2rep"][:, :], t["b2q"][:, :])
            self.inc(qb2, "vEP", 1)
            for tt in range(TT):
                vec.wait_ge(self.sems["tF2"], b_F2 + tt + 1)
                vec.tensor_add(t["o2b"][:, :], ps["psf"][tt % 2][:, 0:E],
                               t["b2q"][:, :])
                q5 = gq5(t["o2b"][:, :], t["o2b"][:, :])
                self.inc(q5, "vEP", 1)
                vec.wait_ge(self.sems["tWO"], b_WO + tt + 1)
                vec.wait_ge(self.sems["dX2"], b_dX2 + 16 * (tt + 1))
                if tt >= 2:
                    vec.wait_ge(self.sems["dO"], b_dO + 16 * (tt - 1))
                vec.tensor_add(t["oS"][:, :], t["o2b"][:, :],
                               ps["pso"][tt % 2][:, 0:E])
                fin = vec.tensor_add(t["obuf"][tt % 2][:, :], t["oS"][:, :],
                                     t["xbuf"][tt % 2][:, :])
                self.inc(fin, "vEP", 1)


_BUILD_CACHE = {}


def get_nc(cfg: Cfg):
    key = (cfg.T, cfg.E, cfg.HEADS, cfg.LAT, cfg.HF, cfg.PAGE)
    if key not in _BUILD_CACHE:
        _BUILD_CACHE[key] = Builder(cfg).build()
    return _BUILD_CACHE[key]


def make_host_constants(cfg: Cfg):
    T = cfg.T
    bf = ml_dtypes.bfloat16
    inv = 1.0 / 10000.0 ** (np.arange(0, 64, 2, dtype=np.float32) / 64.0)
    ang = np.outer(np.arange(T, dtype=np.float32), inv)
    cos, sin = np.cos(ang), np.sin(ang)
    CT = np.zeros((128, T), np.float32)
    ST = np.zeros((128, T), np.float32)
    for p in range(128):
        j = p % 64
        jj = j % 32
        CT[p] = cos[:, jj]
        ST[p] = (-sin[:, jj]) if j < 32 else sin[:, jj]
    P = np.zeros((128, 128), np.float32)
    for i in range(128):
        jj = i + 32 if (i % 64) < 32 else i - 32
        P[jj, i] = 1.0
    E64 = np.zeros((128, 64), np.float32)
    E64[64, :] = 1.0
    return {
        "tc": np.ascontiguousarray(CT),
        "ts": np.ascontiguousarray(ST),
        "Pswap": np.ascontiguousarray(P.astype(bf)),
        "identin": np.ascontiguousarray(np.eye(128, dtype=np.float32).astype(bf)),
        "e64in": np.ascontiguousarray(E64.astype(bf)),
        "onesin": np.ascontiguousarray(np.ones((128, 128), np.float32).astype(bf)),
        "onescol": np.ascontiguousarray(np.ones((128, 1), np.float32).astype(bf)),
    }


def head_perm(E):
    perm = []
    for h in range(E // 64):
        base = h * 64
        perm += [base + 2 * i for i in range(32)] + [base + 2 * i + 1 for i in range(32)]
    return np.array(perm)


def prepare_inputs(cfg: Cfg, x, wq, w_down, wk_up, wv_up, wo, w1, b1, w2, b2):
    perm = head_perm(cfg.E)
    c = lambda a: np.ascontiguousarray(np.asarray(a), dtype=np.float32)
    wq, w_down, wk_up, wv_up, wo = (np.asarray(a, dtype=np.float32)
                                    for a in (wq, w_down, wk_up, wv_up, wo))
    w1, b1, w2, b2 = (np.asarray(a, dtype=np.float32) for a in (w1, b1, w2, b2))
    base = {
        "wq": c(wq[:, perm]),
        "wdn": c(w_down),
        "wku": c(wk_up[:, perm]),
        "wvu": c(wv_up),
        "wo": c(wo),
        "w1b": c(w1.T.reshape(cfg.E, cfg.JT, 128).transpose(1, 0, 2)),
        "w2t": c(w2.T),
        "b1r": c(b1.reshape(cfg.JT, 128).T),
        "b2r": c(b2.reshape(1, cfg.E)),
    }
    base.update(make_host_constants(cfg))
    in_maps = []
    for i in range(x.shape[0]):
        m = dict(base)
        m["x"] = c(x[i])
        in_maps.append(m)
    return in_maps


def kernel(L0_attn_x, L0_attn_wq, L0_attn_w_down, L0_attn_wk_up, L0_attn_wv_up,
           L0_attn_wo, L0_ffn_w1, L0_ffn_b1, L0_ffn_w2, L0_ffn_b2):
    cfg = Cfg()
    x = np.asarray(L0_attn_x, dtype=np.float32)
    in_maps = prepare_inputs(cfg, x, L0_attn_wq, L0_attn_w_down, L0_attn_wk_up,
                             L0_attn_wv_up, L0_attn_wo, L0_ffn_w1, L0_ffn_b1,
                             L0_ffn_w2, L0_ffn_b2)
    nc = get_nc(cfg)
    res = run_bass_kernel_spmd(nc, in_maps, core_ids=list(range(cfg.n_cores)))
    return np.stack([np.asarray(r["out"], dtype=np.float32) for r in res.results])


# version 19
# speedup vs baseline: 1.2299x; 1.0128x over previous
import contextlib
import math

import numpy as np
import ml_dtypes

import concourse.bass as bass
import concourse.mybir as mybir
from concourse.bass_utils import run_bass_kernel_spmd

F32 = mybir.dt.float32
BF16 = mybir.dt.bfloat16
F16 = mybir.dt.float16
F8E4 = mybir.dt.float8e4
I16 = mybir.dt.int16
I32 = mybir.dt.int32
AF = mybir.ActivationFunctionType
OP = mybir.AluOpType
AX = mybir.AxisListType
LN2 = math.log(2.0)
INVLN2 = 1.0 / LN2
LN2_HI = float(np.float32(LN2))
LN2_LO = float(np.float64(LN2) - np.float64(np.float32(LN2)))


class Cfg:
    def __init__(self, T=2048, E=1024, HEADS=16, LAT=512, HF=4096, PAGE=256,
                 n_cores=8, ngrp=None):
        self.T, self.E, self.HEADS, self.LAT, self.HF = T, E, HEADS, LAT, HF
        self.PAGE = PAGE
        self.n_cores = n_cores
        self.D = 64
        assert E == HEADS * 64
        self.PAIRS = HEADS // 2
        self.TT = T // 128
        self.KE = E // 128
        self.KL = LAT // 128
        self.JT = HF // 128
        self.QBW = min(512, T)
        self.NQB = T // self.QBW
        self.KTILES = T // 128
        self.NBW = min(512, T)
        self.NB = T // self.NBW
        self.NEW = min(512, E)
        self.NE = E // self.NEW
        self.NGRP = ngrp if ngrp is not None else min(4, self.PAIRS)
        assert self.PAIRS % self.NGRP == 0
        self.PPG = self.PAIRS // self.NGRP
        self.VW = 65

    def score_items(self, qb):
        out = []
        j = 0
        for kt in range(self.KTILES):
            qlo = (kt * 128 // self.PAGE) * self.PAGE
            qs = max(qb * self.QBW, qlo)
            qe = (qb + 1) * self.QBW
            if qs >= qe:
                break
            out.append((j, kt, qs - qb * self.QBW, qe - qs))
            j += 1
        return out


class Builder:
    def __init__(self, cfg: Cfg):
        self.cfg = cfg
        self.nc = bass.Bass("TRN2", target_bir_lowering=False,
                            detect_race_conditions=False,
                            dynamic_dma_scratch_size=16384)
        self.es = contextlib.ExitStack()
        self.cnt = {}

    def sem(self, name):
        h = self.es.enter_context(self.nc.semaphore(name))
        self.cnt[name] = 0
        return h

    def inc(self, instr, name, n=1):
        instr.then_inc(self.sems[name], n)
        self.cnt[name] += n
        return self.cnt[name]

    def vfence(self, vec, instr):
        c = self.inc(instr, "vrd", 1)
        vec.wait_ge(self.sems["vrd"], c)

    def build(self):
        cfg = self.cfg
        nc = self.nc
        T, E, LAT, HF = cfg.T, cfg.E, cfg.LAT, cfg.HF
        TT, KE, KL, JT = cfg.TT, cfg.KE, cfg.KL, cfg.JT
        QBW, VW = cfg.QBW, cfg.VW
        S = self.es.enter_context
        d = self.d = {}
        t = self.t = {}

        d["x"] = nc.dram_tensor("x", [T, E], F32, kind="ExternalInput")
        d["wq"] = nc.dram_tensor("wq", [E, E], F32, kind="ExternalInput")
        d["wdn"] = nc.dram_tensor("wdn", [E, LAT], F32, kind="ExternalInput")
        d["wku"] = nc.dram_tensor("wku", [LAT, E], F32, kind="ExternalInput")
        d["wvu"] = nc.dram_tensor("wvu", [LAT, E], F32, kind="ExternalInput")
        d["wo"] = nc.dram_tensor("wo", [E, E], F32, kind="ExternalInput")
        d["w1b"] = nc.dram_tensor("w1b", [JT, E, 128], F32, kind="ExternalInput")
        d["w2t"] = nc.dram_tensor("w2t", [HF, E], F32, kind="ExternalInput")
        d["b1r"] = nc.dram_tensor("b1r", [128, JT], F32, kind="ExternalInput")
        d["b2r"] = nc.dram_tensor("b2r", [1, E], F32, kind="ExternalInput")
        d["tc"] = nc.dram_tensor("tc", [128, T], F32, kind="ExternalInput")
        d["ts"] = nc.dram_tensor("ts", [128, T], F32, kind="ExternalInput")
        d["Pswap"] = nc.dram_tensor("Pswap", [128, 128], BF16, kind="ExternalInput")
        d["identin"] = nc.dram_tensor("identin", [128, 128], BF16, kind="ExternalInput")
        d["e64in"] = nc.dram_tensor("e64in", [128, 64], BF16, kind="ExternalInput")
        d["onesin"] = nc.dram_tensor("onesin", [128, 128], BF16, kind="ExternalInput")
        d["onescol"] = nc.dram_tensor("onescol", [128, 1], BF16, kind="ExternalInput")
        d["out"] = nc.dram_tensor("out", [T, E], F32, kind="ExternalOutput")
        spill_kind = "ExternalOutput" if getattr(self, "debug", False) else "Internal"
        d["aspill"] = nc.dram_tensor("aspill", [E, T], BF16, kind=spill_kind)
        if getattr(self, "debug", False):
            d["dbg_qT"] = nc.dram_tensor("dbg_qT", [128, cfg.PPG * T], F32, kind="ExternalOutput")
            d["dbg_kT"] = nc.dram_tensor("dbg_kT", [128, cfg.PPG * T], F32, kind="ExternalOutput")
            d["dbg_ones"] = nc.dram_tensor("dbg_ones", [128, 128], F32, kind="ExternalOutput")
            d["dbg_negB"] = nc.dram_tensor("dbg_negB", [128, cfg.PAIRS], F32, kind="ExternalOutput")
            d["dbg_b2q"] = nc.dram_tensor("dbg_b2q", [128, E], F32, kind="ExternalOutput")
            d["dbg_hq"] = nc.dram_tensor("dbg_hq", [128, T], F32, kind="ExternalOutput")
            d["dbg_w2q"] = nc.dram_tensor("dbg_w2q", [128, E], F32, kind="ExternalOutput")
            d["dbg_xnT"] = nc.dram_tensor("dbg_xnT", [128, T], F32, kind="ExternalOutput")


        L = dict(side="left")
        t["cpool"] = S(nc.sbuf_tensor("cpool", [128, 12], F32, **L))
        t["ones_c"] = S(nc.sbuf_tensor("ones_c", [128, 1], BF16, **L))
        t["onesf"] = S(nc.sbuf_tensor("onesf", [128, 128], BF16, **L))
        t["e64"] = S(nc.sbuf_tensor("e64s", [128, 64], BF16, **L))
        t["Psw"] = S(nc.sbuf_tensor("Psws", [128, 128], BF16, **L))
        t["ident"] = S(nc.sbuf_tensor("idents", [128, 128], BF16, **L))
        t["negB"] = S(nc.sbuf_tensor("negB", [128, cfg.PAIRS], F32, **L))
        t["b1sb"] = S(nc.sbuf_tensor("b1sb", [128, JT], F32, **L))
        t["stats"] = S(nc.sbuf_tensor("stats", [128, 8], F32, **L))
        t["s11"] = S(nc.sbuf_tensor("s11", [1, 8], F32, **L))
        t["xbuf"] = [S(nc.sbuf_tensor(f"xbuf{i}", [128, E], F32, **L)) for i in range(2)]
        t["w2q"] = S(nc.sbuf_tensor("w2q", [128, JT * E], F8E4, **L))

        R = dict(side="right")
        t["xnT"] = S(nc.sbuf_tensor("xnT", [128, KE * T], BF16, **R))

        self.sems = {n: self.sem(n) for n in (
            "dS", "dG", "dO", "dSp", "dX2",
            "vxn", "vvar", "vev", "ttr", "sln",
            "tpj", "vpe", "tsw", "vrp", "tnm", "vnm",
            "tS", "sE", "tPV", "vNR", "sRC", "sBC",
            "vW2", "dW",
            "tF1", "sH", "vH",
            "tF2", "tWO", "vEP", "sEP", "vrd", "vO2", "sC5",
        )}

        t["mgad"] = S(nc.sbuf_tensor("mgad", [128, T], BF16, **R))
        t["w2st"] = S(nc.sbuf_tensor("w2st", [128, E], F32, **R))
        t["wD"] = S(nc.sbuf_tensor("wD", [128, E], BF16, **R))
        t["wA"] = S(nc.sbuf_tensor("wA", [128, E], BF16, **R))
        t["wC"] = S(nc.sbuf_tensor("wC", [128, E], BF16, **R))
        t["wT"] = S(nc.sbuf_tensor("wT", [128, E], BF16, **R))

        sc1 = contextlib.ExitStack()
        t["xcb"] = sc1.enter_context(nc.sbuf_tensor("xcb", [128, E], F32, **R))
        t["xc2b"] = sc1.enter_context(nc.sbuf_tensor("xc2b", [128, E], F32, **R))
        t["xnb"] = [sc1.enter_context(nc.sbuf_tensor(f"xnb{i}", [128, E], BF16, **R))
                    for i in range(2)]
        ps = self.ps = {}
        ps["ptr"] = [sc1.enter_context(nc.psum_tensor(f"ptr{i}", [128, 128], BF16))
                     for i in range(2)]
        with nc.Block() as blk:
            self.phase1(blk)
        sc1.close()

        sc2 = contextlib.ExitStack()
        E2 = sc2.enter_context
        t["tc"] = E2(nc.sbuf_tensor("tcs", [128, T], F32, **R))
        t["ts"] = E2(nc.sbuf_tensor("tss", [128, T], F32, **R))
        t["latT"] = E2(nc.sbuf_tensor("latT", [128, KL * T], BF16, **R))
        t["vext"] = E2(nc.sbuf_tensor("vext", [128, TT * cfg.HEADS * VW], BF16, **R))
        t["qT"] = E2(nc.sbuf_tensor("qT", [128, cfg.PPG * T], BF16, **R))
        t["kT"] = E2(nc.sbuf_tensor("kT", [128, cfg.PPG * T], BF16, **R))
        t["wbuf"] = E2(nc.sbuf_tensor("wbuf", [128, max(KE * LAT, KL * E, (KE + KL) * cfg.PPG * 128)], BF16, **R))
        t["qraw"] = [E2(nc.sbuf_tensor(f"qraw{i}", [128, T], BF16, **R)) for i in range(2)]
        t["ttmp"] = E2(nc.sbuf_tensor("ttmp", [128, cfg.NBW], BF16, **R))
        t["ttmp2"] = E2(nc.sbuf_tensor("ttmp2", [128, cfg.NBW], BF16, **R))
        t["expA"] = [E2(nc.sbuf_tensor(f"expA{i}", [128, QBW], BF16, **R)) for i in range(2)]
        t["expB"] = [E2(nc.sbuf_tensor(f"expB{i}", [128, QBW], BF16, **R)) for i in range(2)]
        t["recA"] = E2(nc.sbuf_tensor("recA", [128, QBW], BF16, **R))
        t["lnz"] = E2(nc.sbuf_tensor("lnz", [1, QBW], F32, **R))
        t["recB"] = E2(nc.sbuf_tensor("recB", [128, QBW], BF16, **R))
        t["bcst"] = E2(nc.sbuf_tensor("bcst", [64, QBW], BF16, **R))
        t["bcstB"] = E2(nc.sbuf_tensor("bcstB", [64, QBW], BF16, **R))
        t["slabA"] = [E2(nc.sbuf_tensor(f"slabA{i}", [64, QBW], BF16, **R)) for i in range(2)]
        t["slabB"] = [E2(nc.sbuf_tensor(f"slabB{i}", [64, QBW], BF16, **R)) for i in range(2)]
        ps["pp"] = [E2(nc.psum_tensor(f"pp{i}", [128, 512], F32)) for i in range(2)]
        ps["pswp"] = [E2(nc.psum_tensor(f"pswp{i}", [128, 512], F32)) for i in range(2)]
        ps["pnorm"] = E2(nc.psum_tensor("pnorm", [128, 512], F32))
        ps["pvA"] = E2(nc.psum_tensor("pvA", [128, 512], F32))
        ps["pvB"] = E2(nc.psum_tensor("pvB", [128, 512], F32))
        with nc.Block() as blk:
            self.phase_proj_v(blk)
        for g in range(cfg.NGRP):
            with nc.Block() as blk:
                self.phase_proj_qk(blk, g)
            with nc.Block() as blk:
                self.phase_attn(blk, g)
        if getattr(self, "debug", False):
            with nc.Block() as blk:
                @blk.gpsimd
                def _(gp):
                    dbga = self.sem("dbga")
                    gp.dma_start(d["dbg_qT"][:, :], t["qT"][:, :]).then_inc(dbga, 16)
                    gp.dma_start(d["dbg_kT"][:, :], t["kT"][:, :]).then_inc(dbga, 16)
                    gp.dma_start(d["dbg_ones"][:, :], t["onesf"][:, :]).then_inc(dbga, 16)
                    gp.wait_ge(dbga, 48)
        sc2.close()

        t["hq"] = S(nc.sbuf_tensor("hq", [128, JT * T], F8E4, **R))
        sc4 = contextlib.ExitStack()
        E4 = sc4.enter_context
        t["w1sb"] = [E4(nc.sbuf_tensor(f"w1sb{i}", [128, KE * 128], BF16, **R))
                     for i in range(2)]
        t["qth"] = [E4(nc.sbuf_tensor(f"qth{i}", [128, T], BF16, **R)) for i in range(3)]
        t["gD"] = E4(nc.sbuf_tensor("gD", [128, T], BF16, **R))
        t["gA"] = E4(nc.sbuf_tensor("gA", [128, T], BF16, **R))
        t["gB"] = E4(nc.sbuf_tensor("gB", [128, T], BF16, **R))
        t["gC"] = E4(nc.sbuf_tensor("gC", [128, T], BF16, **R))
        t["gT"] = E4(nc.sbuf_tensor("gT", [128, T], BF16, **R))
        ps["ph4"] = [E4(nc.psum_tensor(f"ph4_{i}", [128, min(2048, T)], F32))
                     for i in range(2)]
        with nc.Block() as blk:
            self.phase_ffn1(blk)
        sc4.close()

        sc5 = contextlib.ExitStack()
        E5 = sc5.enter_context
        t["wosb"] = E5(nc.sbuf_tensor("wosb", [128, KE * E], BF16, **R))
        t["atb"] = [E5(nc.sbuf_tensor(f"atb{i}", [128, E], BF16, **R)) for i in range(2)]
        t["o2b"] = E5(nc.sbuf_tensor("o2b", [128, E], F32, **R))
        t["oav"] = E5(nc.sbuf_tensor("oav", [128, E], BF16, **R))
        t["oD"] = E5(nc.sbuf_tensor("oD", [128, E], BF16, **R))
        t["oA"] = E5(nc.sbuf_tensor("oA", [128, E], BF16, **R))
        t["oB"] = E5(nc.sbuf_tensor("oB", [128, E], F16, **R))
        t["oC"] = E5(nc.sbuf_tensor("oC", [128, E], BF16, **R))
        t["oT"] = E5(nc.sbuf_tensor("oT", [128, E], BF16, **R))
        t["oS"] = E5(nc.sbuf_tensor("oS", [128, E], F32, **R))
        t["b2q"] = E5(nc.sbuf_tensor("b2q", [128, E], F32, **R))
        t["obuf"] = [E5(nc.sbuf_tensor(f"obuf{i}", [128, E], F32, **R)) for i in range(2)]
        t["b2rep"] = t["obuf"][0]
        ps["psf"] = [E5(nc.psum_tensor(f"psf{i}", [128, E], F32)) for i in range(2)]
        ps["pso"] = [E5(nc.psum_tensor(f"pso{i}", [128, E], F32)) for i in range(2)]
        with nc.Block() as blk:
            self.phase_out(blk)
        sc5.close()

        if getattr(self, "debug", False):
            t["dbgf"] = S(nc.sbuf_tensor("dbgf", [128, T], F32, **R))
            with nc.Block() as blk:
                @blk.vector
                def _(vec):
                    vec.tensor_copy(t["dbgf"][:, 0:T], t["hq"][:, 0:T])
                @blk.gpsimd
                def _(gp):
                    dbg = self.sem("dbgd")
                    gp.wait_ge(self.sems["vEP"], self.cnt["vEP"])
                    gp.dma_start(d["dbg_negB"][:, :], t["negB"][:, :]).then_inc(dbg, 16)
                    gp.dma_start(d["dbg_b2q"][:, :], t["b2q"][:, :]).then_inc(dbg, 16)
                    gp.dma_start(d["dbg_w2q"][:, :], t["w2q"][:, 0:E]).then_inc(dbg, 16)
                    gp.dma_start(d["dbg_xnT"][:, :], t["xnT"][:, 0:T]).then_inc(dbg, 16)
                    gp.wait_ge(dbg, 64)
            with nc.Block() as blk:
                @blk.gpsimd
                def _(gp):
                    dbg2 = self.sem("dbgd2")
                    gp.dma_start(d["dbg_hq"][:, :], t["dbgf"][:, 0:T]).then_inc(dbg2, 16)
                    gp.wait_ge(dbg2, 16)

        self.es.close()
        return nc

    def phase1(self, blk):
        cfg, t, d, ps = self.cfg, self.t, self.d, self.ps
        T, E, TT, KE = cfg.T, cfg.E, cfg.TT, cfg.KE
        NCONST = 6

        @blk.sync
        def _(sync):
            self.inc(sync.dma_start(t["Psw"][:, :], d["Pswap"][:, :]), "dS", 16)
            self.inc(sync.dma_start(t["ident"][:, :], d["identin"][:, :]), "dS", 16)
            self.inc(sync.dma_start(t["e64"][:, :], d["e64in"][:, :]), "dS", 16)
            self.inc(sync.dma_start(t["ones_c"][:, 0:1], d["onescol"][:, :]), "dS", 16)
            self.inc(sync.dma_start(t["onesf"][:, :], d["onesin"][:, :]), "dS", 16)
            self.inc(sync.dma_start(t["b1sb"][:, :], d["b1r"][:, :]), "dS", 16)
            for tt in range(TT):
                if tt >= 2:
                    sync.wait_ge(self.sems["vxn"], tt - 1)
                self.inc(sync.dma_start(t["xbuf"][tt % 2][:, :],
                                        d["x"][tt * 128:(tt + 1) * 128, :]), "dS", 16)

        @blk.vector
        def _(vec):
            W = TT * cfg.HEADS * cfg.VW
            vec.memset(t["cpool"][:, 0:1], 1.0 / E)
            vec.memset(t["cpool"][:, 1:2], 1e-5)
            vec.memset(t["cpool"][:, 2:3], 0.125)
            vec.memset(t["cpool"][:, 3:4], -1.0)
            self.vfence(vec, vec.memset(t["mgad"][:, 0:T], 1024.0))
            self.vfence(vec, vec.tensor_scalar(t["mgad"][:, 0:T], t["mgad"][:, 0:T],
                                               1.0, 2.0 ** 52, OP.mult, OP.add))
            self.vfence(vec, vec.tensor_scalar(t["mgad"][:, 0:T], t["mgad"][:, 0:T],
                                               -(2.0 ** 52), 2.0 ** -10,
                                               OP.add, OP.mult))
            for tt in range(TT):
                vec.wait_ge(self.sems["dS"], 16 * (NCONST + tt + 1))
                xb = t["xbuf"][tt % 2]
                self.vfence(vec, vec.reduce_sum(t["stats"][:, 0:1], xb[:, :],
                                                axis=AX.X))
                self.vfence(vec, vec.tensor_scalar_mul(t["stats"][:, 1:2],
                                                       t["stats"][:, 0:1], 1.0 / E))
                vec.tensor_scalar(t["xcb"][:, :], xb[:, :], t["stats"][:, 1:2], None,
                                  OP.subtract)
                vec.tensor_mul(t["xc2b"][:, :], t["xcb"][:, :], t["xcb"][:, :])
                vv = vec.reduce_sum(t["stats"][:, 2:3], t["xc2b"][:, :], axis=AX.X)
                self.inc(vv, "vvar", 1)
                vec.wait_ge(self.sems["sln"], tt + 1)
                self.vfence(vec, vec.reciprocal(t["stats"][:, 4:5], t["stats"][:, 3:4]))
                xn = vec.tensor_scalar(t["xnb"][tt % 2][:, :], t["xcb"][:, :],
                                       t["stats"][:, 4:5], None, OP.mult)
                self.inc(xn, "vxn", 1)
                for cc in range(KE):
                    g = tt * KE + cc
                    vec.wait_ge(self.sems["ttr"], g + 1)
                    ev = vec.tensor_copy(
                        t["xnT"][:, cc * T + tt * 128: cc * T + (tt + 1) * 128],
                        ps["ptr"][g % 2][:, 0:128])
                    self.inc(ev, "vev", 1)

        @blk.scalar
        def _(sc):
            for tt in range(TT):
                sc.wait_ge(self.sems["vvar"], tt + 1)
                a = sc.activation(t["stats"][:, 3:4], t["stats"][:, 2:3], AF.Sqrt,
                                  bias=t["cpool"][:, 1:2], scale=t["cpool"][:, 0:1])
                self.inc(a, "sln", 1)

        @blk.tensor
        def _(tn):
            tn.wait_ge(self.sems["dS"], 16 * 2)
            for tt in range(TT):
                tn.wait_ge(self.sems["vxn"], tt + 1)
                for cc in range(KE):
                    g = tt * KE + cc
                    if g >= 2:
                        tn.wait_ge(self.sems["vev"], g - 1)
                    tr = tn.transpose(ps["ptr"][g % 2][:, 0:128],
                                      t["xnb"][tt % 2][:, cc * 128:(cc + 1) * 128],
                                      t["ident"][:, :])
                    self.inc(tr, "ttr", 1)

    def phase_proj_v(self, blk):
        cfg, t, d, ps = self.cfg, self.t, self.d, self.ps
        T, E, LAT = cfg.T, cfg.E, cfg.LAT
        TT, KE, KL = cfg.TT, cfg.KE, cfg.KL
        NB, NBW, NE, NEW = cfg.NB, cfg.NBW, cfg.NE, cfg.NEW
        VW, HEADS = cfg.VW, cfg.HEADS
        lat_items = [(lt, nb) for lt in range(KL) for nb in range(NB)]
        v_items = [(tt, fc) for tt in range(TT) for fc in range(NE)]
        b_pj, b_pe = self.cnt["tpj"], self.cnt["vpe"]
        b_dG = self.cnt["dG"]
        b_dS = self.cnt["dS"]

        @blk.sync
        def _(sync):
            self.inc(sync.dma_start(t["tc"][:, :], d["tc"][:, :]), "dS", 16)
            self.inc(sync.dma_start(t["ts"][:, :], d["ts"][:, :]), "dS", 16)
            self.tabs_dS = self.cnt["dS"]

        @blk.gpsimd
        def _(gp):
            for kc in range(KE):
                self.inc(gp.dma_start(t["wbuf"][:, kc * LAT:(kc + 1) * LAT],
                                      d["wdn"][kc * 128:(kc + 1) * 128, :]), "dG", 16)
            gp.wait_ge(self.sems["tpj"], b_pj + len(lat_items))
            for lc in range(KL):
                self.inc(gp.dma_start(t["wbuf"][:, lc * E:(lc + 1) * E],
                                      d["wvu"][lc * 128:(lc + 1) * 128, :]), "dG", 16)

        @blk.tensor
        def _(tn):
            tn.wait_ge(self.sems["dG"], b_dG + 16 * KE)
            for i, (lt, nb) in enumerate(lat_items):
                if i >= 2:
                    tn.wait_ge(self.sems["vpe"], b_pe + i - 1)
                for kc in range(KE):
                    mm = tn.matmul(ps["pp"][i % 2][:, 0:NBW],
                                   t["wbuf"][:, kc * LAT + lt * 128: kc * LAT + (lt + 1) * 128],
                                   t["xnT"][:, kc * T + nb * NBW: kc * T + (nb + 1) * NBW],
                                   start=(kc == 0), stop=(kc == KE - 1))
                self.inc(mm, "tpj", 1)
            tn.wait_ge(self.sems["dG"], b_dG + 16 * (KE + KL))
            for i2, (tt, fc) in enumerate(v_items):
                i = len(lat_items) + i2
                if i >= 2:
                    tn.wait_ge(self.sems["vpe"], b_pe + i - 1)
                for lc in range(KL):
                    mm = tn.matmul(ps["pp"][i % 2][:, 0:NEW],
                                   t["latT"][:, lc * T + tt * 128: lc * T + (tt + 1) * 128],
                                   t["wbuf"][:, lc * E + fc * NEW: lc * E + (fc + 1) * NEW],
                                   start=(lc == 0), stop=(lc == KL - 1))
                self.inc(mm, "tpj", 1)

        @blk.vector
        def _(vec):
            W = TT * HEADS * VW
            for tt in range(TT):
                vec.memset(bass.AP(t["vext"], tt * HEADS * VW + 64,
                                   [[W, 128], [VW, HEADS]]), 1.0)
            vec.memset(t["recA"][:, :], 0.0)
            vec.memset(t["recB"][:, :], 0.0)
            for i, (lt, nb) in enumerate(lat_items):
                vec.wait_ge(self.sems["tpj"], b_pj + i + 1)
                ev = vec.tensor_copy(
                    t["latT"][:, lt * T + nb * NBW: lt * T + (nb + 1) * NBW],
                    ps["pp"][i % 2][:, 0:NBW])
                self.inc(ev, "vpe", 1)
            for i2, (tt, fc) in enumerate(v_items):
                i = len(lat_items) + i2
                vec.wait_ge(self.sems["tpj"], b_pj + i + 1)
                nheads = NEW // 64
                dst = bass.AP(t["vext"], tt * HEADS * VW + fc * nheads * VW,
                              [[W, 128], [VW, nheads], [1, 64]])
                src = bass.AP(ps["pp"][i % 2], 0, [[512, 128], [64, nheads], [1, 64]])
                ev = vec.tensor_copy(dst, src)
                self.inc(ev, "vpe", 1)

    def phase_proj_qk(self, blk, g):
        cfg, t, d, ps = self.cfg, self.t, self.d, self.ps
        T, E, LAT = cfg.T, cfg.E, cfg.LAT
        KE, KL, PPG = cfg.KE, cfg.KL, cfg.PPG
        NB, NBW = cfg.NB, cfg.NBW
        WQC = PPG * 128
        b_dG = self.cnt["dG"]
        b_pj, b_pe = self.cnt["tpj"], self.cnt["vpe"]
        b_sw, b_rp = self.cnt["tsw"], self.cnt["vrp"]
        b_nm, b_vn = self.cnt["tnm"], self.cnt["vnm"]

        proj_items = [(li, w, nb) for li in range(PPG) for w in ("q", "k")
                      for nb in range(NB)]
        VPP = 2 * (1 + NB) + 1
        TPP = 2 * NB

        @blk.gpsimd
        def _(gp):
            gp.wait_ge(self.sems["tpj"], b_pj)
            for kc in range(KE):
                self.inc(gp.dma_start(
                    t["wbuf"][:, kc * WQC:(kc + 1) * WQC],
                    d["wq"][kc * 128:(kc + 1) * 128, g * WQC:(g + 1) * WQC]), "dG", 16)
            for lc in range(KL):
                self.inc(gp.dma_start(
                    t["wbuf"][:, KE * WQC + lc * WQC: KE * WQC + (lc + 1) * WQC],
                    d["wku"][lc * 128:(lc + 1) * 128, g * WQC:(g + 1) * WQC]), "dG", 16)

        @blk.tensor
        def _(tn):
            for idx, (li, w, nb) in enumerate(proj_items):
                if w == "q":
                    tn.wait_ge(self.sems["dG"], b_dG + 16 * KE)
                    nk, wof, src = KE, 0, "xnT"
                else:
                    tn.wait_ge(self.sems["dG"], b_dG + 16 * (KE + KL))
                    nk, wof, src = KL, KE * WQC, "latT"
                if idx >= 2:
                    tn.wait_ge(self.sems["vpe"], b_pe + idx - 1)
                for kc in range(nk):
                    mm = tn.matmul(
                        ps["pp"][idx % 2][:, 0:NBW],
                        t["wbuf"][:, wof + kc * WQC + li * 128: wof + kc * WQC + (li + 1) * 128],
                        t[src][:, kc * T + nb * NBW: kc * T + (nb + 1) * NBW],
                        start=(kc == 0), stop=(kc == nk - 1))
                self.inc(mm, "tpj", 1)
                tn.wait_ge(self.sems["vpe"], b_pe + idx + 1)
                if idx >= 2:
                    tn.wait_ge(self.sems["vrp"], b_rp + idx - 1)
                sw = tn.matmul(ps["pswp"][idx % 2][:, 0:NBW], t["Psw"][:, :],
                               t["qraw"][li % 2][:, nb * NBW:(nb + 1) * NBW],
                               start=True, stop=True)
                self.inc(sw, "tsw", 1)
            for li in range(PPG):
                vb = b_vn + li * VPP
                for side in range(2):
                    for c in range(NB):
                        tn.wait_ge(self.sems["vnm"], vb + side * (1 + NB) + 1 + c)
                        mm = tn.matmul(ps["pnorm"][:, 0:NBW], t["onesf"][:, :],
                                       t["qraw"][0][:, c * NBW:(c + 1) * NBW],
                                       start=True, stop=True)
                        self.inc(mm, "tnm", 1)

        @blk.vector
        def _(vec):
            vec.wait_ge(self.sems["dS"], self.tabs_dS)
            for idx, (li, w, nb) in enumerate(proj_items):
                vec.wait_ge(self.sems["tpj"], b_pj + idx + 1)
                ev = vec.tensor_copy(t["qraw"][li % 2][:, nb * NBW:(nb + 1) * NBW],
                                     ps["pp"][idx % 2][:, 0:NBW])
                self.inc(ev, "vpe", 1)
                vec.wait_ge(self.sems["tsw"], b_sw + idx + 1)
                dstT = t["qT"] if w == "q" else t["kT"]
                cols = slice(nb * NBW, (nb + 1) * NBW)
                vec.tensor_mul(t["ttmp"][:, 0:NBW], t["qraw"][li % 2][:, cols],
                               t["tc"][:, cols])
                vec.tensor_mul(t["ttmp2"][:, 0:NBW], ps["pswp"][idx % 2][:, 0:NBW],
                               t["ts"][:, cols])
                rp = vec.tensor_add(dstT[:, li * T + nb * NBW: li * T + (nb + 1) * NBW],
                                    t["ttmp"][:, 0:NBW], t["ttmp2"][:, 0:NBW])
                self.inc(rp, "vrp", 1)
            for li in range(PPG):
                hp = g * PPG + li
                tb = b_nm + li * TPP
                for side, srcT in ((0, t["qT"]), (1, t["kT"])):
                    vec.wait_ge(self.sems["tnm"], tb + side * NB)
                    sq = vec.tensor_mul(t["qraw"][0][:, 0:T],
                                        srcT[:, li * T:(li + 1) * T],
                                        srcT[:, li * T:(li + 1) * T])
                    self.inc(sq, "vnm", 1)
                    for c in range(NB):
                        vec.wait_ge(self.sems["tnm"], tb + side * NB + c + 1)
                        self.vfence(vec, vec.reduce_max(t["stats"][:, 5:6],
                                                        ps["pnorm"][:, 0:NBW],
                                                        axis=AX.X))
                        if c == 0:
                            mx = vec.tensor_copy(t["stats"][:, 6 + side:7 + side],
                                                 t["stats"][:, 5:6])
                        else:
                            mx = vec.tensor_max(t["stats"][:, 6 + side:7 + side],
                                                t["stats"][:, 6 + side:7 + side],
                                                t["stats"][:, 5:6])
                        self.vfence(vec, mx)
                        self.inc(vec.tensor_copy(t["stats"][:, 4:5],
                                                 t["stats"][:, 5:6]), "vnm", 1)
                self.vfence(vec, vec.tensor_add(t["stats"][:, 5:6], t["stats"][:, 6:7],
                                                t["stats"][:, 7:8]))
                nb_ = vec.tensor_scalar_mul(t["negB"][:, hp:hp + 1],
                                            t["stats"][:, 5:6], -0.0625)
                self.inc(nb_, "vnm", 1)

    def phase_attn(self, blk, g):
        cfg, t, d, ps = self.cfg, self.t, self.d, self.ps
        T, QBW, VW, HEADS = cfg.T, cfg.QBW, cfg.VW, cfg.HEADS
        PPG, NQB, JT, E = cfg.PPG, cfg.NQB, cfg.JT, cfg.E
        b_tS, b_sE = self.cnt["tS"], self.cnt["sE"]
        b_PV, b_NR = self.cnt["tPV"], self.cnt["vNR"]
        b_RC, b_BC = self.cnt["sRC"], self.cnt["sBC"]
        b_nm = self.cnt["tnm"]
        b_dSp = self.cnt["dSp"]
        sl0 = b_dSp // 32

        units = []
        for li in range(PPG):
            for qb in range(NQB):
                units.append((li, qb, cfg.score_items(qb)))
        assert cfg.JT % (cfg.NGRP * len(units)) == 0, "w2 tile/unit mismatch"
        TPS = cfg.JT // (cfg.NGRP * len(units))
        flat = []
        for ui, (li, qb, items) in enumerate(units):
            for (j, kt, coff, N) in items:
                flat.append((ui, li, qb, j, kt, coff, N, j == 0,
                             j == len(items) - 1))
        cum = [0]
        for (_, _, items) in units:
            cum.append(cum[-1] + len(items))

        @blk.tensor
        def _(tn):
            def emit_scores(fi):
                ui, li, qb, j, kt, coff, N, first, last = flat[fi]
                if fi >= 2:
                    tn.wait_ge(self.sems["sE"], b_sE + 2 * (fi - 1))
                tn.matmul(ps["pp"][fi % 2][0:128, coff:coff + N],
                          t["kT"][0:64, li * T + kt * 128: li * T + (kt + 1) * 128],
                          t["qT"][0:64, li * T + qb * QBW + coff: li * T + qb * QBW + coff + N],
                          start=True, stop=True, tile_position=(0, 0))
                mmB = tn.matmul(ps["pswp"][fi % 2][0:128, coff:coff + N],
                                t["kT"][64:128, li * T + kt * 128: li * T + (kt + 1) * 128],
                                t["qT"][64:128, li * T + qb * QBW + coff: li * T + qb * QBW + coff + N],
                                start=True, stop=True, tile_position=(64, 0))
                self.inc(mmB, "tS", 1)

            emit_scores(0)
            for fi, (ui, li, qb, j, kt, coff, N, first, last) in enumerate(flat):
                hp = g * PPG + li
                if fi + 1 < len(flat):
                    emit_scores(fi + 1)
                tn.wait_ge(self.sems["sE"], b_sE + 2 * (fi + 1))
                if first and ui >= 1:
                    tn.wait_ge(self.sems["vNR"], b_NR + 2 * ui)
                vbase = kt * HEADS * VW
                tn.matmul(ps["pvA"][0:VW, coff:coff + N],
                          t["vext"][:, vbase + 2 * hp * VW: vbase + 2 * hp * VW + VW],
                          t["expA"][fi % 2][0:128, 0:N], start=first, stop=last)
                pv = tn.matmul(ps["pvB"][0:VW, coff:coff + N],
                               t["vext"][:, vbase + (2 * hp + 1) * VW: vbase + (2 * hp + 1) * VW + VW],
                               t["expB"][fi % 2][0:128, 0:N], start=first, stop=last)
                self.inc(pv, "tPV", 1)
                if last:
                    tn.wait_ge(self.sems["sRC"], b_RC + 2 * ui + 1)
                    bA = tn.matmul(ps["pnorm"][0:64, 0:QBW], t["e64"][:, 0:64],
                                   t["recA"][:, 0:QBW], start=True, stop=True)
                    self.inc(bA, "tnm", 1)
                    tn.wait_ge(self.sems["sRC"], b_RC + 2 * ui + 2)
                    tn.wait_ge(self.sems["sBC"], b_BC + 2 * ui + 1)
                    bB = tn.matmul(ps["pnorm"][0:64, 0:QBW], t["e64"][:, 0:64],
                                   t["recB"][:, 0:QBW], start=True, stop=True)
                    self.inc(bB, "tnm", 1)

        @blk.scalar
        def _(sc):
            for fi, (ui, li, qb, j, kt, coff, N, first, last) in enumerate(flat):
                hp = g * PPG + li
                sc.wait_ge(self.sems["tS"], b_tS + fi + 1)
                if fi >= 2:
                    sc.wait_ge(self.sems["tPV"], b_PV + fi - 1)
                eA = sc.activation(t["expA"][fi % 2][0:128, 0:N],
                                   ps["pp"][fi % 2][0:128, coff:coff + N], AF.Exp,
                                   bias=t["negB"][:, hp:hp + 1],
                                   scale=t["cpool"][:, 2:3])
                self.inc(eA, "sE", 1)
                eB = sc.activation(t["expB"][fi % 2][0:128, 0:N],
                                   ps["pswp"][fi % 2][0:128, coff:coff + N], AF.Exp,
                                   bias=t["negB"][:, hp:hp + 1],
                                   scale=t["cpool"][:, 2:3])
                self.inc(eB, "sE", 1)
                if last:
                    sc.wait_ge(self.sems["tPV"], b_PV + cum[ui + 1])
                    sc.activation(t["lnz"][0:1, 0:QBW],
                                  ps["pvA"][64:65, 0:QBW], AF.Ln)
                    rA = sc.activation(t["recA"][64:65, 0:QBW],
                                       t["lnz"][0:1, 0:QBW], AF.Exp,
                                       scale=t["cpool"][64:65, 3:4])
                    self.inc(rA, "sRC", 1)
                    sc.activation(t["lnz"][0:1, 0:QBW],
                                  ps["pvB"][64:65, 0:QBW], AF.Ln)
                    rB = sc.activation(t["recB"][64:65, 0:QBW],
                                       t["lnz"][0:1, 0:QBW], AF.Exp,
                                       scale=t["cpool"][64:65, 3:4])
                    self.inc(rB, "sRC", 1)
                    sc.wait_ge(self.sems["tnm"], b_nm + 2 * ui + 1)
                    cA = sc.activation(t["bcst"][0:64, 0:QBW],
                                       ps["pnorm"][0:64, 0:QBW], AF.Copy)
                    self.inc(cA, "sBC", 1)
                    sc.wait_ge(self.sems["tnm"], b_nm + 2 * ui + 2)
                    cB = sc.activation(t["bcstB"][0:64, 0:QBW],
                                       ps["pnorm"][0:64, 0:QBW], AF.Copy)
                    self.inc(cB, "sBC", 1)

        @blk.gpsimd
        def _(gp):
            for ui in range(len(units)):
                for r in range(TPS):
                    w = (g * len(units) + ui) * TPS + r
                    if w >= 1:
                        gp.wait_ge(self.sems["vW2"], 2 * w)
                    self.inc(gp.dma_start(t["w2st"][:, :],
                                          d["w2t"][w * 128:(w + 1) * 128, :]),
                             "dW", 16)

        @blk.vector
        def _(vec):
            def w2_chain(w):
                vec.wait_ge(self.sems["dW"], 16 * (w + 1))
                vec.tensor_scalar_mul(t["wA"][:, :], t["w2st"][:, :], -1.0)
                vec.tensor_max(t["wC"][:, :], t["wA"][:, :], t["w2st"][:, :])
                vec.tensor_single_scalar(t["wA"][:, :], t["wC"][:, :], 0.25,
                                         OP.is_ge)
                vec.tensor_scalar(t["wD"][:, :], t["wA"][:, :], 15.0 * 2.0 ** 43,
                                  2.0 ** 43, OP.mult, OP.add)
                for B in (2, 1):
                    vec.tensor_scalar_mul(t["wT"][:, :], t["wD"][:, :],
                                          2.0 ** (B - 49))
                    vec.tensor_tensor(t["wA"][:, :], t["wT"][:, :], t["wC"][:, :],
                                      OP.is_ge)
                    vec.tensor_scalar(t["wA"][:, :], t["wA"][:, :],
                                      -(2.0 ** B - 1.0), 2.0 ** B, OP.mult, OP.add)
                    vec.tensor_mul(t["wD"][:, :], t["wD"][:, :], t["wA"][:, :])
                w2qs = t["w2q"][:, w * E:(w + 1) * E]
                vec.tensor_scalar_mul(t["wT"][:, :], t["wD"][:, :], 1.5)
                vec.tensor_add(t["wC"][:, :], t["w2st"][:, :], t["wT"][:, :])
                vec.tensor_sub(t["wC"][:, :], t["wC"][:, :], t["wT"][:, :])
                a = vec.tensor_copy(w2qs, t["w2st"][:, :])
                self.inc(a, "vW2", 1)
                vec.tensor_sub(t["wA"][:, :], t["wC"][:, :], w2qs)
                vec.tensor_mul(t["wA"][:, :], t["wA"][:, :], t["mgad"][:, 0:E])
                q2 = vec.tensor_add(w2qs, w2qs, t["wA"][:, :])
                self.inc(q2, "vW2", 1)

            for ui, (li, qb, items) in enumerate(units):
                gsl = sl0 + ui
                if gsl >= 2:
                    vec.wait_ge(self.sems["dSp"], 32 * (gsl - 1))
                vec.wait_ge(self.sems["sBC"], b_BC + 2 * ui + 1)
                mA = vec.tensor_mul(t["slabA"][ui % 2][0:64, 0:QBW],
                                    ps["pvA"][0:64, 0:QBW], t["bcst"][0:64, 0:QBW])
                self.inc(mA, "vNR", 1)
                vec.wait_ge(self.sems["sBC"], b_BC + 2 * ui + 2)
                mB = vec.tensor_mul(t["slabB"][ui % 2][0:64, 0:QBW],
                                    ps["pvB"][0:64, 0:QBW], t["bcstB"][0:64, 0:QBW])
                self.inc(mB, "vNR", 1)
                for r in range(TPS):
                    w2_chain((g * len(units) + ui) * TPS + r)

        @blk.sync
        def _(sync):
            for ui, (li, qb, items) in enumerate(units):
                hp = g * PPG + li
                sync.wait_ge(self.sems["vNR"], b_NR + 2 * ui + 1)
                self.inc(sync.dma_start(
                    d["aspill"][hp * 128: hp * 128 + 64, qb * QBW:(qb + 1) * QBW],
                    t["slabA"][ui % 2][0:64, 0:QBW]), "dSp", 16)
                sync.wait_ge(self.sems["vNR"], b_NR + 2 * ui + 2)
                self.inc(sync.dma_start(
                    d["aspill"][hp * 128 + 64: (hp + 1) * 128, qb * QBW:(qb + 1) * QBW],
                    t["slabB"][ui % 2][0:64, 0:QBW]), "dSp", 16)


    def phase_ffn1(self, blk):
        cfg, t, d, ps = self.cfg, self.t, self.d, self.ps
        T, E, JT, KE = cfg.T, cfg.E, cfg.JT, cfg.KE
        NB, NBW = cfg.NB, cfg.NBW
        b_dG = self.cnt["dG"]
        b_F1, b_sH, b_vH = self.cnt["tF1"], self.cnt["sH"], self.cnt["vH"]

        @blk.gpsimd
        def _(gp):
            for jt in range(JT):
                if jt >= 2:
                    gp.wait_ge(self.sems["tF1"], b_F1 + jt - 1)
                self.inc(gp.dma_start(
                    t["w1sb"][jt % 2][:, :],
                    bass.AP(d["w1b"], jt * E * 128,
                            [[128, 128], [128 * 128, KE], [1, 128]])), "dG", 16)
        @blk.tensor
        def _(tn):
            for jt in range(JT):
                tn.wait_ge(self.sems["dG"], b_dG + 16 * (jt + 1))
                if jt >= 2:
                    tn.wait_ge(self.sems["sH"], b_sH + 2 * (jt - 2) + 1)
                for nb in range(NB):
                    for kc in range(KE):
                        mm = tn.matmul(
                            ps["ph4"][jt % 2][:, nb * NBW:(nb + 1) * NBW],
                            t["w1sb"][jt % 2][:, kc * 128:(kc + 1) * 128],
                            t["xnT"][:, kc * T + nb * NBW: kc * T + (nb + 1) * NBW],
                            start=(kc == 0), stop=(kc == KE - 1))
                self.inc(mm, "tF1", 1)

        @blk.scalar
        def _(sc):
            for jt in range(JT):
                sc.wait_ge(self.sems["tF1"], b_F1 + jt + 1)
                if jt >= 3:
                    sc.wait_ge(self.sems["vH"], b_vH + (jt - 3) + 1)
                h = sc.activation(t["qth"][jt % 3][:, 0:T], ps["ph4"][jt % 2][:, 0:T],
                                  AF.Relu, bias=t["b1sb"][:, jt:jt + 1])
                self.inc(h, "sH", 1)
                c8 = sc.activation(t["hq"][:, jt * T:(jt + 1) * T],
                                   t["qth"][jt % 3][:, 0:T], AF.Copy)
                self.inc(c8, "sH", 1)

        @blk.vector
        def _(vec):
            for jt in range(JT):
                vec.wait_ge(self.sems["sH"], b_sH + 2 * jt + 2)
                h = t["qth"][jt % 3][:, 0:T]
                gD, gA, gB, gC = (t[k][:, 0:T] for k in ("gD", "gA", "gB", "gC"))
                gT = t["gT"][:, 0:T]
                vec.tensor_single_scalar(gA, h, 4.0, OP.is_ge)
                vec.tensor_scalar(gD, gA, 255.0 * 2.0 ** 43, 2.0 ** 43,
                                  OP.mult, OP.add)
                for B in (4, 2, 1):
                    vec.tensor_scalar_mul(gT, gD, 2.0 ** (B - 49))
                    vec.tensor_tensor(gA, gT, h, OP.is_ge)
                    vec.tensor_scalar(gA, gA, -(2.0 ** B - 1.0), 2.0 ** B,
                                      OP.mult, OP.add)
                    vec.tensor_mul(gD, gD, gA)
                hqs = t["hq"][:, jt * T:(jt + 1) * T]
                vec.tensor_scalar_mul(gT, gD, 1.5)
                z = vec.tensor_add(gC, h, gT)
                self.inc(z, "vH", 1)
                vec.tensor_sub(gC, gC, gT)
                vec.tensor_sub(gA, gC, hqs)
                vec.tensor_mul(gA, gA, t["mgad"][:, 0:T])
                vec.tensor_add(hqs, hqs, gA)
    def phase_out(self, blk):
        cfg, t, d, ps = self.cfg, self.t, self.d, self.ps
        T, E, JT, KE, TT = cfg.T, cfg.E, cfg.JT, cfg.KE, cfg.TT
        NE, NEW = cfg.NE, cfg.NEW
        b_dS, b_dG = self.cnt["dS"], self.cnt["dG"]
        b_F2, b_WO = self.cnt["tF2"], self.cnt["tWO"]
        b_EP = self.cnt["vEP"]
        b_O2, b_C5 = self.cnt["vO2"], self.cnt["sC5"]
        b_dX2, b_dO = self.cnt["dX2"], self.cnt["dO"]
        b_dSp = self.cnt["dSp"]

        @blk.gpsimd
        def _(gp):
            for kc in range(KE):
                self.inc(gp.dma_start(t["wosb"][:, kc * E:(kc + 1) * E],
                                      d["wo"][kc * 128:(kc + 1) * 128, :]), "dG", 16)

        @blk.sync
        def _(sync):
            self.inc(sync.dma_start(t["b2rep"][:, :],
                                    bass.AP(d["b2r"], 0, [[0, 128], [1, E]])), "dS", 16)
            sync.wait_ge(self.sems["dSp"], b_dSp)

            def out_dma(k):
                sync.wait_ge(self.sems["vEP"], b_EP + 1 + 2 * k + 2)
                self.inc(sync.dma_start(d["out"][k * 128:(k + 1) * 128, :],
                                        t["obuf"][k % 2][:, :]), "dO", 16)

            for tt in range(TT):
                if tt >= 2:
                    sync.wait_ge(self.sems["tWO"], b_WO + tt - 1)
                self.inc(sync.dma_start(
                    t["atb"][tt % 2][:, :],
                    bass.AP(d["aspill"], tt * 128,
                            [[T, 128], [128 * T, KE], [1, 128]])), "dS", 16)
                if tt >= 2:
                    sync.wait_ge(self.sems["vEP"], b_EP + 1 + 2 * (tt - 2) + 2)
                self.inc(sync.dma_start(t["xbuf"][tt % 2][:, :],
                                        d["x"][tt * 128:(tt + 1) * 128, :]), "dX2", 16)
                if tt >= 2:
                    out_dma(tt - 2)
            out_dma(TT - 2)
            out_dma(TT - 1)
            sync.wait_ge(self.sems["dO"], b_dO + 16 * TT)

        @blk.tensor
        def _(tn):
            tn.wait_ge(self.sems["dG"], b_dG + 16 * KE)
            for tt in range(TT):
                if tt >= 2:
                    tn.wait_ge(self.sems["vEP"], b_EP + 1 + 2 * (tt - 2) + 1)
                for jc in range(JT):
                    for ne in range(NE):
                        mm = tn.matmul(
                            ps["psf"][tt % 2][:, ne * NEW:(ne + 1) * NEW],
                            t["hq"][:, jc * T + tt * 128: jc * T + (tt + 1) * 128],
                            t["w2q"][:, jc * E + ne * NEW: jc * E + (ne + 1) * NEW],
                            start=(jc == 0), stop=(jc == JT - 1))
                self.inc(mm, "tF2", 1)
                tn.wait_ge(self.sems["dS"], b_dS + 16 * (tt + 2))
                if tt >= 2:
                    tn.wait_ge(self.sems["vEP"], b_EP + 1 + 2 * (tt - 2) + 2)
                for kc in range(KE):
                    for ne in range(NE):
                        mm = tn.matmul(
                            ps["pso"][tt % 2][:, ne * NEW:(ne + 1) * NEW],
                            t["atb"][tt % 2][:, kc * 128:(kc + 1) * 128],
                            t["wosb"][:, kc * E + ne * NEW: kc * E + (ne + 1) * NEW],
                            start=(kc == 0), stop=(kc == KE - 1))
                self.inc(mm, "tWO", 1)

        @blk.scalar
        def _(sc):
            sc.wait_ge(self.sems["dS"], b_dS + 16)
            c0 = sc.activation(t["oB"][:, :], t["b2rep"][:, :], AF.Copy)
            self.inc(c0, "sC5", 1)
            for tt in range(TT):
                sc.wait_ge(self.sems["vO2"], b_O2 + tt + 1)
                c16 = sc.activation(t["oB"][:, :], t["o2b"][:, :], AF.Copy)
                self.inc(c16, "sC5", 1)

        @blk.vector
        def _(vec):
            def gq5(xn_f, out_ap, c5):
                vec.tensor_scalar_mul(t["oA"][:, :], xn_f, -1.0)
                vec.tensor_max(t["oav"][:, :], t["oA"][:, :], xn_f)
                vec.tensor_single_scalar(t["oA"][:, :], t["oav"][:, :], 2.0 ** -6,
                                         OP.is_ge)
                vec.tensor_scalar(t["oD"][:, :], t["oA"][:, :], 255.0 * 2.0 ** 28,
                                  2.0 ** 28, OP.mult, OP.add)
                for B in (8, 4, 2, 1):
                    vec.tensor_scalar_mul(t["oT"][:, :], t["oD"][:, :],
                                          2.0 ** (B - 42))
                    vec.tensor_tensor(t["oA"][:, :], t["oT"][:, :], t["oav"][:, :],
                                      OP.is_ge)
                    vec.tensor_scalar(t["oA"][:, :], t["oA"][:, :],
                                      -(2.0 ** B - 1.0), 2.0 ** B, OP.mult, OP.add)
                    vec.tensor_mul(t["oD"][:, :], t["oD"][:, :], t["oA"][:, :])
                vec.tensor_scalar_mul(t["oT"][:, :], t["oD"][:, :], 1.5)
                vec.tensor_add(t["oC"][:, :], xn_f, t["oT"][:, :])
                vec.tensor_sub(t["oC"][:, :], t["oC"][:, :], t["oT"][:, :])
                vec.wait_ge(self.sems["sC5"], c5)
                vec.tensor_sub(t["oS"][:, :], t["oC"][:, :], t["oB"][:, :])
                vec.tensor_mul(t["oS"][:, :], t["oS"][:, :], t["mgad"][:, 0:E])
                return vec.tensor_add(out_ap, t["oB"][:, :], t["oS"][:, :])

            vec.wait_ge(self.sems["dS"], b_dS + 16)
            qb2 = gq5(t["b2rep"][:, :], t["b2q"][:, :], b_C5 + 1)
            self.inc(qb2, "vEP", 1)
            for tt in range(TT):
                vec.wait_ge(self.sems["tF2"], b_F2 + tt + 1)
                oadd = vec.tensor_add(t["o2b"][:, :], ps["psf"][tt % 2][:, 0:E],
                                      t["b2q"][:, :])
                self.inc(oadd, "vO2", 1)
                q5 = gq5(t["o2b"][:, :], t["o2b"][:, :], b_C5 + tt + 2)
                self.inc(q5, "vEP", 1)
                vec.wait_ge(self.sems["tWO"], b_WO + tt + 1)
                vec.wait_ge(self.sems["dX2"], b_dX2 + 16 * (tt + 1))
                if tt >= 2:
                    vec.wait_ge(self.sems["dO"], b_dO + 16 * (tt - 1))
                vec.tensor_add(t["oS"][:, :], t["o2b"][:, :],
                               ps["pso"][tt % 2][:, 0:E])
                fin = vec.tensor_add(t["obuf"][tt % 2][:, :], t["oS"][:, :],
                                     t["xbuf"][tt % 2][:, :])
                self.inc(fin, "vEP", 1)


_BUILD_CACHE = {}


def get_nc(cfg: Cfg):
    key = (cfg.T, cfg.E, cfg.HEADS, cfg.LAT, cfg.HF, cfg.PAGE)
    if key not in _BUILD_CACHE:
        _BUILD_CACHE[key] = Builder(cfg).build()
    return _BUILD_CACHE[key]


def make_host_constants(cfg: Cfg):
    T = cfg.T
    bf = ml_dtypes.bfloat16
    inv = 1.0 / 10000.0 ** (np.arange(0, 64, 2, dtype=np.float32) / 64.0)
    ang = np.outer(np.arange(T, dtype=np.float32), inv)
    cos, sin = np.cos(ang), np.sin(ang)
    CT = np.zeros((128, T), np.float32)
    ST = np.zeros((128, T), np.float32)
    for p in range(128):
        j = p % 64
        jj = j % 32
        CT[p] = cos[:, jj]
        ST[p] = (-sin[:, jj]) if j < 32 else sin[:, jj]
    P = np.zeros((128, 128), np.float32)
    for i in range(128):
        jj = i + 32 if (i % 64) < 32 else i - 32
        P[jj, i] = 1.0
    E64 = np.zeros((128, 64), np.float32)
    E64[64, :] = 1.0
    return {
        "tc": np.ascontiguousarray(CT),
        "ts": np.ascontiguousarray(ST),
        "Pswap": np.ascontiguousarray(P.astype(bf)),
        "identin": np.ascontiguousarray(np.eye(128, dtype=np.float32).astype(bf)),
        "e64in": np.ascontiguousarray(E64.astype(bf)),
        "onesin": np.ascontiguousarray(np.ones((128, 128), np.float32).astype(bf)),
        "onescol": np.ascontiguousarray(np.ones((128, 1), np.float32).astype(bf)),
    }


def head_perm(E):
    perm = []
    for h in range(E // 64):
        base = h * 64
        perm += [base + 2 * i for i in range(32)] + [base + 2 * i + 1 for i in range(32)]
    return np.array(perm)


def prepare_inputs(cfg: Cfg, x, wq, w_down, wk_up, wv_up, wo, w1, b1, w2, b2):
    perm = head_perm(cfg.E)
    c = lambda a: np.ascontiguousarray(np.asarray(a), dtype=np.float32)
    wq, w_down, wk_up, wv_up, wo = (np.asarray(a, dtype=np.float32)
                                    for a in (wq, w_down, wk_up, wv_up, wo))
    w1, b1, w2, b2 = (np.asarray(a, dtype=np.float32) for a in (w1, b1, w2, b2))
    base = {
        "wq": c(wq[:, perm]),
        "wdn": c(w_down),
        "wku": c(wk_up[:, perm]),
        "wvu": c(wv_up),
        "wo": c(wo),
        "w1b": c(w1.T.reshape(cfg.E, cfg.JT, 128).transpose(1, 0, 2)),
        "w2t": c(w2.T),
        "b1r": c(b1.reshape(cfg.JT, 128).T),
        "b2r": c(b2.reshape(1, cfg.E)),
    }
    base.update(make_host_constants(cfg))
    in_maps = []
    for i in range(x.shape[0]):
        m = dict(base)
        m["x"] = c(x[i])
        in_maps.append(m)
    return in_maps


def kernel(L0_attn_x, L0_attn_wq, L0_attn_w_down, L0_attn_wk_up, L0_attn_wv_up,
           L0_attn_wo, L0_ffn_w1, L0_ffn_b1, L0_ffn_w2, L0_ffn_b2):
    cfg = Cfg()
    x = np.asarray(L0_attn_x, dtype=np.float32)
    in_maps = prepare_inputs(cfg, x, L0_attn_wq, L0_attn_w_down, L0_attn_wk_up,
                             L0_attn_wv_up, L0_attn_wo, L0_ffn_w1, L0_ffn_b1,
                             L0_ffn_w2, L0_ffn_b2)
    nc = get_nc(cfg)
    res = run_bass_kernel_spmd(nc, in_maps, core_ids=list(range(cfg.n_cores)))
    return np.stack([np.asarray(r["out"], dtype=np.float32) for r in res.results])


# version 20
# speedup vs baseline: 1.2382x; 1.0068x over previous
import contextlib
import math

import numpy as np
import ml_dtypes

import concourse.bass as bass
import concourse.mybir as mybir
from concourse.bass_utils import run_bass_kernel_spmd

F32 = mybir.dt.float32
BF16 = mybir.dt.bfloat16
F16 = mybir.dt.float16
F8E4 = mybir.dt.float8e4
I16 = mybir.dt.int16
I32 = mybir.dt.int32
AF = mybir.ActivationFunctionType
OP = mybir.AluOpType
AX = mybir.AxisListType
LN2 = math.log(2.0)
INVLN2 = 1.0 / LN2
LN2_HI = float(np.float32(LN2))
LN2_LO = float(np.float64(LN2) - np.float64(np.float32(LN2)))


class Cfg:
    def __init__(self, T=2048, E=1024, HEADS=16, LAT=512, HF=4096, PAGE=256,
                 n_cores=8, ngrp=None):
        self.T, self.E, self.HEADS, self.LAT, self.HF = T, E, HEADS, LAT, HF
        self.PAGE = PAGE
        self.n_cores = n_cores
        self.D = 64
        assert E == HEADS * 64
        self.PAIRS = HEADS // 2
        self.TT = T // 128
        self.KE = E // 128
        self.KL = LAT // 128
        self.JT = HF // 128
        self.QBW = min(512, T)
        self.NQB = T // self.QBW
        self.KTILES = T // 128
        self.NBW = min(512, T)
        self.NB = T // self.NBW
        self.NEW = min(512, E)
        self.NE = E // self.NEW
        self.NGRP = ngrp if ngrp is not None else min(4, self.PAIRS)
        assert self.PAIRS % self.NGRP == 0
        self.PPG = self.PAIRS // self.NGRP
        self.VW = 65

    def score_items(self, qb):
        out = []
        j = 0
        for kt in range(self.KTILES):
            qlo = (kt * 128 // self.PAGE) * self.PAGE
            qs = max(qb * self.QBW, qlo)
            qe = (qb + 1) * self.QBW
            if qs >= qe:
                break
            out.append((j, kt, qs - qb * self.QBW, qe - qs))
            j += 1
        return out


class Builder:
    def __init__(self, cfg: Cfg):
        self.cfg = cfg
        self.nc = bass.Bass("TRN2", target_bir_lowering=False,
                            detect_race_conditions=False,
                            dynamic_dma_scratch_size=16384)
        self.es = contextlib.ExitStack()
        self.cnt = {}

    def sem(self, name):
        h = self.es.enter_context(self.nc.semaphore(name))
        self.cnt[name] = 0
        return h

    def inc(self, instr, name, n=1):
        instr.then_inc(self.sems[name], n)
        self.cnt[name] += n
        return self.cnt[name]

    def vfence(self, vec, instr):
        c = self.inc(instr, "vrd", 1)
        vec.wait_ge(self.sems["vrd"], c)

    def build(self):
        cfg = self.cfg
        nc = self.nc
        T, E, LAT, HF = cfg.T, cfg.E, cfg.LAT, cfg.HF
        TT, KE, KL, JT = cfg.TT, cfg.KE, cfg.KL, cfg.JT
        QBW, VW = cfg.QBW, cfg.VW
        S = self.es.enter_context
        d = self.d = {}
        t = self.t = {}

        d["x"] = nc.dram_tensor("x", [T, E], F32, kind="ExternalInput")
        d["wq"] = nc.dram_tensor("wq", [E, E], F32, kind="ExternalInput")
        d["wdn"] = nc.dram_tensor("wdn", [E, LAT], F32, kind="ExternalInput")
        d["wku"] = nc.dram_tensor("wku", [LAT, E], F32, kind="ExternalInput")
        d["wvu"] = nc.dram_tensor("wvu", [LAT, E], F32, kind="ExternalInput")
        d["wo"] = nc.dram_tensor("wo", [E, E], F32, kind="ExternalInput")
        d["w1b"] = nc.dram_tensor("w1b", [JT, E, 128], F32, kind="ExternalInput")
        d["w2t"] = nc.dram_tensor("w2t", [HF, E], F32, kind="ExternalInput")
        d["b1r"] = nc.dram_tensor("b1r", [128, JT], F32, kind="ExternalInput")
        d["b2r"] = nc.dram_tensor("b2r", [1, E], F32, kind="ExternalInput")
        d["tc"] = nc.dram_tensor("tc", [128, T], F32, kind="ExternalInput")
        d["ts"] = nc.dram_tensor("ts", [128, T], F32, kind="ExternalInput")
        d["Pswap"] = nc.dram_tensor("Pswap", [128, 128], BF16, kind="ExternalInput")
        d["identin"] = nc.dram_tensor("identin", [128, 128], BF16, kind="ExternalInput")
        d["e64in"] = nc.dram_tensor("e64in", [128, 64], BF16, kind="ExternalInput")
        d["onesin"] = nc.dram_tensor("onesin", [128, 128], BF16, kind="ExternalInput")
        d["onescol"] = nc.dram_tensor("onescol", [128, 1], BF16, kind="ExternalInput")
        d["out"] = nc.dram_tensor("out", [T, E], F32, kind="ExternalOutput")
        spill_kind = "ExternalOutput" if getattr(self, "debug", False) else "Internal"
        d["aspill"] = nc.dram_tensor("aspill", [E, T], BF16, kind=spill_kind)
        if getattr(self, "debug", False):
            d["dbg_qT"] = nc.dram_tensor("dbg_qT", [128, cfg.PPG * T], F32, kind="ExternalOutput")
            d["dbg_kT"] = nc.dram_tensor("dbg_kT", [128, cfg.PPG * T], F32, kind="ExternalOutput")
            d["dbg_ones"] = nc.dram_tensor("dbg_ones", [128, 128], F32, kind="ExternalOutput")
            d["dbg_negB"] = nc.dram_tensor("dbg_negB", [128, cfg.PAIRS], F32, kind="ExternalOutput")
            d["dbg_b2q"] = nc.dram_tensor("dbg_b2q", [128, E], F32, kind="ExternalOutput")
            d["dbg_hq"] = nc.dram_tensor("dbg_hq", [128, T], F32, kind="ExternalOutput")
            d["dbg_w2q"] = nc.dram_tensor("dbg_w2q", [128, E], F32, kind="ExternalOutput")
            d["dbg_xnT"] = nc.dram_tensor("dbg_xnT", [128, T], F32, kind="ExternalOutput")


        L = dict(side="left")
        t["cpool"] = S(nc.sbuf_tensor("cpool", [128, 12], F32, **L))
        t["ones_c"] = S(nc.sbuf_tensor("ones_c", [128, 1], BF16, **L))
        t["onesf"] = S(nc.sbuf_tensor("onesf", [128, 128], BF16, **L))
        t["e64"] = S(nc.sbuf_tensor("e64s", [128, 64], BF16, **L))
        t["Psw"] = S(nc.sbuf_tensor("Psws", [128, 128], BF16, **L))
        t["ident"] = S(nc.sbuf_tensor("idents", [128, 128], BF16, **L))
        t["negB"] = S(nc.sbuf_tensor("negB", [128, cfg.PAIRS], F32, **L))
        t["b1sb"] = S(nc.sbuf_tensor("b1sb", [128, JT], F32, **L))
        t["stats"] = S(nc.sbuf_tensor("stats", [128, 8], F32, **L))
        t["s11"] = S(nc.sbuf_tensor("s11", [1, 8], F32, **L))
        t["xbuf"] = [S(nc.sbuf_tensor(f"xbuf{i}", [128, E], F32, **L)) for i in range(2)]
        t["w2q"] = S(nc.sbuf_tensor("w2q", [128, JT * E], F8E4, **L))

        R = dict(side="right")
        t["xnT"] = S(nc.sbuf_tensor("xnT", [128, KE * T], BF16, **R))

        self.sems = {n: self.sem(n) for n in (
            "dS", "dG", "dO", "dSp", "dX2",
            "vxn", "vvar", "vev", "ttr", "sln",
            "tpj", "vpe", "tsw", "vrp", "tnm", "vnm",
            "tS", "sE", "tPV", "vNR", "sRC", "sBC",
            "vW2", "dW",
            "tF1", "sH", "vH",
            "tF2", "tWO", "vEP", "sEP", "vrd", "vO2", "sC5",
        )}

        t["mgad"] = S(nc.sbuf_tensor("mgad", [128, T], BF16, **R))
        t["w2st"] = S(nc.sbuf_tensor("w2st", [128, E], F32, **R))
        t["wD"] = S(nc.sbuf_tensor("wD", [128, E], BF16, **R))
        t["wA"] = S(nc.sbuf_tensor("wA", [128, E], BF16, **R))
        t["wC"] = S(nc.sbuf_tensor("wC", [128, E], BF16, **R))
        t["wT"] = S(nc.sbuf_tensor("wT", [128, E], BF16, **R))

        sc1 = contextlib.ExitStack()
        t["xcb"] = [sc1.enter_context(nc.sbuf_tensor(f"xcb{i}", [128, E], F32, **R))
                    for i in range(2)]
        t["xc2b"] = [sc1.enter_context(nc.sbuf_tensor(f"xc2b{i}", [128, E], F32, **R))
                     for i in range(2)]
        t["xnb"] = [sc1.enter_context(nc.sbuf_tensor(f"xnb{i}", [128, E], BF16, **R))
                    for i in range(2)]
        ps = self.ps = {}
        ps["ptr"] = [sc1.enter_context(nc.psum_tensor(f"ptr{i}", [128, 128], BF16))
                     for i in range(2)]
        with nc.Block() as blk:
            self.phase1(blk)
        sc1.close()

        sc2 = contextlib.ExitStack()
        E2 = sc2.enter_context
        t["tc"] = E2(nc.sbuf_tensor("tcs", [128, T], F32, **R))
        t["ts"] = E2(nc.sbuf_tensor("tss", [128, T], F32, **R))
        t["latT"] = E2(nc.sbuf_tensor("latT", [128, KL * T], BF16, **R))
        t["vext"] = E2(nc.sbuf_tensor("vext", [128, TT * cfg.HEADS * VW], BF16, **R))
        t["qT"] = E2(nc.sbuf_tensor("qT", [128, cfg.PPG * T], BF16, **R))
        t["kT"] = E2(nc.sbuf_tensor("kT", [128, cfg.PPG * T], BF16, **R))
        t["wbuf"] = E2(nc.sbuf_tensor("wbuf", [128, max(KE * LAT, KL * E, (KE + KL) * cfg.PPG * 128)], BF16, **R))
        t["qraw"] = [E2(nc.sbuf_tensor(f"qraw{i}", [128, T], BF16, **R)) for i in range(2)]
        t["ttmp"] = E2(nc.sbuf_tensor("ttmp", [128, cfg.NBW], BF16, **R))
        t["ttmp2"] = E2(nc.sbuf_tensor("ttmp2", [128, cfg.NBW], BF16, **R))
        t["expA"] = [E2(nc.sbuf_tensor(f"expA{i}", [128, QBW], BF16, **R)) for i in range(2)]
        t["expB"] = [E2(nc.sbuf_tensor(f"expB{i}", [128, QBW], BF16, **R)) for i in range(2)]
        t["recA"] = E2(nc.sbuf_tensor("recA", [128, QBW], BF16, **R))
        t["lnz"] = E2(nc.sbuf_tensor("lnz", [1, QBW], F32, **R))
        t["recB"] = E2(nc.sbuf_tensor("recB", [128, QBW], BF16, **R))
        t["bcst"] = E2(nc.sbuf_tensor("bcst", [64, QBW], BF16, **R))
        t["bcstB"] = E2(nc.sbuf_tensor("bcstB", [64, QBW], BF16, **R))
        t["slabA"] = [E2(nc.sbuf_tensor(f"slabA{i}", [64, QBW], BF16, **R)) for i in range(2)]
        t["slabB"] = [E2(nc.sbuf_tensor(f"slabB{i}", [64, QBW], BF16, **R)) for i in range(2)]
        ps["pp"] = [E2(nc.psum_tensor(f"pp{i}", [128, 512], F32)) for i in range(2)]
        ps["pswp"] = [E2(nc.psum_tensor(f"pswp{i}", [128, 512], F32)) for i in range(2)]
        ps["pnorm"] = E2(nc.psum_tensor("pnorm", [128, 512], F32))
        ps["pvA"] = E2(nc.psum_tensor("pvA", [128, 512], F32))
        ps["pvB"] = E2(nc.psum_tensor("pvB", [128, 512], F32))
        with nc.Block() as blk:
            self.phase_proj_v(blk)
        for g in range(cfg.NGRP):
            with nc.Block() as blk:
                self.phase_proj_qk(blk, g)
            with nc.Block() as blk:
                self.phase_attn(blk, g)
        if getattr(self, "debug", False):
            with nc.Block() as blk:
                @blk.gpsimd
                def _(gp):
                    dbga = self.sem("dbga")
                    gp.dma_start(d["dbg_qT"][:, :], t["qT"][:, :]).then_inc(dbga, 16)
                    gp.dma_start(d["dbg_kT"][:, :], t["kT"][:, :]).then_inc(dbga, 16)
                    gp.dma_start(d["dbg_ones"][:, :], t["onesf"][:, :]).then_inc(dbga, 16)
                    gp.wait_ge(dbga, 48)
        sc2.close()

        t["hq"] = S(nc.sbuf_tensor("hq", [128, JT * T], F8E4, **R))
        sc4 = contextlib.ExitStack()
        E4 = sc4.enter_context
        t["w1sb"] = [E4(nc.sbuf_tensor(f"w1sb{i}", [128, KE * 128], BF16, **R))
                     for i in range(2)]
        t["qth"] = [E4(nc.sbuf_tensor(f"qth{i}", [128, T], BF16, **R)) for i in range(3)]
        t["gD"] = E4(nc.sbuf_tensor("gD", [128, T], BF16, **R))
        t["gA"] = E4(nc.sbuf_tensor("gA", [128, T], BF16, **R))
        t["gB"] = E4(nc.sbuf_tensor("gB", [128, T], BF16, **R))
        t["gC"] = E4(nc.sbuf_tensor("gC", [128, T], BF16, **R))
        t["gT"] = E4(nc.sbuf_tensor("gT", [128, T], BF16, **R))
        ps["ph4"] = [E4(nc.psum_tensor(f"ph4_{i}", [128, min(2048, T)], F32))
                     for i in range(2)]
        with nc.Block() as blk:
            self.phase_ffn1(blk)
        sc4.close()

        sc5 = contextlib.ExitStack()
        E5 = sc5.enter_context
        t["wosb"] = E5(nc.sbuf_tensor("wosb", [128, KE * E], BF16, **R))
        t["atb"] = [E5(nc.sbuf_tensor(f"atb{i}", [128, E], BF16, **R)) for i in range(2)]
        t["o2b"] = E5(nc.sbuf_tensor("o2b", [128, E], F32, **R))
        t["oav"] = E5(nc.sbuf_tensor("oav", [128, E], BF16, **R))
        t["oD"] = E5(nc.sbuf_tensor("oD", [128, E], BF16, **R))
        t["oA"] = E5(nc.sbuf_tensor("oA", [128, E], BF16, **R))
        t["oB"] = E5(nc.sbuf_tensor("oB", [128, E], F16, **R))
        t["oC"] = E5(nc.sbuf_tensor("oC", [128, E], BF16, **R))
        t["oT"] = E5(nc.sbuf_tensor("oT", [128, E], BF16, **R))
        t["oS"] = E5(nc.sbuf_tensor("oS", [128, E], F32, **R))
        t["b2q"] = E5(nc.sbuf_tensor("b2q", [128, E], F32, **R))
        t["obuf"] = [E5(nc.sbuf_tensor(f"obuf{i}", [128, E], F32, **R)) for i in range(2)]
        t["b2rep"] = t["obuf"][0]
        ps["psf"] = [E5(nc.psum_tensor(f"psf{i}", [128, E], F32)) for i in range(2)]
        ps["pso"] = [E5(nc.psum_tensor(f"pso{i}", [128, E], F32)) for i in range(2)]
        with nc.Block() as blk:
            self.phase_out(blk)
        sc5.close()

        if getattr(self, "debug", False):
            t["dbgf"] = S(nc.sbuf_tensor("dbgf", [128, T], F32, **R))
            with nc.Block() as blk:
                @blk.vector
                def _(vec):
                    vec.tensor_copy(t["dbgf"][:, 0:T], t["hq"][:, 0:T])
                @blk.gpsimd
                def _(gp):
                    dbg = self.sem("dbgd")
                    gp.wait_ge(self.sems["vEP"], self.cnt["vEP"])
                    gp.dma_start(d["dbg_negB"][:, :], t["negB"][:, :]).then_inc(dbg, 16)
                    gp.dma_start(d["dbg_b2q"][:, :], t["b2q"][:, :]).then_inc(dbg, 16)
                    gp.dma_start(d["dbg_w2q"][:, :], t["w2q"][:, 0:E]).then_inc(dbg, 16)
                    gp.dma_start(d["dbg_xnT"][:, :], t["xnT"][:, 0:T]).then_inc(dbg, 16)
                    gp.wait_ge(dbg, 64)
            with nc.Block() as blk:
                @blk.gpsimd
                def _(gp):
                    dbg2 = self.sem("dbgd2")
                    gp.dma_start(d["dbg_hq"][:, :], t["dbgf"][:, 0:T]).then_inc(dbg2, 16)
                    gp.wait_ge(dbg2, 16)

        self.es.close()
        return nc

    def phase1(self, blk):
        cfg, t, d, ps = self.cfg, self.t, self.d, self.ps
        T, E, TT, KE = cfg.T, cfg.E, cfg.TT, cfg.KE
        NCONST = 6

        @blk.sync
        def _(sync):
            self.inc(sync.dma_start(t["Psw"][:, :], d["Pswap"][:, :]), "dS", 16)
            self.inc(sync.dma_start(t["ident"][:, :], d["identin"][:, :]), "dS", 16)
            self.inc(sync.dma_start(t["e64"][:, :], d["e64in"][:, :]), "dS", 16)
            self.inc(sync.dma_start(t["ones_c"][:, 0:1], d["onescol"][:, :]), "dS", 16)
            self.inc(sync.dma_start(t["onesf"][:, :], d["onesin"][:, :]), "dS", 16)
            self.inc(sync.dma_start(t["b1sb"][:, :], d["b1r"][:, :]), "dS", 16)
            for tt in range(TT):
                if tt >= 2:
                    sync.wait_ge(self.sems["vxn"], tt - 1)
                self.inc(sync.dma_start(t["xbuf"][tt % 2][:, :],
                                        d["x"][tt * 128:(tt + 1) * 128, :]), "dS", 16)

        @blk.vector
        def _(vec):
            W = TT * cfg.HEADS * cfg.VW
            vec.memset(t["cpool"][:, 0:1], 1.0 / E)
            vec.memset(t["cpool"][:, 1:2], 1e-5)
            vec.memset(t["cpool"][:, 2:3], 0.125)
            vec.memset(t["cpool"][:, 3:4], -1.0)
            self.vfence(vec, vec.memset(t["mgad"][:, 0:T], 1024.0))
            self.vfence(vec, vec.tensor_scalar(t["mgad"][:, 0:T], t["mgad"][:, 0:T],
                                               1.0, 2.0 ** 52, OP.mult, OP.add))
            self.vfence(vec, vec.tensor_scalar(t["mgad"][:, 0:T], t["mgad"][:, 0:T],
                                               -(2.0 ** 52), 2.0 ** -10,
                                               OP.add, OP.mult))
            def ln_stats(tt):
                c = 4 * (tt % 2)
                vec.wait_ge(self.sems["dS"], 16 * (NCONST + tt + 1))
                xb = t["xbuf"][tt % 2]
                self.vfence(vec, vec.reduce_sum(t["stats"][:, c:c + 1], xb[:, :],
                                                axis=AX.X))
                self.vfence(vec, vec.tensor_scalar_mul(
                    t["stats"][:, c + 1:c + 2], t["stats"][:, c:c + 1], 1.0 / E))
                vec.tensor_scalar(t["xcb"][tt % 2][:, :], xb[:, :],
                                  t["stats"][:, c + 1:c + 2], None, OP.subtract)
                vec.tensor_mul(t["xc2b"][tt % 2][:, :], t["xcb"][tt % 2][:, :],
                               t["xcb"][tt % 2][:, :])
                vv = vec.reduce_sum(t["stats"][:, c + 2:c + 3],
                                    t["xc2b"][tt % 2][:, :], axis=AX.X)
                self.inc(vv, "vvar", 1)

            ln_stats(0)
            for tt in range(TT):
                if tt + 1 < TT:
                    ln_stats(tt + 1)
                c = 4 * (tt % 2)
                vec.wait_ge(self.sems["sln"], tt + 1)
                self.vfence(vec, vec.reciprocal(t["stats"][:, c + 1:c + 2],
                                                t["stats"][:, c + 3:c + 4]))
                xn = vec.tensor_scalar(t["xnb"][tt % 2][:, :],
                                       t["xcb"][tt % 2][:, :],
                                       t["stats"][:, c + 1:c + 2], None, OP.mult)
                self.inc(xn, "vxn", 1)
                for cc in range(KE):
                    g = tt * KE + cc
                    vec.wait_ge(self.sems["ttr"], g + 1)
                    ev = vec.tensor_copy(
                        t["xnT"][:, cc * T + tt * 128: cc * T + (tt + 1) * 128],
                        ps["ptr"][g % 2][:, 0:128])
                    self.inc(ev, "vev", 1)

        @blk.scalar
        def _(sc):
            for tt in range(TT):
                sc.wait_ge(self.sems["vvar"], tt + 1)
                c = 4 * (tt % 2)
                a = sc.activation(t["stats"][:, c + 3:c + 4],
                                  t["stats"][:, c + 2:c + 3], AF.Sqrt,
                                  bias=t["cpool"][:, 1:2], scale=t["cpool"][:, 0:1])
                self.inc(a, "sln", 1)

        @blk.tensor
        def _(tn):
            tn.wait_ge(self.sems["dS"], 16 * 2)
            for tt in range(TT):
                tn.wait_ge(self.sems["vxn"], tt + 1)
                for cc in range(KE):
                    g = tt * KE + cc
                    if g >= 2:
                        tn.wait_ge(self.sems["vev"], g - 1)
                    tr = tn.transpose(ps["ptr"][g % 2][:, 0:128],
                                      t["xnb"][tt % 2][:, cc * 128:(cc + 1) * 128],
                                      t["ident"][:, :])
                    self.inc(tr, "ttr", 1)

    def phase_proj_v(self, blk):
        cfg, t, d, ps = self.cfg, self.t, self.d, self.ps
        T, E, LAT = cfg.T, cfg.E, cfg.LAT
        TT, KE, KL = cfg.TT, cfg.KE, cfg.KL
        NB, NBW, NE, NEW = cfg.NB, cfg.NBW, cfg.NE, cfg.NEW
        VW, HEADS = cfg.VW, cfg.HEADS
        lat_items = [(lt, nb) for lt in range(KL) for nb in range(NB)]
        v_items = [(tt, fc) for tt in range(TT) for fc in range(NE)]
        b_pj, b_pe = self.cnt["tpj"], self.cnt["vpe"]
        b_dG = self.cnt["dG"]
        b_dS = self.cnt["dS"]

        @blk.sync
        def _(sync):
            self.inc(sync.dma_start(t["tc"][:, :], d["tc"][:, :]), "dS", 16)
            self.inc(sync.dma_start(t["ts"][:, :], d["ts"][:, :]), "dS", 16)
            self.tabs_dS = self.cnt["dS"]

        @blk.gpsimd
        def _(gp):
            for kc in range(KE):
                self.inc(gp.dma_start(t["wbuf"][:, kc * LAT:(kc + 1) * LAT],
                                      d["wdn"][kc * 128:(kc + 1) * 128, :]), "dG", 16)
            gp.wait_ge(self.sems["tpj"], b_pj + len(lat_items))
            for lc in range(KL):
                self.inc(gp.dma_start(t["wbuf"][:, lc * E:(lc + 1) * E],
                                      d["wvu"][lc * 128:(lc + 1) * 128, :]), "dG", 16)

        @blk.tensor
        def _(tn):
            tn.wait_ge(self.sems["dG"], b_dG + 16 * KE)
            for i, (lt, nb) in enumerate(lat_items):
                if i >= 2:
                    tn.wait_ge(self.sems["vpe"], b_pe + i - 1)
                for kc in range(KE):
                    mm = tn.matmul(ps["pp"][i % 2][:, 0:NBW],
                                   t["wbuf"][:, kc * LAT + lt * 128: kc * LAT + (lt + 1) * 128],
                                   t["xnT"][:, kc * T + nb * NBW: kc * T + (nb + 1) * NBW],
                                   start=(kc == 0), stop=(kc == KE - 1))
                self.inc(mm, "tpj", 1)
            tn.wait_ge(self.sems["dG"], b_dG + 16 * (KE + KL))
            for i2, (tt, fc) in enumerate(v_items):
                i = len(lat_items) + i2
                if i >= 2:
                    tn.wait_ge(self.sems["vpe"], b_pe + i - 1)
                for lc in range(KL):
                    mm = tn.matmul(ps["pp"][i % 2][:, 0:NEW],
                                   t["latT"][:, lc * T + tt * 128: lc * T + (tt + 1) * 128],
                                   t["wbuf"][:, lc * E + fc * NEW: lc * E + (fc + 1) * NEW],
                                   start=(lc == 0), stop=(lc == KL - 1))
                self.inc(mm, "tpj", 1)

        @blk.vector
        def _(vec):
            W = TT * HEADS * VW
            for tt in range(TT):
                vec.memset(bass.AP(t["vext"], tt * HEADS * VW + 64,
                                   [[W, 128], [VW, HEADS]]), 1.0)
            vec.memset(t["recA"][:, :], 0.0)
            vec.memset(t["recB"][:, :], 0.0)
            for i, (lt, nb) in enumerate(lat_items):
                vec.wait_ge(self.sems["tpj"], b_pj + i + 1)
                ev = vec.tensor_copy(
                    t["latT"][:, lt * T + nb * NBW: lt * T + (nb + 1) * NBW],
                    ps["pp"][i % 2][:, 0:NBW])
                self.inc(ev, "vpe", 1)
            for i2, (tt, fc) in enumerate(v_items):
                i = len(lat_items) + i2
                vec.wait_ge(self.sems["tpj"], b_pj + i + 1)
                nheads = NEW // 64
                dst = bass.AP(t["vext"], tt * HEADS * VW + fc * nheads * VW,
                              [[W, 128], [VW, nheads], [1, 64]])
                src = bass.AP(ps["pp"][i % 2], 0, [[512, 128], [64, nheads], [1, 64]])
                ev = vec.tensor_copy(dst, src)
                self.inc(ev, "vpe", 1)

    def phase_proj_qk(self, blk, g):
        cfg, t, d, ps = self.cfg, self.t, self.d, self.ps
        T, E, LAT = cfg.T, cfg.E, cfg.LAT
        KE, KL, PPG = cfg.KE, cfg.KL, cfg.PPG
        NB, NBW = cfg.NB, cfg.NBW
        WQC = PPG * 128
        b_dG = self.cnt["dG"]
        b_pj, b_pe = self.cnt["tpj"], self.cnt["vpe"]
        b_sw, b_rp = self.cnt["tsw"], self.cnt["vrp"]
        b_nm, b_vn = self.cnt["tnm"], self.cnt["vnm"]

        proj_items = [(li, w, nb) for li in range(PPG) for w in ("q", "k")
                      for nb in range(NB)]
        VPP = 2 * (1 + NB) + 1
        TPP = 2 * NB

        @blk.gpsimd
        def _(gp):
            gp.wait_ge(self.sems["tpj"], b_pj)
            for kc in range(KE):
                self.inc(gp.dma_start(
                    t["wbuf"][:, kc * WQC:(kc + 1) * WQC],
                    d["wq"][kc * 128:(kc + 1) * 128, g * WQC:(g + 1) * WQC]), "dG", 16)
            for lc in range(KL):
                self.inc(gp.dma_start(
                    t["wbuf"][:, KE * WQC + lc * WQC: KE * WQC + (lc + 1) * WQC],
                    d["wku"][lc * 128:(lc + 1) * 128, g * WQC:(g + 1) * WQC]), "dG", 16)

        @blk.tensor
        def _(tn):
            for idx, (li, w, nb) in enumerate(proj_items):
                if w == "q":
                    tn.wait_ge(self.sems["dG"], b_dG + 16 * KE)
                    nk, wof, src = KE, 0, "xnT"
                else:
                    tn.wait_ge(self.sems["dG"], b_dG + 16 * (KE + KL))
                    nk, wof, src = KL, KE * WQC, "latT"
                if idx >= 2:
                    tn.wait_ge(self.sems["vpe"], b_pe + idx - 1)
                for kc in range(nk):
                    mm = tn.matmul(
                        ps["pp"][idx % 2][:, 0:NBW],
                        t["wbuf"][:, wof + kc * WQC + li * 128: wof + kc * WQC + (li + 1) * 128],
                        t[src][:, kc * T + nb * NBW: kc * T + (nb + 1) * NBW],
                        start=(kc == 0), stop=(kc == nk - 1))
                self.inc(mm, "tpj", 1)
                tn.wait_ge(self.sems["vpe"], b_pe + idx + 1)
                if idx >= 2:
                    tn.wait_ge(self.sems["vrp"], b_rp + idx - 1)
                sw = tn.matmul(ps["pswp"][idx % 2][:, 0:NBW], t["Psw"][:, :],
                               t["qraw"][li % 2][:, nb * NBW:(nb + 1) * NBW],
                               start=True, stop=True)
                self.inc(sw, "tsw", 1)
            for li in range(PPG):
                vb = b_vn + li * VPP
                for side in range(2):
                    for c in range(NB):
                        tn.wait_ge(self.sems["vnm"], vb + side * (1 + NB) + 1 + c)
                        mm = tn.matmul(ps["pnorm"][:, 0:NBW], t["onesf"][:, :],
                                       t["qraw"][0][:, c * NBW:(c + 1) * NBW],
                                       start=True, stop=True)
                        self.inc(mm, "tnm", 1)

        @blk.vector
        def _(vec):
            vec.wait_ge(self.sems["dS"], self.tabs_dS)
            for idx, (li, w, nb) in enumerate(proj_items):
                vec.wait_ge(self.sems["tpj"], b_pj + idx + 1)
                ev = vec.tensor_copy(t["qraw"][li % 2][:, nb * NBW:(nb + 1) * NBW],
                                     ps["pp"][idx % 2][:, 0:NBW])
                self.inc(ev, "vpe", 1)
                vec.wait_ge(self.sems["tsw"], b_sw + idx + 1)
                dstT = t["qT"] if w == "q" else t["kT"]
                cols = slice(nb * NBW, (nb + 1) * NBW)
                vec.tensor_mul(t["ttmp"][:, 0:NBW], t["qraw"][li % 2][:, cols],
                               t["tc"][:, cols])
                vec.tensor_mul(t["ttmp2"][:, 0:NBW], ps["pswp"][idx % 2][:, 0:NBW],
                               t["ts"][:, cols])
                rp = vec.tensor_add(dstT[:, li * T + nb * NBW: li * T + (nb + 1) * NBW],
                                    t["ttmp"][:, 0:NBW], t["ttmp2"][:, 0:NBW])
                self.inc(rp, "vrp", 1)
            for li in range(PPG):
                hp = g * PPG + li
                tb = b_nm + li * TPP
                for side, srcT in ((0, t["qT"]), (1, t["kT"])):
                    vec.wait_ge(self.sems["tnm"], tb + side * NB)
                    sq = vec.tensor_mul(t["qraw"][0][:, 0:T],
                                        srcT[:, li * T:(li + 1) * T],
                                        srcT[:, li * T:(li + 1) * T])
                    self.inc(sq, "vnm", 1)
                    for c in range(NB):
                        vec.wait_ge(self.sems["tnm"], tb + side * NB + c + 1)
                        self.vfence(vec, vec.reduce_max(t["stats"][:, 5:6],
                                                        ps["pnorm"][:, 0:NBW],
                                                        axis=AX.X))
                        if c == 0:
                            mx = vec.tensor_copy(t["stats"][:, 6 + side:7 + side],
                                                 t["stats"][:, 5:6])
                        else:
                            mx = vec.tensor_max(t["stats"][:, 6 + side:7 + side],
                                                t["stats"][:, 6 + side:7 + side],
                                                t["stats"][:, 5:6])
                        self.vfence(vec, mx)
                        self.inc(vec.tensor_copy(t["stats"][:, 4:5],
                                                 t["stats"][:, 5:6]), "vnm", 1)
                self.vfence(vec, vec.tensor_add(t["stats"][:, 5:6], t["stats"][:, 6:7],
                                                t["stats"][:, 7:8]))
                nb_ = vec.tensor_scalar_mul(t["negB"][:, hp:hp + 1],
                                            t["stats"][:, 5:6], -0.0625)
                self.inc(nb_, "vnm", 1)

    def phase_attn(self, blk, g):
        cfg, t, d, ps = self.cfg, self.t, self.d, self.ps
        T, QBW, VW, HEADS = cfg.T, cfg.QBW, cfg.VW, cfg.HEADS
        PPG, NQB, JT, E = cfg.PPG, cfg.NQB, cfg.JT, cfg.E
        b_tS, b_sE = self.cnt["tS"], self.cnt["sE"]
        b_PV, b_NR = self.cnt["tPV"], self.cnt["vNR"]
        b_RC, b_BC = self.cnt["sRC"], self.cnt["sBC"]
        b_nm = self.cnt["tnm"]
        b_dSp = self.cnt["dSp"]
        sl0 = b_dSp // 32

        units = []
        for li in range(PPG):
            for qb in range(NQB):
                units.append((li, qb, cfg.score_items(qb)))
        assert cfg.JT % (cfg.NGRP * len(units)) == 0, "w2 tile/unit mismatch"
        TPS = cfg.JT // (cfg.NGRP * len(units))
        flat = []
        for ui, (li, qb, items) in enumerate(units):
            for (j, kt, coff, N) in items:
                flat.append((ui, li, qb, j, kt, coff, N, j == 0,
                             j == len(items) - 1))
        cum = [0]
        for (_, _, items) in units:
            cum.append(cum[-1] + len(items))

        @blk.tensor
        def _(tn):
            def emit_scores(fi):
                ui, li, qb, j, kt, coff, N, first, last = flat[fi]
                if fi >= 2:
                    tn.wait_ge(self.sems["sE"], b_sE + 2 * (fi - 1))
                tn.matmul(ps["pp"][fi % 2][0:128, coff:coff + N],
                          t["kT"][0:64, li * T + kt * 128: li * T + (kt + 1) * 128],
                          t["qT"][0:64, li * T + qb * QBW + coff: li * T + qb * QBW + coff + N],
                          start=True, stop=True, tile_position=(0, 0))
                mmB = tn.matmul(ps["pswp"][fi % 2][0:128, coff:coff + N],
                                t["kT"][64:128, li * T + kt * 128: li * T + (kt + 1) * 128],
                                t["qT"][64:128, li * T + qb * QBW + coff: li * T + qb * QBW + coff + N],
                                start=True, stop=True, tile_position=(64, 0))
                self.inc(mmB, "tS", 1)

            emit_scores(0)
            for fi, (ui, li, qb, j, kt, coff, N, first, last) in enumerate(flat):
                hp = g * PPG + li
                if fi + 1 < len(flat):
                    emit_scores(fi + 1)
                tn.wait_ge(self.sems["sE"], b_sE + 2 * (fi + 1))
                if first and ui >= 1:
                    tn.wait_ge(self.sems["vNR"], b_NR + 2 * ui)
                vbase = kt * HEADS * VW
                tn.matmul(ps["pvA"][0:VW, coff:coff + N],
                          t["vext"][:, vbase + 2 * hp * VW: vbase + 2 * hp * VW + VW],
                          t["expA"][fi % 2][0:128, 0:N], start=first, stop=last)
                pv = tn.matmul(ps["pvB"][0:VW, coff:coff + N],
                               t["vext"][:, vbase + (2 * hp + 1) * VW: vbase + (2 * hp + 1) * VW + VW],
                               t["expB"][fi % 2][0:128, 0:N], start=first, stop=last)
                self.inc(pv, "tPV", 1)
                if last:
                    tn.wait_ge(self.sems["sRC"], b_RC + 2 * ui + 1)
                    bA = tn.matmul(ps["pnorm"][0:64, 0:QBW], t["e64"][:, 0:64],
                                   t["recA"][:, 0:QBW], start=True, stop=True)
                    self.inc(bA, "tnm", 1)
                    tn.wait_ge(self.sems["sRC"], b_RC + 2 * ui + 2)
                    tn.wait_ge(self.sems["sBC"], b_BC + 2 * ui + 1)
                    bB = tn.matmul(ps["pnorm"][0:64, 0:QBW], t["e64"][:, 0:64],
                                   t["recB"][:, 0:QBW], start=True, stop=True)
                    self.inc(bB, "tnm", 1)

        @blk.scalar
        def _(sc):
            for fi, (ui, li, qb, j, kt, coff, N, first, last) in enumerate(flat):
                hp = g * PPG + li
                sc.wait_ge(self.sems["tS"], b_tS + fi + 1)
                if fi >= 2:
                    sc.wait_ge(self.sems["tPV"], b_PV + fi - 1)
                eA = sc.activation(t["expA"][fi % 2][0:128, 0:N],
                                   ps["pp"][fi % 2][0:128, coff:coff + N], AF.Exp,
                                   bias=t["negB"][:, hp:hp + 1],
                                   scale=t["cpool"][:, 2:3])
                self.inc(eA, "sE", 1)
                eB = sc.activation(t["expB"][fi % 2][0:128, 0:N],
                                   ps["pswp"][fi % 2][0:128, coff:coff + N], AF.Exp,
                                   bias=t["negB"][:, hp:hp + 1],
                                   scale=t["cpool"][:, 2:3])
                self.inc(eB, "sE", 1)
                if last:
                    sc.wait_ge(self.sems["tPV"], b_PV + cum[ui + 1])
                    sc.activation(t["lnz"][0:1, 0:QBW],
                                  ps["pvA"][64:65, 0:QBW], AF.Ln)
                    rA = sc.activation(t["recA"][64:65, 0:QBW],
                                       t["lnz"][0:1, 0:QBW], AF.Exp,
                                       scale=t["cpool"][64:65, 3:4])
                    self.inc(rA, "sRC", 1)
                    sc.activation(t["lnz"][0:1, 0:QBW],
                                  ps["pvB"][64:65, 0:QBW], AF.Ln)
                    rB = sc.activation(t["recB"][64:65, 0:QBW],
                                       t["lnz"][0:1, 0:QBW], AF.Exp,
                                       scale=t["cpool"][64:65, 3:4])
                    self.inc(rB, "sRC", 1)
                    sc.wait_ge(self.sems["tnm"], b_nm + 2 * ui + 1)
                    cA = sc.activation(t["bcst"][0:64, 0:QBW],
                                       ps["pnorm"][0:64, 0:QBW], AF.Copy)
                    self.inc(cA, "sBC", 1)
                    sc.wait_ge(self.sems["tnm"], b_nm + 2 * ui + 2)
                    cB = sc.activation(t["bcstB"][0:64, 0:QBW],
                                       ps["pnorm"][0:64, 0:QBW], AF.Copy)
                    self.inc(cB, "sBC", 1)

        @blk.gpsimd
        def _(gp):
            for ui in range(len(units)):
                for r in range(TPS):
                    w = (g * len(units) + ui) * TPS + r
                    if w >= 1:
                        gp.wait_ge(self.sems["vW2"], 2 * w)
                    self.inc(gp.dma_start(t["w2st"][:, :],
                                          d["w2t"][w * 128:(w + 1) * 128, :]),
                             "dW", 16)

        @blk.vector
        def _(vec):
            def w2_chain(w):
                vec.wait_ge(self.sems["dW"], 16 * (w + 1))
                vec.tensor_scalar_mul(t["wA"][:, :], t["w2st"][:, :], -1.0)
                vec.tensor_max(t["wC"][:, :], t["wA"][:, :], t["w2st"][:, :])
                vec.tensor_single_scalar(t["wA"][:, :], t["wC"][:, :], 0.25,
                                         OP.is_ge)
                vec.tensor_scalar(t["wD"][:, :], t["wA"][:, :], 15.0 * 2.0 ** 43,
                                  2.0 ** 43, OP.mult, OP.add)
                for B in (2, 1):
                    vec.tensor_scalar_mul(t["wT"][:, :], t["wD"][:, :],
                                          2.0 ** (B - 49))
                    vec.tensor_tensor(t["wA"][:, :], t["wT"][:, :], t["wC"][:, :],
                                      OP.is_ge)
                    vec.tensor_scalar(t["wA"][:, :], t["wA"][:, :],
                                      -(2.0 ** B - 1.0), 2.0 ** B, OP.mult, OP.add)
                    vec.tensor_mul(t["wD"][:, :], t["wD"][:, :], t["wA"][:, :])
                w2qs = t["w2q"][:, w * E:(w + 1) * E]
                vec.tensor_scalar_mul(t["wT"][:, :], t["wD"][:, :], 1.5)
                vec.tensor_add(t["wC"][:, :], t["w2st"][:, :], t["wT"][:, :])
                vec.tensor_sub(t["wC"][:, :], t["wC"][:, :], t["wT"][:, :])
                a = vec.tensor_copy(w2qs, t["w2st"][:, :])
                self.inc(a, "vW2", 1)
                vec.tensor_sub(t["wA"][:, :], t["wC"][:, :], w2qs)
                vec.tensor_mul(t["wA"][:, :], t["wA"][:, :], t["mgad"][:, 0:E])
                q2 = vec.tensor_add(w2qs, w2qs, t["wA"][:, :])
                self.inc(q2, "vW2", 1)

            for ui, (li, qb, items) in enumerate(units):
                gsl = sl0 + ui
                if gsl >= 2:
                    vec.wait_ge(self.sems["dSp"], 32 * (gsl - 1))
                vec.wait_ge(self.sems["sBC"], b_BC + 2 * ui + 1)
                mA = vec.tensor_mul(t["slabA"][ui % 2][0:64, 0:QBW],
                                    ps["pvA"][0:64, 0:QBW], t["bcst"][0:64, 0:QBW])
                self.inc(mA, "vNR", 1)
                vec.wait_ge(self.sems["sBC"], b_BC + 2 * ui + 2)
                mB = vec.tensor_mul(t["slabB"][ui % 2][0:64, 0:QBW],
                                    ps["pvB"][0:64, 0:QBW], t["bcstB"][0:64, 0:QBW])
                self.inc(mB, "vNR", 1)
                for r in range(TPS):
                    w2_chain((g * len(units) + ui) * TPS + r)

        @blk.sync
        def _(sync):
            for ui, (li, qb, items) in enumerate(units):
                hp = g * PPG + li
                sync.wait_ge(self.sems["vNR"], b_NR + 2 * ui + 1)
                self.inc(sync.dma_start(
                    d["aspill"][hp * 128: hp * 128 + 64, qb * QBW:(qb + 1) * QBW],
                    t["slabA"][ui % 2][0:64, 0:QBW]), "dSp", 16)
                sync.wait_ge(self.sems["vNR"], b_NR + 2 * ui + 2)
                self.inc(sync.dma_start(
                    d["aspill"][hp * 128 + 64: (hp + 1) * 128, qb * QBW:(qb + 1) * QBW],
                    t["slabB"][ui % 2][0:64, 0:QBW]), "dSp", 16)


    def phase_ffn1(self, blk):
        cfg, t, d, ps = self.cfg, self.t, self.d, self.ps
        T, E, JT, KE = cfg.T, cfg.E, cfg.JT, cfg.KE
        NB, NBW = cfg.NB, cfg.NBW
        b_dG = self.cnt["dG"]
        b_F1, b_sH, b_vH = self.cnt["tF1"], self.cnt["sH"], self.cnt["vH"]

        @blk.gpsimd
        def _(gp):
            for jt in range(JT):
                if jt >= 2:
                    gp.wait_ge(self.sems["tF1"], b_F1 + jt - 1)
                self.inc(gp.dma_start(
                    t["w1sb"][jt % 2][:, :],
                    bass.AP(d["w1b"], jt * E * 128,
                            [[128, 128], [128 * 128, KE], [1, 128]])), "dG", 16)
        @blk.tensor
        def _(tn):
            for jt in range(JT):
                tn.wait_ge(self.sems["dG"], b_dG + 16 * (jt + 1))
                if jt >= 2:
                    tn.wait_ge(self.sems["sH"], b_sH + 2 * (jt - 2) + 1)
                for nb in range(NB):
                    for kc in range(KE):
                        mm = tn.matmul(
                            ps["ph4"][jt % 2][:, nb * NBW:(nb + 1) * NBW],
                            t["w1sb"][jt % 2][:, kc * 128:(kc + 1) * 128],
                            t["xnT"][:, kc * T + nb * NBW: kc * T + (nb + 1) * NBW],
                            start=(kc == 0), stop=(kc == KE - 1))
                self.inc(mm, "tF1", 1)

        @blk.scalar
        def _(sc):
            for jt in range(JT):
                sc.wait_ge(self.sems["tF1"], b_F1 + jt + 1)
                if jt >= 3:
                    sc.wait_ge(self.sems["vH"], b_vH + (jt - 3) + 1)
                h = sc.activation(t["qth"][jt % 3][:, 0:T], ps["ph4"][jt % 2][:, 0:T],
                                  AF.Relu, bias=t["b1sb"][:, jt:jt + 1])
                self.inc(h, "sH", 1)
                c8 = sc.activation(t["hq"][:, jt * T:(jt + 1) * T],
                                   t["qth"][jt % 3][:, 0:T], AF.Copy)
                self.inc(c8, "sH", 1)

        @blk.vector
        def _(vec):
            for jt in range(JT):
                vec.wait_ge(self.sems["sH"], b_sH + 2 * jt + 2)
                h = t["qth"][jt % 3][:, 0:T]
                gD, gA, gB, gC = (t[k][:, 0:T] for k in ("gD", "gA", "gB", "gC"))
                gT = t["gT"][:, 0:T]
                vec.tensor_single_scalar(gA, h, 4.0, OP.is_ge)
                vec.tensor_scalar(gD, gA, 255.0 * 2.0 ** 43, 2.0 ** 43,
                                  OP.mult, OP.add)
                for B in (4, 2, 1):
                    vec.tensor_scalar_mul(gT, gD, 2.0 ** (B - 49))
                    vec.tensor_tensor(gA, gT, h, OP.is_ge)
                    vec.tensor_scalar(gA, gA, -(2.0 ** B - 1.0), 2.0 ** B,
                                      OP.mult, OP.add)
                    vec.tensor_mul(gD, gD, gA)
                hqs = t["hq"][:, jt * T:(jt + 1) * T]
                vec.tensor_scalar_mul(gT, gD, 1.5)
                z = vec.tensor_add(gC, h, gT)
                self.inc(z, "vH", 1)
                vec.tensor_sub(gC, gC, gT)
                vec.tensor_sub(gA, gC, hqs)
                vec.tensor_mul(gA, gA, t["mgad"][:, 0:T])
                vec.tensor_add(hqs, hqs, gA)
    def phase_out(self, blk):
        cfg, t, d, ps = self.cfg, self.t, self.d, self.ps
        T, E, JT, KE, TT = cfg.T, cfg.E, cfg.JT, cfg.KE, cfg.TT
        NE, NEW = cfg.NE, cfg.NEW
        b_dS, b_dG = self.cnt["dS"], self.cnt["dG"]
        b_F2, b_WO = self.cnt["tF2"], self.cnt["tWO"]
        b_EP = self.cnt["vEP"]
        b_O2, b_C5 = self.cnt["vO2"], self.cnt["sC5"]
        b_dX2, b_dO = self.cnt["dX2"], self.cnt["dO"]
        b_dSp = self.cnt["dSp"]

        @blk.gpsimd
        def _(gp):
            for kc in range(KE):
                self.inc(gp.dma_start(t["wosb"][:, kc * E:(kc + 1) * E],
                                      d["wo"][kc * 128:(kc + 1) * 128, :]), "dG", 16)

        @blk.sync
        def _(sync):
            self.inc(sync.dma_start(t["b2rep"][:, :],
                                    bass.AP(d["b2r"], 0, [[0, 128], [1, E]])), "dS", 16)
            sync.wait_ge(self.sems["dSp"], b_dSp)

            def out_dma(k):
                sync.wait_ge(self.sems["vEP"], b_EP + 1 + 2 * k + 2)
                self.inc(sync.dma_start(d["out"][k * 128:(k + 1) * 128, :],
                                        t["obuf"][k % 2][:, :]), "dO", 16)

            for tt in range(TT):
                if tt >= 2:
                    sync.wait_ge(self.sems["tWO"], b_WO + tt - 1)
                self.inc(sync.dma_start(
                    t["atb"][tt % 2][:, :],
                    bass.AP(d["aspill"], tt * 128,
                            [[T, 128], [128 * T, KE], [1, 128]])), "dS", 16)
                if tt >= 2:
                    sync.wait_ge(self.sems["vEP"], b_EP + 1 + 2 * (tt - 2) + 2)
                self.inc(sync.dma_start(t["xbuf"][tt % 2][:, :],
                                        d["x"][tt * 128:(tt + 1) * 128, :]), "dX2", 16)
                if tt >= 2:
                    out_dma(tt - 2)
            out_dma(TT - 2)
            out_dma(TT - 1)
            sync.wait_ge(self.sems["dO"], b_dO + 16 * TT)

        @blk.tensor
        def _(tn):
            tn.wait_ge(self.sems["dG"], b_dG + 16 * KE)
            for tt in range(TT):
                if tt >= 2:
                    tn.wait_ge(self.sems["vEP"], b_EP + 1 + 2 * (tt - 2) + 1)
                for jc in range(JT):
                    for ne in range(NE):
                        mm = tn.matmul(
                            ps["psf"][tt % 2][:, ne * NEW:(ne + 1) * NEW],
                            t["hq"][:, jc * T + tt * 128: jc * T + (tt + 1) * 128],
                            t["w2q"][:, jc * E + ne * NEW: jc * E + (ne + 1) * NEW],
                            start=(jc == 0), stop=(jc == JT - 1))
                self.inc(mm, "tF2", 1)
                tn.wait_ge(self.sems["dS"], b_dS + 16 * (tt + 2))
                if tt >= 2:
                    tn.wait_ge(self.sems["vEP"], b_EP + 1 + 2 * (tt - 2) + 2)
                for kc in range(KE):
                    for ne in range(NE):
                        mm = tn.matmul(
                            ps["pso"][tt % 2][:, ne * NEW:(ne + 1) * NEW],
                            t["atb"][tt % 2][:, kc * 128:(kc + 1) * 128],
                            t["wosb"][:, kc * E + ne * NEW: kc * E + (ne + 1) * NEW],
                            start=(kc == 0), stop=(kc == KE - 1))
                self.inc(mm, "tWO", 1)

        @blk.scalar
        def _(sc):
            sc.wait_ge(self.sems["dS"], b_dS + 16)
            c0 = sc.activation(t["oB"][:, :], t["b2rep"][:, :], AF.Copy)
            self.inc(c0, "sC5", 1)
            for tt in range(TT):
                sc.wait_ge(self.sems["vO2"], b_O2 + tt + 1)
                c16 = sc.activation(t["oB"][:, :], t["o2b"][:, :], AF.Copy)
                self.inc(c16, "sC5", 1)

        @blk.vector
        def _(vec):
            def gq5(xn_f, out_ap, c5):
                vec.tensor_scalar_mul(t["oA"][:, :], xn_f, -1.0)
                vec.tensor_max(t["oav"][:, :], t["oA"][:, :], xn_f)
                vec.tensor_single_scalar(t["oA"][:, :], t["oav"][:, :], 2.0 ** -6,
                                         OP.is_ge)
                vec.tensor_scalar(t["oD"][:, :], t["oA"][:, :], 255.0 * 2.0 ** 28,
                                  2.0 ** 28, OP.mult, OP.add)
                for B in (8, 4, 2, 1):
                    vec.tensor_scalar_mul(t["oT"][:, :], t["oD"][:, :],
                                          2.0 ** (B - 42))
                    vec.tensor_tensor(t["oA"][:, :], t["oT"][:, :], t["oav"][:, :],
                                      OP.is_ge)
                    vec.tensor_scalar(t["oA"][:, :], t["oA"][:, :],
                                      -(2.0 ** B - 1.0), 2.0 ** B, OP.mult, OP.add)
                    vec.tensor_mul(t["oD"][:, :], t["oD"][:, :], t["oA"][:, :])
                vec.tensor_scalar_mul(t["oT"][:, :], t["oD"][:, :], 1.5)
                vec.tensor_add(t["oC"][:, :], xn_f, t["oT"][:, :])
                vec.tensor_sub(t["oC"][:, :], t["oC"][:, :], t["oT"][:, :])
                vec.wait_ge(self.sems["sC5"], c5)
                vec.tensor_sub(t["oS"][:, :], t["oC"][:, :], t["oB"][:, :])
                vec.tensor_mul(t["oS"][:, :], t["oS"][:, :], t["mgad"][:, 0:E])
                return vec.tensor_add(out_ap, t["oB"][:, :], t["oS"][:, :])

            vec.wait_ge(self.sems["dS"], b_dS + 16)
            qb2 = gq5(t["b2rep"][:, :], t["b2q"][:, :], b_C5 + 1)
            self.inc(qb2, "vEP", 1)
            for tt in range(TT):
                vec.wait_ge(self.sems["tF2"], b_F2 + tt + 1)
                oadd = vec.tensor_add(t["o2b"][:, :], ps["psf"][tt % 2][:, 0:E],
                                      t["b2q"][:, :])
                self.inc(oadd, "vO2", 1)
                q5 = gq5(t["o2b"][:, :], t["o2b"][:, :], b_C5 + tt + 2)
                self.inc(q5, "vEP", 1)
                vec.wait_ge(self.sems["tWO"], b_WO + tt + 1)
                vec.wait_ge(self.sems["dX2"], b_dX2 + 16 * (tt + 1))
                if tt >= 2:
                    vec.wait_ge(self.sems["dO"], b_dO + 16 * (tt - 1))
                vec.tensor_add(t["oS"][:, :], t["o2b"][:, :],
                               ps["pso"][tt % 2][:, 0:E])
                fin = vec.tensor_add(t["obuf"][tt % 2][:, :], t["oS"][:, :],
                                     t["xbuf"][tt % 2][:, :])
                self.inc(fin, "vEP", 1)


_BUILD_CACHE = {}


def get_nc(cfg: Cfg):
    key = (cfg.T, cfg.E, cfg.HEADS, cfg.LAT, cfg.HF, cfg.PAGE)
    if key not in _BUILD_CACHE:
        _BUILD_CACHE[key] = Builder(cfg).build()
    return _BUILD_CACHE[key]


def make_host_constants(cfg: Cfg):
    T = cfg.T
    bf = ml_dtypes.bfloat16
    inv = 1.0 / 10000.0 ** (np.arange(0, 64, 2, dtype=np.float32) / 64.0)
    ang = np.outer(np.arange(T, dtype=np.float32), inv)
    cos, sin = np.cos(ang), np.sin(ang)
    CT = np.zeros((128, T), np.float32)
    ST = np.zeros((128, T), np.float32)
    for p in range(128):
        j = p % 64
        jj = j % 32
        CT[p] = cos[:, jj]
        ST[p] = (-sin[:, jj]) if j < 32 else sin[:, jj]
    P = np.zeros((128, 128), np.float32)
    for i in range(128):
        jj = i + 32 if (i % 64) < 32 else i - 32
        P[jj, i] = 1.0
    E64 = np.zeros((128, 64), np.float32)
    E64[64, :] = 1.0
    return {
        "tc": np.ascontiguousarray(CT),
        "ts": np.ascontiguousarray(ST),
        "Pswap": np.ascontiguousarray(P.astype(bf)),
        "identin": np.ascontiguousarray(np.eye(128, dtype=np.float32).astype(bf)),
        "e64in": np.ascontiguousarray(E64.astype(bf)),
        "onesin": np.ascontiguousarray(np.ones((128, 128), np.float32).astype(bf)),
        "onescol": np.ascontiguousarray(np.ones((128, 1), np.float32).astype(bf)),
    }


def head_perm(E):
    perm = []
    for h in range(E // 64):
        base = h * 64
        perm += [base + 2 * i for i in range(32)] + [base + 2 * i + 1 for i in range(32)]
    return np.array(perm)


def prepare_inputs(cfg: Cfg, x, wq, w_down, wk_up, wv_up, wo, w1, b1, w2, b2):
    perm = head_perm(cfg.E)
    c = lambda a: np.ascontiguousarray(np.asarray(a), dtype=np.float32)
    wq, w_down, wk_up, wv_up, wo = (np.asarray(a, dtype=np.float32)
                                    for a in (wq, w_down, wk_up, wv_up, wo))
    w1, b1, w2, b2 = (np.asarray(a, dtype=np.float32) for a in (w1, b1, w2, b2))
    base = {
        "wq": c(wq[:, perm]),
        "wdn": c(w_down),
        "wku": c(wk_up[:, perm]),
        "wvu": c(wv_up),
        "wo": c(wo),
        "w1b": c(w1.T.reshape(cfg.E, cfg.JT, 128).transpose(1, 0, 2)),
        "w2t": c(w2.T),
        "b1r": c(b1.reshape(cfg.JT, 128).T),
        "b2r": c(b2.reshape(1, cfg.E)),
    }
    base.update(make_host_constants(cfg))
    in_maps = []
    for i in range(x.shape[0]):
        m = dict(base)
        m["x"] = c(x[i])
        in_maps.append(m)
    return in_maps


def kernel(L0_attn_x, L0_attn_wq, L0_attn_w_down, L0_attn_wk_up, L0_attn_wv_up,
           L0_attn_wo, L0_ffn_w1, L0_ffn_b1, L0_ffn_w2, L0_ffn_b2):
    cfg = Cfg()
    x = np.asarray(L0_attn_x, dtype=np.float32)
    in_maps = prepare_inputs(cfg, x, L0_attn_wq, L0_attn_w_down, L0_attn_wk_up,
                             L0_attn_wv_up, L0_attn_wo, L0_ffn_w1, L0_ffn_b1,
                             L0_ffn_w2, L0_ffn_b2)
    nc = get_nc(cfg)
    res = run_bass_kernel_spmd(nc, in_maps, core_ids=list(range(cfg.n_cores)))
    return np.stack([np.asarray(r["out"], dtype=np.float32) for r in res.results])
